# version 10
# speedup vs baseline: 1.0049x; 1.0024x over previous
import contextlib
import numpy as np
import ml_dtypes
import concourse.bass as bass
import concourse.mybir as mybir
from concourse import aot_env
from concourse.bass_utils import run_bass_kernel_spmd

F32 = mybir.dt.float32
BF16 = mybir.dt.bfloat16
AF = mybir.ActivationFunctionType
ALU = mybir.AluOpType
AX = mybir.AxisListType
BF = ml_dtypes.bfloat16

D = 4096
NCH = 32
TPC = 1024
NCORES = 8



class Sem:
    def __init__(self, nc, es, name):
        self.h = es.enter_context(nc.semaphore(name))
        self.v = 0

    def inc(self, inst, n=1):
        self.v += n
        inst.then_inc(self.h, n)
        return inst


SEM_H = {"ln_dx0", "ln_dx1", "res_dx0", "res_dx1", "res_st0", "res_st1",
         "do_res_dx0", "do_res_dx1", "do_res_st0", "do_res_st1",
         "d_st0", "d_st1", "dmab", "d_in0", "d_in1", "s_h", "s_eg",
         "s_y0", "s_y1", "d_hb", "d_h", "d_cs", "d_tl", "s_S",
         "s_ob0", "s_ob1", "s_qg0", "s_qg1", "dmr", "dmw", "s_o0", "s_o1",
         "s_c0", "s_c1", "dma", "s_v"}
SEM_S = {"dmaw", "dw_up0", "dw_up1", "dmq0", "dmq1", "dw0", "dw1", "dma_s",
         "do_dw0", "do_dw1", "dkv0", "dkv1", "dmq"}
SEM_SHARED_PRE = ("g_", "fence", "dma_c", "coll_")


class LP:

    def __init__(self, name):
        self.name = name
        self.nc = bass.Bass("TRN2", target_bir_lowering=False)
        self.es = contextlib.ExitStack()
        self._sems = {}
        self.mega = False
        self._alias = None
        self._pool_idx = None
        self._io = None
        self._scopes = []
        self._name_counts = {}
        self._cst_cache = None

    def begin_section(self, io):
        self._alias = {}
        self._pool_idx = {"h": 0, "s": 0, "c": 0}
        self._io = io
        self._scopes.append(contextlib.ExitStack())

    def end_section(self):
        self._scopes.pop().close()
        nc = self.nc
        f = self.sem("fence")
        for s in self._sems.values():
            if s is not f and s.v > 0:
                nc.vector.wait_ge(s.h, s.v)
        f.inc(nc.vector.engine_nop())
        for e in [nc.tensor, nc.scalar, nc.gpsimd, nc.sync, nc.vector]:
            e.wait_ge(f.h, f.v)
        self._io = None
        self._alias = None

    def _uniq(self, name):
        n = self._name_counts.get(name, 0)
        self._name_counts[name] = n + 1
        return name if n == 0 else f"{name}_u{n}"

    def sem(self, name):
        if self._alias is not None and not name.startswith(SEM_SHARED_PRE):
            if name not in self._alias:
                t = "h" if name in SEM_H else ("s" if name in SEM_S else "c")
                i = self._pool_idx[t]
                self._pool_idx[t] += 1
                self._alias[name] = f"{t}P{i}"
            name = self._alias[name]
        if name not in self._sems:
            self._sems[name] = Sem(self.nc, self.es, name)
        return self._sems[name]

    def sbuf(self, name, shape, dtype):
        return self.nc.sbuf_tensor(self._uniq(name), shape, dtype)

    def psum(self, name, shape, dtype):
        return self.nc.psum_tensor(self._uniq(name), shape, dtype)

    def sb(self, name, shape, dtype):
        st = self._scopes[-1] if self._scopes else self.es
        return st.enter_context(self.nc.sbuf_tensor(self._uniq(name), shape,
                                                    dtype))

    def g(self, eng, inst):
        nm = "g_" + str(eng.engine)
        s = self.sem(nm)
        s.inc(inst)
        eng.wait_ge(s.h, s.v)
        return inst

    def inp(self, name, shape, dtype=F32):
        if self._io is not None:
            return self._io[name]
        return self.nc.dram_tensor(name, list(shape), dtype, kind="ExternalInput").ap()

    def out(self, name, shape, dtype=F32):
        if self._io is not None:
            return self._io[name]
        return self.nc.dram_tensor(name, list(shape), dtype, kind="ExternalOutput").ap()

    def done(self):
        if self.mega:
            return self.nc
        self.es.close()
        return self.nc


def fm_chunk(hbm, c, T):
    return hbm.rearrange("(c p) t -> c p t", p=128)[c]


def load_fm(p, eng, dst_sb, src_hbm, nch, T, sem, splits=4):
    src = src_hbm.rearrange("(c p) t -> p c t", p=128)
    bounds = [round(i * nch / splits) for i in range(splits + 1)]
    for i in range(splits):
        cs, ce = bounds[i], bounds[i + 1]
        if cs == ce:
            continue
        d = dst_sb[:, cs * T:ce * T].rearrange("p (c t) -> p c t", t=T)
        sem.inc(eng.dma_start(out=d, in_=src[:, cs:ce, :]), 16)


def col_tiles(T, width=512):
    out = []
    t = 0
    while t < T:
        w = min(width, T - t)
        out.append((t, t + w))
        t += w
    return out


def load_consts(p, cst_specs):
    if p._cst_cache is not None:
        return ({key: p._cst_cache[(key, tuple(shape))]
                 for key, shape, dt in cst_specs}, p.sem("dma_c"))
    nc = p.nc
    dma = p.sem("dma_c")
    out = {}
    for key, shape, dt in cst_specs:
        ap = p.inp("c_" + key, shape, dt)
        sb = p.sb("cst_" + key, list(shape), dt)
        dma.inc(nc.sync.dma_start(out=sb[:], in_=ap), 16)
        out[key] = sb
    return out, dma


HOST_CONSTS = {}


def host_const_arrays():
    if HOST_CONSTS:
        return HOST_CONSTS
    HOST_CONSTS["onesb"] = np.ones((128, 1), BF)
    HOST_CONSTS["ones1f"] = np.ones((1, 128), np.float32)
    HOST_CONSTS["identf"] = np.eye(128, dtype=np.float32)
    HOST_CONSTS["identb"] = np.eye(128, dtype=BF)
    return HOST_CONSTS



def emit_ln(p, x_hbm, xn_sb, T, cst, mode="plain", lnw_col=None):
    nc = p.nc
    d_xp = [p.sem("ln_dx0"), p.sem("ln_dx1")]
    v_cast = p.sem("ln_cast")
    a_sq = p.sem("ln_sq")
    t_st = p.sem("ln_stats")
    v_rows = p.sem("ln_rows")
    a_row = p.sem("ln_rowa")
    v_row2 = p.sem("ln_rowv2")
    t_bc = p.sem("ln_bc")
    v_bc = p.sem("ln_bcv")
    v_ap = p.sem("ln_apply")

    ct = col_tiles(T)
    with (
        p.sbuf("ln_xst", [128, 2 * T], F32) as xst,
        p.sbuf("ln_xbt", [128, 2 * T], BF16) as xbt,
        p.sbuf("ln_sq", [128, 2 * T], BF16) as sq,
        p.sbuf("ln_rows", [1, (6 if mode == "mamba" else 5) * T], F32) as rows,
        p.sbuf("ln_mbc", [128, T], F32) as mbc,
        p.sbuf("ln_rbc", [128, T], F32) as rbc,
        p.psum("ln_s1", [1, T], F32) as pS1,
        p.psum("ln_s2", [1, T], F32) as pS2,
    ):
        for c in range(NCH):
            par = c % 2
            if c >= 2:
                nc.sync.wait_ge(v_cast.h, v_cast.v - 1)
                nc.sync.wait_ge(d_xp[par].h, d_xp[par].v)
            d_xp[par].inc(nc.sync.dma_start(out=xst[:, par * T:(par + 1) * T],
                                            in_=fm_chunk(x_hbm, c, T)), 16)
            nc.vector.wait_ge(d_xp[par].h, d_xp[par].v)
            if c >= 2:
                nc.vector.wait_ge(t_st.h, t_st.v - 1)
            v_cast.inc(nc.vector.tensor_copy(xbt[:, par * T:(par + 1) * T],
                                             xst[:, par * T:(par + 1) * T]))
            nc.scalar.wait_ge(v_cast.h, v_cast.v)
            if c >= 2:
                nc.scalar.wait_ge(t_st.h, t_st.v - 1)
            a_sq.inc(nc.scalar.activation(
                sq[:, par * T:(par + 1) * T], xbt[:, par * T:(par + 1) * T],
                AF.Square))
            first, last = (c == 0), (c == NCH - 1)
            nc.tensor.wait_ge(v_cast.h, v_cast.v)
            for (t0, t1) in ct:
                nc.tensor.matmul(pS1[:, t0:t1], cst["onesb"][:, :],
                                 xbt[:, par * T + t0:par * T + t1],
                                 start=first, stop=last, skip_group_check=True)
            nc.tensor.wait_ge(a_sq.h, a_sq.v)
            ins = None
            for (t0, t1) in ct:
                ins = nc.tensor.matmul(pS2[:, t0:t1], cst["onesb"][:, :],
                                       sq[:, par * T + t0:par * T + t1],
                                       start=first, stop=last, skip_group_check=True)
            t_st.inc(ins)

        m_row = rows[:, 0 * T:1 * T]
        e2_row = rows[:, 1 * T:2 * T]
        var_row = rows[:, 2 * T:3 * T]
        r_row = rows[:, 3 * T:4 * T]
        vpe_row = rows[:, 4 * T:5 * T]
        nc.vector.wait_ge(t_st.h, t_st.v)
        p.g(nc.vector, nc.vector.tensor_scalar(m_row, pS1[:, :], 1.0 / D, None, ALU.mult))
        p.g(nc.vector, nc.vector.tensor_scalar(e2_row, pS2[:, :], 1.0 / D, None, ALU.mult))
        p.g(nc.vector, nc.vector.tensor_tensor(var_row, m_row, m_row, ALU.mult))
        p.g(nc.vector, nc.vector.tensor_tensor(var_row, e2_row, var_row, ALU.subtract))
        v_rows.inc(nc.vector.tensor_scalar(vpe_row, var_row, 1e-5, None, ALU.add))
        nc.scalar.wait_ge(v_rows.h, v_rows.v)
        p.g(nc.scalar, nc.scalar.activation(r_row, vpe_row, AF.Ln))
        a_row.inc(nc.scalar.activation(r_row, r_row, AF.Exp, scale=-0.5))
        if mode == "mamba":
            r2 = rows[:, 5 * T:6 * T]
            tmp = vpe_row
            nc.vector.wait_ge(a_row.h, a_row.v)
            p.g(nc.vector, nc.vector.tensor_tensor(tmp, r_row, r_row, ALU.mult))
            p.g(nc.vector, nc.vector.tensor_tensor(tmp, var_row, tmp, ALU.mult))
            v_row2.inc(nc.vector.tensor_scalar(tmp, tmp, 1e-5, None, ALU.add))
            nc.scalar.wait_ge(v_row2.h, v_row2.v)
            p.g(nc.scalar, nc.scalar.activation(r2, tmp, AF.Ln))
            a_row.inc(nc.scalar.activation(r2, r2, AF.Exp, scale=-0.5))
            nc.vector.wait_ge(a_row.h, a_row.v)
            v_row2.inc(nc.vector.tensor_tensor(r_row, r_row, r2, ALU.mult))
            nc.tensor.wait_ge(v_row2.h, v_row2.v)
        else:
            nc.tensor.wait_ge(a_row.h, a_row.v)

        with p.psum("ln_pbc", [128, 512], F32) as pBC, \
             p.psum("ln_pbc2", [128, 512], F32) as pBC2:
            for (t0, t1) in ct:
                w_ = t1 - t0
                if t0 > 0:
                    nc.tensor.wait_ge(v_bc.h, v_bc.v)
                nc.tensor.matmul(pBC[:, 0:w_], cst["ones1f"][:, :], m_row[:, t0:t1],
                                 start=True, stop=True)
                t_bc.inc(nc.tensor.matmul(pBC2[:, 0:w_], cst["ones1f"][:, :],
                                          r_row[:, t0:t1], start=True, stop=True))
                nc.vector.wait_ge(t_bc.h, t_bc.v)
                p.g(nc.vector, nc.vector.tensor_copy(mbc[:, t0:t1], pBC[:, 0:w_]))
                v_bc.inc(nc.vector.tensor_copy(rbc[:, t0:t1], pBC2[:, 0:w_]))

        nc.sync.wait_ge(v_cast.h, v_cast.v)
        nc.gpsimd.wait_ge(v_bc.h, v_bc.v)
        nc.vector.wait_ge(v_bc.h, v_bc.v)
        for c in range(NCH):
            par = c % 2
            nc.sync.wait_ge(d_xp[par].h, d_xp[par].v)
            if c >= 2:
                nc.sync.wait_ge(v_ap.h, v_ap.v - 3)
            ins = nc.sync.dma_start(out=xst[:, par * T:(par + 1) * T],
                                    in_=fm_chunk(x_hbm, c, T))
            if c >= 2:
                ins._wait_ge(v_ap.h, v_ap.v - 3)
            d_xp[par].inc(ins, 16)
            eng = nc.vector
            eng.wait_ge(d_xp[par].h, d_xp[par].v)
            xnc = xn_sb[:, c * T:(c + 1) * T]
            v_ap.inc(eng.tensor_tensor(xnc, xst[:, par * T:(par + 1) * T],
                                       mbc[:, :], ALU.subtract))
            eng.wait_ge(v_ap.h, v_ap.v)
            if mode == "mamba":
                ins = eng.scalar_tensor_tensor(
                    xnc, xnc, lnw_col[:, c:c + 1], rbc[:, :], ALU.mult, ALU.mult)
            else:
                ins = eng.tensor_tensor(xnc, xnc, rbc[:, :], ALU.mult)
            v_ap.inc(ins)
    return v_ap


def emit_residual_stream(p, x_hbm, xout_hbm, T, delta_cb, scale=1.0, n_oc=NCH,
                         ready=None):
    nc = p.nc
    d_rp = [p.sem("res_dx0"), p.sem("res_dx1")]
    v_x = p.sem("res_vx")
    s_outp = [p.sem("res_st0"), p.sem("res_st1")]
    with p.sbuf("res_xst", [128, 2 * T], F32) as xst:
        for oc in range(n_oc):
            par = oc % 2
            if oc >= 2:
                nc.sync.wait_ge(s_outp[par].h, s_outp[par].v)
                nc.sync.wait_ge(d_rp[par].h, d_rp[par].v)
            elif ready is not None:
                nc.sync.wait_ge(ready.h, ready.v)
            d_rp[par].inc(nc.sync.dma_start(out=xst[:, par * T:(par + 1) * T],
                                            in_=fm_chunk(x_hbm, oc, T)), 16)
            psum_ap, t_sem = delta_cb(oc)
            nc.vector.wait_ge(t_sem.h, t_sem.v)
            nc.vector.wait_ge(d_rp[par].h, d_rp[par].v)
            v_x.inc(nc.vector.scalar_tensor_tensor(
                xst[:, par * T:(par + 1) * T], psum_ap, scale,
                xst[:, par * T:(par + 1) * T], ALU.mult, ALU.add))
            ins = nc.sync.dma_start(out=fm_chunk(xout_hbm, oc, T),
                                    in_=xst[:, par * T:(par + 1) * T])
            ins._wait_ge(v_x.h, v_x.v)
            s_outp[par].inc(ins, 16)
        nc.sync.wait_ge(s_outp[0].h, s_outp[0].v)
        nc.sync.wait_ge(s_outp[1].h, s_outp[1].v)
    return v_x



def build_ffn(which, T=TPC, debug=False, p=None):
    R = 16 if which == 0 else 8
    E = 4
    ER = E * R
    SCALE = 4.0 if which == 0 else 1.0
    p = p or LP(f"ffn{which}")
    nc = p.nc
    xfm = p.inp("xfm", (D, T))
    wrt = p.inp("wrt", (D, E))
    wa = p.inp("wa", (D, ER))
    wb = p.inp("wb", (ER, D))
    if which == 1:
        wup = p.inp("wup", (NCH, 128, NCH, 128), BF16)
        ebias = p.inp("ebias", (1, E))
    xout = p.out("xout", (D, T))

    cst, dma_c = load_consts(p, [
        ("onesb", (128, 1), BF16), ("ones1f", (1, 128), F32),
        ("identf", (128, 128), F32),
        ("expe", (E, ER), BF16),
    ])
    dmaw = p.sem("dmaw")

    nT = T // 128
    xn_sb = p.sb("xn_sb", [128, NCH * T], BF16)
    wrt_sb = p.sb("wrt_sb", [128, NCH * E], BF16)
    wa_sb = p.sb("wa_sb", [128, NCH * ER], BF16)
    wb_sb = p.sb("wb_sb", [ER, D], BF16)
    lgT_sb = p.sb("lgT", [128, nT * E], F32)
    g_m1 = p.sb("g_m1", [128, nT], F32)
    g_m2 = p.sb("g_m2", [128, nT], F32)
    g_ex = p.sb("g_ex", [128, nT * E], F32)
    g_kp = p.sb("g_kp", [128, nT * E], F32)
    g_l2 = p.sb("g_l2", [128, nT * E], F32)
    g_z = p.sb("g_z", [128, nT], F32)
    g_eb = p.sb("g_eb", [128, E], F32)
    gT_sb = p.sb("gT", [E, T], BF16)
    lg_row = p.sb("lgrow", [E, T], F32)
    gexp_sb = p.sb("gexp", [ER, T], F32)
    midg_sb = p.sb("midg", [ER, T], BF16)
    if which == 1:
        wup_sb = p.sb("wup_sb", [128, 2 * NCH * 128], BF16)
        ebias_sb = p.sb("ebias_sb", [1, E], F32)
        dma_c.inc(nc.sync.dma_start(out=ebias_sb[:, :], in_=ebias), 16)

    dmaw.inc(nc.gpsimd.dma_start(
        out=wrt_sb[:, :].rearrange("p (c e) -> p c e", e=E),
        in_=wrt.rearrange("(c p) e -> p c e", p=128)), 16)
    dmaw.inc(nc.gpsimd.dma_start(
        out=wa_sb[:, :].rearrange("p (c j) -> p c j", j=ER),
        in_=wa.rearrange("(c p) j -> p c j", p=128)), 16)
    dmaw.inc(nc.gpsimd.dma_start(out=wb_sb[:, :], in_=wb), 16)

    for e in [nc.vector, nc.scalar, nc.tensor, nc.gpsimd, nc.sync]:
        e.wait_ge(dma_c.h, dma_c.v)

    v_ap = emit_ln(p, xfm, xn_sb, T, cst)
    nc.tensor.wait_ge(v_ap.h, v_ap.v)
    nc.tensor.wait_ge(dmaw.h, dmaw.v)

    ct = col_tiles(T)
    t_lg = p.sem("t_lg")
    t_tr = p.sem("t_tr")
    v_g = p.sem("v_g")
    t_gx = p.sem("t_gx")
    t_d = p.sem("t_d")

    with (
        p.psum("pL", [E, T], F32) as pL,
        p.psum("pM", [ER, T], F32) as pM,
    ):
        for c in range(NCH):
            xnc3 = xn_sb[:, c * T:(c + 1) * T]
            first, last = (c == 0), (c == NCH - 1)
            ins = None
            for (t0, t1) in ct:
                nc.tensor.matmul(pL[:, t0:t1], wrt_sb[:, c * E:(c + 1) * E],
                                 xnc3[:, t0:t1], start=first, stop=last,
                                 skip_group_check=True)
                ins = nc.tensor.matmul(pM[:, t0:t1], wa_sb[:, c * ER:(c + 1) * ER],
                                       xnc3[:, t0:t1], start=first, stop=last,
                                       skip_group_check=True)
            if last:
                t_lg.inc(ins)

        nc.vector.wait_ge(t_lg.h, t_lg.v)
        v_lg = p.sem("v_lg")
        v_lg.inc(nc.vector.tensor_copy(lg_row[:, :], pL[:, :]))

        with p.psum("pTr", [128, nT * E], F32) as pTr:
            nc.tensor.wait_ge(v_lg.h, v_lg.v)
            ins = None
            for i in range(nT):
                ins = nc.tensor.transpose(pTr[:, i * E:(i + 1) * E],
                                          lg_row[:, i * 128:(i + 1) * 128],
                                          cst["identf"][0:E, 0:E])
            t_tr.inc(ins)
            nc.vector.wait_ge(t_tr.h, t_tr.v)
            v_tr = p.sem("v_tr")
            v_tr.inc(nc.vector.tensor_copy(lgT_sb[:, :], pTr[:, :]))

        L3 = lgT_sb[:, :].rearrange("p (n e) -> p n e", e=E)
        m1 = g_m1[:, :].rearrange("p (n o) -> p n o", o=1)
        m2 = g_m2[:, :].rearrange("p (n o) -> p n o", o=1)
        ex = g_ex[:, :].rearrange("p (n e) -> p n e", e=E)
        kp = g_kp[:, :].rearrange("p (n e) -> p n e", e=E)
        l2 = g_l2[:, :].rearrange("p (n e) -> p n e", e=E)
        zrow = g_z[:, :].rearrange("p (n o) -> p n o", o=1)
        nc.vector.wait_ge(v_tr.h, v_tr.v)
        if which == 0:
            p.g(nc.vector, nc.vector.reduce_max(m1, L3, axis=AX.X))
            p.g(nc.vector, nc.vector.tensor_tensor(ex, L3,
                m1.broadcast_to([128, nT, E]), ALU.is_equal))
            p.g(nc.vector, nc.vector.scalar_tensor_tensor(l2, ex, -1e30, L3,
                ALU.mult, ALU.add))
            p.g(nc.vector, nc.vector.reduce_max(m2, l2, axis=AX.X))
            nc.vector.tensor_tensor(ex, L3, m1.broadcast_to([128, nT, E]), ALU.subtract)
            va = p.sem("v_gpre")
            va.inc(nc.vector.tensor_tensor(kp, L3, m2.broadcast_to([128, nT, E]),
                                           ALU.is_ge))
            nc.scalar.wait_ge(va.h, va.v)
            ae = p.sem("a_gexp")
            ae.inc(nc.scalar.activation(ex, ex, AF.Exp))
            nc.vector.wait_ge(ae.h, ae.v)
            p.g(nc.vector, nc.vector.tensor_tensor(ex, ex, kp, ALU.mult))
            p.g(nc.vector, nc.vector.reduce_sum(zrow, ex, axis=AX.X))
            p.g(nc.vector, nc.vector.reciprocal(zrow, zrow))
            v_g.inc(nc.vector.tensor_tensor(ex, ex, zrow.broadcast_to([128, nT, E]),
                                            ALU.mult))
        else:
            with p.psum("pEB", [128, E], F32) as pEB:
                t_eb = p.sem("t_eb")
                nc.tensor.wait_ge(p.sem("v_tr").h, p.sem("v_tr").v)
                t_eb.inc(nc.tensor.matmul(pEB[:, :], cst["ones1f"][:, :],
                                          ebias_sb[:, :], start=True, stop=True))
                nc.vector.wait_ge(t_eb.h, t_eb.v)
                p.g(nc.vector, nc.vector.tensor_copy(g_eb[:, :], pEB[:, :]))
            p.g(nc.vector, nc.vector.tensor_tensor(
                L3, L3,
                g_eb[:, :].rearrange("p (n e) -> p n e", n=1).broadcast_to([128, nT, E]),
                ALU.add))
            p.g(nc.vector, nc.vector.reduce_max(m1, L3, axis=AX.X))
            v_g.inc(nc.vector.tensor_tensor(ex, L3, m1.broadcast_to([128, nT, E]),
                                            ALU.is_ge))

        with p.psum("pTg", [E, T], F32) as pTg:
            nc.tensor.wait_ge(v_g.h, v_g.v)
            ins = None
            for i in range(nT):
                ins = nc.tensor.transpose(pTg[:, i * 128:(i + 1) * 128],
                                          ex[:, i, :], cst["identf"][:, :])
            t_g2 = p.sem("t_g2")
            t_g2.inc(ins)
            nc.vector.wait_ge(t_g2.h, t_g2.v)
            v_g2 = p.sem("v_g2")
            v_g2.inc(nc.vector.tensor_copy(gT_sb[:, :], pTg[:, :]))

        with p.psum("pGX", [ER, T], F32) as pGX:
            nc.tensor.wait_ge(v_g2.h, v_g2.v)
            ins = None
            for (t0, t1) in ct:
                ins = nc.tensor.matmul(pGX[:, t0:t1], cst["expe"][:, :],
                                       gT_sb[:, t0:t1], start=True, stop=True)
            t_gx.inc(ins)
            nc.vector.wait_ge(t_gx.h, t_gx.v)
            nc.vector.wait_ge(t_lg.h, t_lg.v)
            p.g(nc.vector, nc.vector.tensor_copy(gexp_sb[:, :], pGX[:, :]))
            v_m2 = p.sem("v_m2")
            v_m2.inc(nc.vector.tensor_tensor(midg_sb[:, :], pM[:, :], gexp_sb[:, :],
                                             ALU.mult))

    v_m2 = p.sem("v_m2")
    nc.tensor.wait_ge(v_m2.h, v_m2.v)
    if which == 1:
        dwp = [p.sem("dw_up0"), p.sem("dw_up1")]
        dwpb0 = [dwp[0].v, dwp[1].v]

        def load_panel(oc, par):
            dst = wup_sb[:, par * NCH * 128:(par + 1) * NCH * 128].rearrange(
                "p (kc m) -> p kc m", m=128)
            dwp[par].inc(nc.gpsimd.dma_start(out=dst, in_=wup[oc]), 16)

        nc.gpsimd.wait_ge(v_ap.h, v_ap.v)
        load_panel(0, 0)
        load_panel(1, 1)

    with (
        p.psum("pD0", [128, T], F32) as pD0,
        p.psum("pD1", [128, T], F32) as pD1,
    ):
        pD = [pD0, pD1]
        v_res = p.sem("res_vx")
        vrb = v_res.v
        dwpb = dwpb0 if which == 1 else None

        def delta_cb(oc):
            par = oc % 2
            if oc >= 2:
                nc.tensor.wait_ge(v_res.h, vrb + oc - 1)
            if which == 1:
                nc.tensor.wait_ge(dwp[oc % 2].h, dwpb[oc % 2] + 16 * (oc // 2 + 1))
            ins = None
            for (t0, t1) in ct:
                ins = nc.tensor.matmul(pD[par][:, t0:t1],
                                       wb_sb[:, oc * 128:(oc + 1) * 128],
                                       midg_sb[:, t0:t1], start=True,
                                       stop=(which == 0), skip_group_check=True)
            if which == 1:
                for kc in range(NCH):
                    w = wup_sb[:, par * NCH * 128 + kc * 128:
                               par * NCH * 128 + (kc + 1) * 128]
                    for (t0, t1) in ct:
                        ins = nc.tensor.matmul(pD[par][:, t0:t1], w,
                                               xn_sb[:, kc * T + t0:kc * T + t1],
                                               start=False, stop=(kc == NCH - 1),
                                               skip_group_check=True)
            t_d.inc(ins)
            if which == 1 and oc + 2 < NCH:
                nc.gpsimd.wait_ge(t_d.h, t_d.v)
                load_panel(oc + 2, par)
            return pD[par][:, :], t_d

        emit_residual_stream(p, xfm, xout, T, delta_cb, scale=SCALE,
                             ready=v_ap)

    if debug:
        dbg = p.sem("dbg")
        d_xn = p.out("d_xn", (D, T), BF16)
        d_lg = p.out("d_lg", (E, T))
        d_g = p.out("d_g", (E, T), BF16)
        d_midg = p.out("d_midg", (ER, T), BF16)
        nc.sync.wait_ge(p.sem("res_st0").h, p.sem("res_st0").v)
        nc.sync.wait_ge(p.sem("res_st1").h, p.sem("res_st1").v)
        dbg.inc(nc.sync.dma_start(out=d_xn.rearrange("(c p) t -> p c t", p=128),
                                  in_=xn_sb[:, :].rearrange("p (c t) -> p c t", t=T)), 16)
        dbg.inc(nc.sync.dma_start(out=d_lg, in_=lg_row[:, :]), 16)
        dbg.inc(nc.sync.dma_start(out=d_g, in_=gT_sb[:, :]), 16)
        dbg.inc(nc.sync.dma_start(out=d_midg, in_=midg_sb[:, :]), 16)
        nc.sync.wait_ge(dbg.h, dbg.v)
    return p.done()



def _bf(x):
    return np.ascontiguousarray(np.asarray(x), dtype=BF)


def _f32(x):
    return np.ascontiguousarray(np.asarray(x), dtype=np.float32)


def blockify(w):
    K, M = w.shape
    b = w.reshape(K // 128, 128, M // 128, 128).transpose(2, 1, 0, 3)
    return np.ascontiguousarray(b)


def ffn_inmaps(which, inputs, xs):
    hc = host_const_arrays()
    if which == 0:
        A = np.asarray(inputs["L0_ffn_A"])
        Bm = np.asarray(inputs["L0_ffn_B"])
        R = 16
        base = {
            "wrt": _f32(np.asarray(inputs["L0_ffn_Wr"]).T),
            "wa": _f32(A.transpose(2, 0, 1).reshape(4096, 4 * R)),
            "wb": _f32(Bm.transpose(0, 2, 1).reshape(4 * R, 4096)),
        }
    else:
        A = np.asarray(inputs["L1_ffn_lora_A"])
        Bm = np.asarray(inputs["L1_ffn_lora_B"])
        R = 8
        base = {
            "wrt": _f32(np.asarray(inputs["L1_ffn_gate_w"]).T),
            "wa": _f32(A.transpose(2, 0, 1).reshape(4096, 4 * R)),
            "wb": _f32(Bm.transpose(0, 2, 1).reshape(4 * R, 4096)),
            "wup": _bf(blockify(np.asarray(inputs["L1_ffn_W_up"]).T)),
            "ebias": _f32(np.asarray(inputs["L1_ffn_expert_bias"]).reshape(1, 4)),
        }
    base["c_expe"] = _bf((np.arange(4)[:, None] == (np.arange(4 * R)[None, :] // R))
                         .astype(np.float32))
    base["c_onesb"] = hc["onesb"]
    base["c_ones1f"] = hc["ones1f"]
    base["c_identf"] = hc["identf"]
    return [dict(base, xfm=xs[c]) for c in range(NCORES)]


def shard_x(x):
    out = []
    for c in range(NCORES):
        b, h = c // 2, c % 2
        out.append(_f32(x[b, h * TPC:(h + 1) * TPC, :].T))
    return out


def unshard_x(xs):
    x = np.empty((4, 2048, 4096), np.float32)
    for c in range(NCORES):
        b, h = c // 2, c % 2
        x[b, h * TPC:(h + 1) * TPC, :] = xs[c].T
    return x


_prog_cache = {}


def get_prog(key, builder, *args):
    if key not in _prog_cache:
        _prog_cache[key] = builder(*args)
    return _prog_cache[key]


def run(nc, in_maps):
    res = run_bass_kernel_spmd(nc, in_maps, core_ids=list(range(NCORES)))
    return res.results



def kernel(**inputs):
    _ = inputs["L0_attn_a_residual"]
    x_full = np.asarray(inputs["L0_attn_a_hidden_states"], dtype=np.float32)
    nc = get_prog("mega", build_mega)
    res = run(nc, mega_inmaps(inputs, x_full))
    return unshard_x([r["xout"] for r in res])


def kernel_unfused(**inputs):
    x_full = np.asarray(inputs["L0_attn_a_hidden_states"], dtype=np.float32)
    xs = shard_x(x_full)
    ra = run(get_prog("mamba_a", build_mamba_a), mamba_inmaps_a(inputs, xs, x_full))
    rb = run(get_prog("mamba_b", build_mamba_b), mamba_inmaps_b(inputs, ra))
    rc = run(get_prog("mamba_c", build_mamba_c), mamba_inmaps_c(inputs, xs, ra, rb))
    xs = [r["xout"] for r in rc]
    res = run(get_prog("ffn0", build_ffn, 0), ffn_inmaps(0, inputs, xs))
    xs = [r["xout"] for r in res]
    res = run(get_prog("attnb0", build_attnb0), attnb0_inmaps(inputs, xs))
    xs = [r["xout"] for r in res]
    res = run(get_prog("ffn0", build_ffn, 0), ffn_inmaps(0, inputs, xs))
    xs = [r["xout"] for r in res]
    ra = run(get_prog("gqa_a", build_gqa_a), gqa_inmaps_a(inputs, xs))
    rb = run(get_prog("gqa_b", build_gqa_b), gqa_inmaps_b(ra))
    rc = run(get_prog("dense_out", build_dense_out), gqa_inmaps_c(inputs, xs, rb))
    xs = [r["xout"] for r in rc]
    res = run(get_prog("ffn1", build_ffn, 1), ffn_inmaps(1, inputs, xs))
    xs = [r["xout"] for r in res]
    ra = run(get_prog("gla_a", build_gla_a), gla_inmaps_a(inputs, xs))
    rb = run(get_prog("gla_b", build_gla_b), gla_inmaps_b(ra))
    rc = run(get_prog("gla_c", build_gla_c), gla_inmaps_c(inputs, xs, rb))
    xs = [r["xout"] for r in rc]
    res = run(get_prog("ffn1", build_ffn, 1), ffn_inmaps(1, inputs, xs))
    xs = [r["xout"] for r in res]
    return unshard_x(xs)



class _MegaDone(Exception):
    pass


def build_mega(T=TPC, upto=99):
    p = LP("mega")
    p.mega = True
    nc = p.nc
    _step = [0]

    def ck():
        _step[0] += 1
        if _step[0] > upto:
            raise _MegaDone()

    def scr(name, shape, dt=F32):
        return nc.dram_tensor(name, list(shape), dt, kind="Internal").ap()

    def ext(name, shape, dt=F32):
        return nc.dram_tensor(name, list(shape), dt, kind="ExternalInput").ap()

    dma_c = p.sem("dma_c")
    p._cst_cache = {}
    for key, shape, dt in [("onesb", (128, 1), BF16), ("ones1f", (1, 128), F32),
                           ("identf", (128, 128), F32),
                           ("identb", (128, 128), BF16),
                           ("expe", (4, 64), BF16), ("expe", (4, 32), BF16),
                           ("ltri", (128, 128), BF16),
                           ("ltrib", (128, 128), BF16)]:
        nm = f"c_{key}_{shape[0]}x{shape[1]}"
        ap = ext(nm, shape, dt)
        sb = p.sb("cst_" + nm, list(shape), dt)
        dma_c.inc(nc.sync.dma_start(out=sb[:], in_=ap), 16)
        p._cst_cache[(key, tuple(shape))] = sb
    psel_ap = ext("psel", (128, 8))
    psel_sb = p.sb("psel_sb", [128, 8], F32)
    dma_c.inc(nc.sync.dma_start(out=psel_sb[:, :], in_=psel_ap), 16)
    for e in [nc.vector, nc.scalar, nc.tensor, nc.gpsimd, nc.sync]:
        e.wait_ge(dma_c.h, dma_c.v)

    x1 = scr("x1", (D, T)); x2 = scr("x2", (D, T)); x3 = scr("x3", (D, T))
    x4 = scr("x4", (D, T)); x5 = scr("x5", (D, T)); x6 = scr("x6", (D, T))
    x7 = scr("x7", (D, T))
    zs = scr("zs", (D, T), BF16)
    xbcs = scr("xbcs", (34 * 128, T), BF16)
    dts = scr("dts", (64, T), BF16)
    dtas = scr("dtas", (64, T), BF16)
    ys = scr("ys", (T, D), BF16)
    hout = scr("hout", (128, D), F32)
    elg = scr("elg", (T, 64), F32)
    hout_all = scr("hout_all", (8 * 128, D), F32)
    hin_s = scr("hin_s", (128, D), F32)
    gxq = scr("gxq", (D, T), BF16)
    gxk = scr("gxk", (1024, T), BF16)
    gxv = scr("gxv", (T, 1024), BF16)
    kprep = scr("kprep", (2, 8, 128, T), BF16)
    vprep = scr("vprep", (2, 8, 128, T), BF16)
    kex = scr("kex", (2, 8, 128, T), BF16)
    vex = scr("vex", (2, 8, 128, T), BF16)
    xctx = scr("xctx", (D, T), BF16)
    lxq = scr("lxq", (D, T), BF16)
    lxk = scr("lxk", (1024, T), BF16)
    lxv = scr("lxv", (1024, T), BF16)
    ktp = scr("ktp", (1024, 64), BF16)
    vtp = scr("vtp", (1024, 64), BF16)
    ktail_all = scr("ktail_all", (8 * 1024, 64), BF16)
    vtail_all = scr("vtail_all", (8 * 1024, 64), BF16)
    ktail_s = scr("ktail_s", (1024, 128), BF16)
    vtail_s = scr("vtail_s", (1024, 128), BF16)
    obase = scr("obase", (T, D), BF16)
    qgs = scr("qgs", (D, T), BF16)
    sloc = scr("sloc", (1024, 128), F32)
    sloc_all = scr("sloc_all", (8 * 1024, 128), F32)
    sin_s = scr("sin_s", (1024, 128), F32)
    xout = nc.dram_tensor("xout", [D, T], F32, kind="ExternalOutput").ap()

    coll_r = p.sem("coll_r")

    def coll(out_ap, in_ap, kind="AllGather"):
        ctx = (aot_env.allow_ici_all_to_all() if kind == "AllToAll"
               else contextlib.nullcontext())
        with ctx, nc.named_scope("coll"):
            coll_r.inc(nc.gpsimd.collective_compute(
                kind, ALU.bypass, replica_groups=[list(range(8))],
                ins=[in_ap], outs=[out_ap]))

    _secn = [0]

    def sec(fn, io, *args, **kw):
        tag = f"s{_secn[0]:02d}_{getattr(fn, '__name__', 'f')}"
        _secn[0] += 1
        with nc.named_scope(tag):
            p.begin_section(io)
            fn(*args, p=p, **kw)
            p.end_section()

    def emit_select8(tag, slot_ap_fn, dst_view, ncols, dt, gdim=None,
                     zero_view=None):
        p.begin_section({})
        _sid, _ = nc.enter_named_scope("sel_" + tag, False)
        d_s = [p.sem("d_in0"), p.sem("d_in1")]
        v_s = p.sem("v_e")
        s_st = p.sem("s_o0")
        nc.sync.wait_ge(coll_r.h, coll_r.v)

        def v3(sb2d):
            return (sb2d if gdim is None
                    else sb2d.rearrange("p (g v) -> p g v", v=gdim))

        with p.sbuf(f"{tag}_acc", [128, ncols], F32) as acc, \
             p.sbuf(f"{tag}_acco", [128, ncols], dt) as acco, \
             p.sbuf(f"{tag}_sl", [128, 2 * ncols], dt) as sl:
            p.g(nc.vector, nc.vector.memset(acc[:, :], 0.0))
            for r in range(8):
                par = r % 2
                if r >= 2:
                    nc.sync.wait_ge(v_s.h, v_s.v - 1)
                d_s[par].inc(nc.sync.dma_start(
                    out=v3(sl[:, par * ncols:(par + 1) * ncols]),
                    in_=slot_ap_fn(r)), 16)
                nc.vector.wait_ge(d_s[par].h, d_s[par].v)
                v_s.inc(nc.vector.scalar_tensor_tensor(
                    acc[:, :], sl[:, par * ncols:(par + 1) * ncols],
                    psel_sb[:, r:r + 1], acc[:, :], ALU.mult, ALU.add))
                nc.vector.wait_ge(v_s.h, v_s.v)
            v_s.inc(nc.vector.tensor_copy(acco[:, :], acc[:, :]))
            ins = nc.sync.dma_start(out=dst_view, in_=v3(acco[:, :]))
            ins._wait_ge(v_s.h, v_s.v)
            s_st.inc(ins, 16)
            if zero_view is not None:
                p.g(nc.vector, nc.vector.memset(sl[:, 0:ncols], 0.0))
                v_s.inc(nc.vector.engine_nop())
                ins = nc.sync.dma_start(out=zero_view, in_=v3(sl[:, 0:ncols]))
                ins._wait_ge(v_s.h, v_s.v)
                s_st.inc(ins, 16)
            nc.sync.wait_ge(s_st.h, s_st.v)
        nc.leave_named_scope("sel_" + tag, _sid, False)
        p.end_section()

    mie = mega_static_ext()
    exts = {}

    def eio(tag, keys):
        io = {}
        for k in keys:
            nm = f"{tag}_{k}"
            if nm not in exts:
                shape, dt = mie[nm]
                exts[nm] = ext(nm, shape, dt)
            io[k] = exts[nm]
        return io

    x0s = ext("x0s", (D, T), F32)

    try:
        _build_mega_body(p, nc, T, ck, sec, coll, emit_select8, eio, exts, ext,
                         mie, locals())
    except _MegaDone:
        pass
    p.mega = False
    return p.done()


def _build_mega_body(p, nc, T, ck, sec, coll, emit_select8, eio, exts, ext,
                     mie, L):
    (x0s, x1, x2, x3, x4, x5, x6, x7, zs, xbcs, dts, dtas, ys, hout, elg,
     hout_all, hin_s, gxq, gxk, gxv, kprep, vprep, kex, vex, xctx, lxq, lxk,
     lxv, ktp, vtp, ktail_all, vtail_all, ktail_s, vtail_s, obase, qgs, sloc,
     sloc_all, sin_s, xout) = (
        L["x0s"], L["x1"], L["x2"], L["x3"], L["x4"], L["x5"], L["x6"],
        L["x7"], L["zs"], L["xbcs"], L["dts"], L["dtas"], L["ys"], L["hout"],
        L["elg"], L["hout_all"], L["hin_s"], L["gxq"], L["gxk"], L["gxv"],
        L["kprep"], L["vprep"], L["kex"], L["vex"], L["xctx"], L["lxq"],
        L["lxk"], L["lxv"], L["ktp"], L["vtp"], L["ktail_all"],
        L["vtail_all"], L["ktail_s"], L["vtail_s"], L["obase"], L["qgs"],
        L["sloc"], L["sloc_all"], L["sin_s"], L["xout"])

    io = eio("ma", ["xfm", "wproj", "mupt", "lnw", "wconv", "dtb", "alog"])
    io.update(zs=zs, xbcs=xbcs, dts=dts, dtas=dtas)
    sec(build_mamba_a, io)
    ck()
    io = eio("mb", ["drow"])
    io.update(xbcs=xbcs, dts=dts, dtas=dtas, ys=ys, hout=hout, elg=elg)
    sec(build_mamba_b, io)
    ck()
    coll(hout_all, hout[:, :])
    ck()
    emit_select8("hsel", lambda r: hout_all[r * 128:(r + 1) * 128, :],
                 hin_s[:, :], D, F32)
    ck()
    io = eio("mc", ["nwrow", "wout"])
    io.update(xfm=x0s, ys=ys, elg=elg, hin=hin_s, zsp=zs, xbcs=xbcs, xout=x1)
    sec(build_mamba_c, io)
    ck()
    io0 = eio("f0", ["wrt", "wa", "wb"])
    sec(build_ffn, dict(io0, xfm=x1, xout=x2), 0)
    ck()
    io = eio("ab", ["wq32", "bq", "wk", "bk", "wv", "bv", "bige"])
    io.update(xfm=x2, xout=x3)
    sec(build_attnb0, io)
    ck()
    sec(build_ffn, dict(io0, xfm=x3, xout=x4), 0)
    ck()
    io = eio("ga", ["wq", "wkv"])
    io.update(xfm=x4, xq=gxq, xk=gxk, xv=gxv, kprep=kprep, vprep=vprep)
    sec(build_gqa_a, io)
    ck()
    for j in range(2):
        coll(kex[j].rearrange("s f t -> (s f) t"),
             kprep[j].rearrange("s f t -> (s f) t"), kind="AllToAll")
        coll(vex[j].rearrange("s q t -> (s q) t"),
             vprep[j].rearrange("s q t -> (s q) t"), kind="AllToAll")
    ck()
    for nmm in ("c_onesbk", "c_ones1fk"):
        shape, dt = mie[nmm]
        exts[nmm] = ext(nmm, shape, dt)
    io = eio("gb", ["vmask"])
    io.update(xq=gxq, kex=kex, vex=vex, xctx=xctx,
              c_onesbk=exts["c_onesbk"], c_ones1fk=exts["c_ones1fk"])
    sec(build_gqa_b, io)
    ck()
    io = eio("gc", ["w", "bias"])
    io.update(xfm=x4, xin=xctx, xout=x5)
    sec(build_dense_out, io)
    ck()
    io1 = eio("f1", ["wrt", "wa", "wb", "wup", "ebias"])
    sec(build_ffn, dict(io1, xfm=x5, xout=x6), 1)
    ck()
    io = eio("la", ["wq", "wk", "wv"])
    io.update(xfm=x6, xq=lxq, xk=lxk, xv=lxv, ktp=ktp, vtp=vtp)
    sec(build_gla_a, io)
    ck()
    coll(ktail_all, ktp[:, :])
    coll(vtail_all, vtp[:, :])
    emit_select8(
        "ktsel", lambda r: ktail_all[r * 1024:(r + 1) * 1024]
        .rearrange("(g pp) v -> pp g v", pp=128),
        ktail_s.rearrange("(g pp) v -> pp g v", pp=128)[:, :, 64:128],
        512, BF16, gdim=64,
        zero_view=ktail_s.rearrange("(g pp) v -> pp g v", pp=128)[:, :, 0:64])
    emit_select8(
        "vtsel", lambda r: vtail_all[r * 1024:(r + 1) * 1024]
        .rearrange("(g pp) v -> pp g v", pp=128),
        vtail_s.rearrange("(g pp) v -> pp g v", pp=128)[:, :, 64:128],
        512, BF16, gdim=64,
        zero_view=vtail_s.rearrange("(g pp) v -> pp g v", pp=128)[:, :, 0:64])
    ck()
    io = eio("lb", ["qcos", "qsin", "kcos", "ksin", "swm"])
    io.update(xq=lxq, xk=lxk, xv=lxv, ktail=ktail_s, vtail=vtail_s,
              obase=obase, qgs=qgs, sloc=sloc)
    sec(build_gla_b, io)
    ck()
    coll(sloc_all, sloc[:, :])
    emit_select8(
        "ssel", lambda r: sloc_all[r * 1024:(r + 1) * 1024]
        .rearrange("(g pp) v -> pp g v", pp=128),
        sin_s.rearrange("(g pp) v -> pp g v", pp=128), 1024, F32, gdim=128)
    ck()
    io = eio("lc", ["wo"])
    io.update(xfm=x6, ob=obase, qgs=qgs, sin=sin_s, xout=x7)
    sec(build_gla_c, io)
    ck()
    sec(build_ffn, dict(io1, xfm=x7, xout=xout), 1)


def mega_static_ext():
    E0, ER0, E1, ER1 = 4, 64, 4, 32
    return {
        "ma_xfm": ((D, TPC + 4), F32), "ma_wproj": ((67, 128, NCH, 128), BF16),
        "ma_mupt": ((128, 67), F32), "ma_lnw": ((128, NCH), F32),
        "ma_wconv": ((128, 34 * 4), F32), "ma_dtb": ((128, 1), F32),
        "ma_alog": ((128, 1), F32),
        "mb_drow": ((1, 64), F32),
        "mc_nwrow": ((1, D), F32), "mc_wout": ((NCH, 128, NCH, 128), BF16),
        "f0_wrt": ((D, E0), F32), "f0_wa": ((D, ER0), F32),
        "f0_wb": ((ER0, D), F32),
        "ab_wq32": ((32, 128, NCH, 128), BF16), "ab_bq": ((128, 32), F32),
        "ab_wk": ((D, 128), BF16), "ab_bk": ((128, 1), F32),
        "ab_wv": ((D, 128), BF16), "ab_bv": ((128, 1), F32),
        "ab_bige": ((128, D), BF16),
        "ga_wq": ((NCH, 128, NCH, 128), BF16),
        "ga_wkv": ((16, 128, NCH, 128), BF16),
        "gb_vmask": ((2 * TPC, TPC), BF16),
        "gc_w": ((NCH, 128, NCH, 128), BF16), "gc_bias": ((D, 1), F32),
        "f1_wrt": ((D, E1), F32), "f1_wa": ((D, ER1), F32),
        "f1_wb": ((ER1, D), F32), "f1_wup": ((NCH, 128, NCH, 128), BF16),
        "f1_ebias": ((1, E1), F32),
        "la_wq": ((NCH, 128, NCH, 128), BF16), "la_wk": ((8, 128, NCH, 128), BF16),
        "la_wv": ((8, 128, NCH, 128), BF16),
        "lb_qcos": ((TPC, 128), F32), "lb_qsin": ((TPC, 128), F32),
        "lb_kcos": ((TPC + 64, 128), F32), "lb_ksin": ((TPC + 64, 128), F32),
        "lb_swm": ((3, 128, 64), F32),
        "lc_wo": ((NCH, 128, NCH, 128), BF16),
        "c_onesbk": ((128, 1), BF16), "c_ones1fk": ((1, 128), F32),
    }


def mega_inmaps(inputs, x_full):
    xs = shard_x(x_full)
    hc = host_const_arrays()
    ma = mamba_inmaps_a(inputs, xs, x_full)
    mc = mamba_inmaps_c(inputs, xs,
                        [dict(zs=None, xbcs=None)] * NCORES,
                        [dict(ys=None, elg=None, hout=np.zeros((128, D),
                                                              np.float32))]
                        * NCORES)
    f0 = ffn_inmaps(0, inputs, xs)
    f1 = ffn_inmaps(1, inputs, xs)
    ab = attnb0_inmaps(inputs, xs)
    ga = gqa_inmaps_a(inputs, xs)
    gc = gqa_inmaps_c(inputs, xs, [dict(xctx=None)] * NCORES)
    la = gla_inmaps_a(inputs, xs)
    rt = _rope_tables(TPC)
    consts = {
        "c_onesb_128x1": hc["onesb"], "c_ones1f_1x128": hc["ones1f"],
        "c_identf_128x128": hc["identf"], "c_identb_128x128": hc["identb"],
        "c_expe_4x64": _bf((np.arange(4)[:, None] ==
                            (np.arange(64)[None, :] // 16)).astype(np.float32)),
        "c_expe_4x32": _bf((np.arange(4)[:, None] ==
                            (np.arange(32)[None, :] // 8)).astype(np.float32)),
        "c_ltri_128x128": _bf(np.triu(np.ones((128, 128), np.float32))),
        "c_ltrib_128x128": _bf(np.triu(np.ones((128, 128), np.float32))),
        "c_onesbk": hc["onesb"], "c_ones1fk": hc["ones1f"],
        "mb_drow": _f32(np.asarray(inputs["L0_attn_a_D"]).reshape(1, 64)),
    }
    maps = []
    for c in range(NCORES):
        b, h = c // 2, c % 2
        m = dict(consts)
        m["x0s"] = _f32(xs[c])
        for k in ["xfm", "wproj", "mupt", "lnw", "wconv", "dtb", "alog"]:
            m[f"ma_{k}"] = ma[c][k]
        for k in ["nwrow", "wout"]:
            m[f"mc_{k}"] = mc[c][k]
        for k in ["wrt", "wa", "wb"]:
            m[f"f0_{k}"] = f0[c][k]
        for k in ["wrt", "wa", "wb", "wup", "ebias"]:
            m[f"f1_{k}"] = f1[c][k]
        for k in ["wq32", "bq", "wk", "bk", "wv", "bv", "bige"]:
            m[f"ab_{k}"] = ab[c][k]
        for k in ["wq", "wkv"]:
            m[f"ga_{k}"] = ga[c][k]
        for k in ["w", "bias"]:
            m[f"gc_{k}"] = gc[c][k]
        for k in ["wq", "wk", "wv"]:
            m[f"la_{k}"] = la[c][k]
        m["lc_wo"] = _bf(blockify(np.asarray(inputs["L1_attn_b_Wo"])))
        kpos = np.arange(2 * TPC)[:, None]
        qpos = h * TPC + np.arange(TPC)[None, :]
        m["gb_vmask"] = _bf((kpos <= qpos).astype(np.float32))
        qc, qs_, kc, ks_ = rt[h]
        m["lb_qcos"], m["lb_qsin"] = qc, qs_
        m["lb_kcos"], m["lb_ksin"] = kc, ks_
        m["lb_swm"] = _swa_masks(h == 1)
        psel = np.zeros((128, 8), np.float32)
        if h == 1:
            psel[:, c - 1] = 1.0
        m["psel"] = psel
        maps.append(m)
    return maps



SC128 = 1.0 / float(np.sqrt(128.0))


def build_attnb0(T=TPC, debug=False, skip_attn=0, p=None):
    p = p or LP("attnb0")
    nc = p.nc
    xfm = p.inp("xfm", (D, T))
    wq32 = p.inp("wq32", (32, 128, NCH, 128), BF16)
    bq = p.inp("bq", (128, 32))
    wk = p.inp("wk", (D, 128), BF16)
    bk = p.inp("bk", (128, 1))
    wv = p.inp("wv", (D, 128), BF16)
    bv = p.inp("bv", (128, 1))
    bige = p.inp("bige", (128, D), BF16)
    xout = p.out("xout", (D, T))

    cst, dma_c = load_consts(p, [
        ("onesb", (128, 1), BF16), ("ones1f", (1, 128), F32),
        ("identb", (128, 128), BF16),
    ])
    dmaw = p.sem("dmaw")
    dmq = p.sem("dmq")

    nt = T // 128
    xn_sb = p.sb("xn_sb", [128, NCH * T], BF16)
    wqsb = p.sb("wqsb", [128, NCH * 128], BF16)
    wk_sb = p.sb("wk_sb", [128, NCH * 128], BF16)
    wv_sb = p.sb("wv_sb", [128, NCH * 128], BF16)
    bige_sb = p.sb("bige_sb", [128, D], BF16)
    brows = p.sb("brows", [128, 35], F32)
    qsT_sb = p.sb("qsT_sb", [128, T], BF16)
    kT_sb = p.sb("kT_sb", [128, T], BF16)
    v2_sb = p.sb("v2_sb", [128, 2 * T], BF16)
    q_tok = p.sb("q_tok", [128, T], BF16)
    k_tok = p.sb("k_tok", [128, T], BF16)
    q_st = p.sb("q_st", [128, 8192], BF16)
    k_st = p.sb("k_st", [128, 8192], BF16)
    e4_sb = p.sb("e4_sb", [128, 2 * 512], BF16)
    rd_sb = p.sb("rd_sb", [128, 128], F32)
    oT_sb = p.sb("oT_sb", [128, T], BF16)

    with p.sbuf("wqs_f32", [128, NCH * 128], F32) as wqs, \
         p.sbuf("wq_pan", [128, 2 * NCH * 128], BF16) as wqp:
        dmqs = [p.sem(f"dmq{s_}") for s_ in range(2)]
        v_acc = p.sem("v_wqacc")
        NW = NCH * 128
        for h in range(32):
            par_h = h % 2
            sem = dmqs[par_h]
            if h >= 2:
                nc.gpsimd.wait_ge(v_acc.h, v_acc.v - 1)
            sem.inc(nc.gpsimd.dma_start(
                out=wqp[:, par_h * NW:(par_h + 1) * NW], in_=wq32[h]), 16)
            nc.vector.wait_ge(sem.h, sem.v)
            if h == 0:
                v_acc.inc(nc.vector.tensor_copy(wqs[:, :], wqp[:, 0:NW]))
            else:
                v_acc.inc(nc.vector.tensor_tensor(
                    wqs[:, :], wqs[:, :], wqp[:, par_h * NW:(par_h + 1) * NW],
                    ALU.add))
            nc.vector.wait_ge(v_acc.h, v_acc.v)
        for sem in dmqs:
            dmq.inc(nc.gpsimd.dma_start(out=wk_sb[0:1, 0:1], in_=wk[0:1, 0:1]), 16)             if False else None
        dmaw.inc(nc.gpsimd.dma_start(
            out=wk_sb[:, :].rearrange("p (c d) -> p c d", d=128),
            in_=wk.rearrange("(c p) d -> p c d", p=128)), 16)
        dmaw.inc(nc.gpsimd.dma_start(
            out=wv_sb[:, :].rearrange("p (c d) -> p c d", d=128),
            in_=wv.rearrange("(c p) d -> p c d", p=128)), 16)
        dmaw.inc(nc.gpsimd.dma_start(out=bige_sb[:, :], in_=bige), 16)
        dmab = p.sem("dmab")
        dmab.inc(nc.sync.dma_start(out=brows[:, 0:32], in_=bq), 16)
        dmab.inc(nc.sync.dma_start(out=brows[:, 33:34], in_=bk), 16)
        dmab.inc(nc.sync.dma_start(out=brows[:, 34:35], in_=bv), 16)

        for e in [nc.vector, nc.scalar, nc.tensor, nc.gpsimd, nc.sync]:
            e.wait_ge(dma_c.h, dma_c.v)

        v_w = p.sem("v_w")
        nc.vector.wait_ge(dmaw.h, dmaw.v)
        nc.vector.wait_ge(p.sem("dmab").h, p.sem("dmab").v)
        p.g(nc.vector, nc.vector.reduce_sum(brows[:, 32:33], brows[:, 0:32], axis=AX.X))
        v_w.inc(nc.vector.tensor_copy(wqsb[:, :], wqs[:, :]))
        nc.vector.wait_ge(v_w.h, v_w.v)

    for e in [nc.sync, nc.scalar, nc.tensor, nc.gpsimd]:
        e.wait_ge(p.sem("v_w").h, p.sem("v_w").v)
    v_ap = emit_ln(p, xfm, xn_sb, T, cst)

    bqs_col = brows[:, 32:33]
    bk_col = brows[:, 33:34]
    bv_col = brows[:, 34:35]
    ct = col_tiles(T)
    v_w = p.sem("v_w")
    t_q = p.sem("t_q")
    v_qe = p.sem("v_qe")
    t_tok = p.sem("t_tok")
    v_tok = p.sem("v_tok")
    d_stp = [p.sem("d_st0"), p.sem("d_st1")]
    t_op = p.sem("t_op")
    topb = t_op.v
    a_exp = p.sem("a_exp")
    t_nd = p.sem("t_nd")
    v_o = p.sem("v_o")
    t_d = p.sem("t_d")

    v2v = v2_sb[:, :].rearrange("p (t two) -> p t two", two=2)
    vm = p.sem("v_memset")
    vm.inc(nc.vector.memset(v2v[:, :, 1], 1.0))

    nc.tensor.wait_ge(v_ap.h, v_ap.v)
    nc.tensor.wait_ge(v_w.h, v_w.v)
    nc.tensor.wait_ge(vm.h, vm.v)

    with (
        p.psum("pQ", [128, T], F32) as pQ,
        p.psum("pK", [128, T], F32) as pK,
        p.psum("pV", [128, T], F32) as pV,
    ):
        for c in range(NCH):
            first, last = (c == 0), (c == NCH - 1)
            xnc = xn_sb[:, c * T:(c + 1) * T]
            ins = None
            for (t0, t1) in ct:
                nc.tensor.matmul(pQ[:, t0:t1], wqsb[:, c * 128:(c + 1) * 128],
                                 xnc[:, t0:t1], start=first, stop=last,
                                 skip_group_check=True)
                nc.tensor.matmul(pK[:, t0:t1], wk_sb[:, c * 128:(c + 1) * 128],
                                 xnc[:, t0:t1], start=first, stop=last,
                                 skip_group_check=True)
                ins = nc.tensor.matmul(pV[:, t0:t1], wv_sb[:, c * 128:(c + 1) * 128],
                                       xnc[:, t0:t1], start=first, stop=last,
                                       skip_group_check=True)
            if last:
                t_q.inc(ins)
        nc.vector.wait_ge(t_q.h, t_q.v)
        nc.vector.tensor_scalar(qsT_sb[:, :], pQ[:, :], bqs_col, SC128,
                                ALU.add, ALU.mult)
        nc.vector.tensor_scalar(kT_sb[:, :], pK[:, :], bk_col, None, ALU.add)
        v_qe.inc(nc.vector.tensor_scalar(v2v[:, :, 0], pV[:, :], bv_col, None,
                                         ALU.add))

    with p.psum("pT1a", [128, 128], BF16) as pT1a, \
         p.psum("pT1b", [128, 128], BF16) as pT1b:
        pT1 = [pT1a, pT1b]
        nc.tensor.wait_ge(v_qe.h, v_qe.v)
        for i in range(nt):
            par = i % 2
            if i >= 2:
                nc.tensor.wait_ge(v_tok.h, v_tok.v - 1)
            t_tok.inc(nc.tensor.transpose(pT1[par][:, :],
                                          qsT_sb[:, i * 128:(i + 1) * 128],
                                          cst["identb"][:, :]))
            nc.vector.wait_ge(t_tok.h, t_tok.v)
            v_tok.inc(nc.vector.tensor_copy(q_tok[:, i * 128:(i + 1) * 128],
                                            pT1[par][:, :]))
    with p.psum("pT2a", [128, 128], BF16) as pT2a, \
         p.psum("pT2b", [128, 128], BF16) as pT2b:
        pT2 = [pT2a, pT2b]
        nc.tensor.wait_ge(v_tok.h, v_tok.v)
        for i in range(nt):
            par = i % 2
            if i >= 2:
                nc.tensor.wait_ge(v_tok.h, v_tok.v - 1)
            t_tok.inc(nc.tensor.transpose(pT2[par][:, :],
                                          kT_sb[:, i * 128:(i + 1) * 128],
                                          cst["identb"][:, :]))
            nc.vector.wait_ge(t_tok.h, t_tok.v)
            v_tok.inc(nc.vector.tensor_copy(k_tok[:, i * 128:(i + 1) * 128],
                                            pT2[par][:, :]))

    with (
        p.psum("pOP0", [128, 512], F32) as pOP0,
        p.psum("pOP1", [128, 512], F32) as pOP1,
        p.psum("pND0", [128, 256], F32) as pND0,
        p.psum("pND1", [128, 256], F32) as pND1,
    ):
        pOP = [pOP0, pOP1]
        pND = [pND0, pND1]
        nc.tensor.wait_ge(v_tok.h, v_tok.v)
        nc.sync.wait_ge(v_tok.h, v_tok.v)
        if skip_attn & 1:
            vz = p.sem("v_zero")
            vz.inc(nc.vector.memset(oT_sb[:, :], 0.001))
            nc.tensor.wait_ge(vz.h, vz.v)
        for i in range(0 if (skip_attn & 1) else nt):
            par = i % 2
            for half in range(2):
                hh = 2 * i + half
                if hh >= 1:
                    nc.sync.wait_ge(t_op.h, topb + 16 * hh)
                    nc.sync.wait_ge(d_stp[0].h, d_stp[0].v)
                d_stp[0].inc(nc.sync.dma_start(
                    out=q_st[0:1, 0:8192].rearrange("p (g d) -> p g d", d=128),
                    in_=q_tok[64 * half:64 * half + 64, i * 128:(i + 1) * 128]), 16)
                d_stp[0].inc(nc.sync.dma_start(
                    out=k_st[0:1, 0:8192].rearrange("p (g d) -> p g d", d=128),
                    in_=k_tok[64 * half:64 * half + 64, i * 128:(i + 1) * 128]), 16)
                nc.tensor.wait_ge(d_stp[0].h, d_stp[0].v)
                for g in range(16):
                    gg = 16 * hh + g
                    if gg >= 2:
                        nc.tensor.wait_ge(a_exp.h, a_exp.v - 1)
                    ins = None
                    for u in range(4):
                        g2 = 4 * g + u
                        ins = nc.tensor.matmul(
                            pOP[gg % 2][:, u * 128:(u + 1) * 128],
                            k_st[0:1, g2 * 128:(g2 + 1) * 128],
                            q_st[0:1, g2 * 128:(g2 + 1) * 128],
                            start=True, stop=True, skip_group_check=True)
                    t_op.inc(ins)
                    nc.scalar.wait_ge(t_op.h, t_op.v)
                    a_exp.inc(nc.scalar.activation(
                        e4_sb[:, (gg % 2) * 512:(gg % 2 + 1) * 512],
                        pOP[gg % 2][:, :], AF.Exp))
                    nc.tensor.wait_ge(a_exp.h, a_exp.v)
                    ins = None
                    for u in range(4):
                        tloc = 64 * half + 4 * g + u
                        ins = nc.tensor.matmul(
                            pND[par][:, 2 * tloc:2 * tloc + 2],
                            e4_sb[:, (gg % 2) * 512 + u * 128:
                                  (gg % 2) * 512 + (u + 1) * 128],
                            v2_sb[:, 2 * (i * 128 + tloc):2 * (i * 128 + tloc) + 2],
                            start=True, stop=True, skip_group_check=True)
                    t_nd.inc(ins)
            nc.vector.wait_ge(t_nd.h, t_nd.v)
            pv = pND[par][:, :].rearrange("p (t two) -> p t two", two=2)
            p.g(nc.vector, nc.vector.reciprocal(rd_sb[:, :], pv[:, :, 1]))
            v_o.inc(nc.vector.tensor_tensor(oT_sb[:, i * 128:(i + 1) * 128],
                                            pv[:, :, 0], rd_sb[:, :], ALU.mult))
            nc.tensor.wait_ge(v_o.h, v_o.v)

    with (
        p.psum("pA0", [128, T], F32) as pA0,
        p.psum("pA1", [128, T], F32) as pA1,
    ):
        pA = [pA0, pA1]
        v_res = p.sem("res_vx")
        vrb996 = v_res.v

        def delta_cb(oc):
            par = oc % 2
            if oc >= 2:
                nc.tensor.wait_ge(v_res.h, vrb996 + oc - 1)
            ins = None
            for (t0, t1) in ct:
                ins = nc.tensor.matmul(pA[par][:, t0:t1],
                                       bige_sb[:, oc * 128:(oc + 1) * 128],
                                       oT_sb[:, t0:t1], start=True, stop=True,
                                       skip_group_check=True)
            t_d.inc(ins)
            return pA[par][:, :], t_d

        emit_residual_stream(p, xfm, xout, T, delta_cb, scale=1.0,
                             ready=v_ap)

    if debug:
        dbg = p.sem("dbg")
        d_qs = p.out("d_qs", (128, T), BF16)
        d_k = p.out("d_k", (128, T), BF16)
        d_o = p.out("d_o", (128, T), BF16)
        nc.sync.wait_ge(p.sem("res_st0").h, p.sem("res_st0").v)
        nc.sync.wait_ge(p.sem("res_st1").h, p.sem("res_st1").v)
        dbg.inc(nc.sync.dma_start(out=d_qs, in_=qsT_sb[:, :]), 16)
        dbg.inc(nc.sync.dma_start(out=d_k, in_=kT_sb[:, :]), 16)
        dbg.inc(nc.sync.dma_start(out=d_o, in_=oT_sb[:, :]), 16)
        nc.sync.wait_ge(dbg.h, dbg.v)
    return p.done()


def attnb0_inmaps(inputs, xs):
    hc = host_const_arrays()
    Wq = np.asarray(inputs["L0_attn_b_Wq"])
    d_idx = np.arange(128)
    m_idx = np.arange(4096)
    bige = (d_idx[:, None] == (m_idx[None, :] // 128) * 4 + (m_idx[None, :] % 128) // 32)
    base = {
        "wq32": _bf(Wq.reshape(32, 128, 128, 32).transpose(3, 1, 0, 2)),
        "bq": _f32(inputs["L0_attn_b_bq"]),
        "wk": _bf(np.asarray(inputs["L0_attn_b_Wk"]).reshape(4096, 128)),
        "bk": _f32(np.asarray(inputs["L0_attn_b_bk"]).reshape(128, 1)),
        "wv": _bf(np.asarray(inputs["L0_attn_b_Wv"]).reshape(4096, 128)),
        "bv": _f32(np.asarray(inputs["L0_attn_b_bv"]).reshape(128, 1)),
        "bige": _bf(bige.astype(np.float32)),
        "c_onesb": hc["onesb"], "c_ones1f": hc["ones1f"], "c_identb": hc["identb"],
    }
    return [dict(base, xfm=xs[c]) for c in range(NCORES)]



def build_gqa_a(T=TPC, p=None):
    p = p or LP("gqa_a")
    nc = p.nc
    xfm = p.inp("xfm", (D, T))
    wq = p.inp("wq", (NCH, 128, NCH, 128), BF16)
    wkv = p.inp("wkv", (16, 128, NCH, 128), BF16)
    xq = p.out("xq", (D, T), BF16)
    xk = p.out("xk", (8 * 128, T), BF16)
    xv = p.out("xv", (T, 8 * 128), BF16)
    kprep = p._io.get("kprep") if p._io is not None else None
    vprep = p._io.get("vprep") if p._io is not None else None

    cst, dma_c = load_consts(p, [
        ("onesb", (128, 1), BF16), ("ones1f", (1, 128), F32),
        ("identb", (128, 128), BF16),
    ])
    xn_sb = p.sb("xn_sb", [128, NCH * T], BF16)
    w_sb = p.sb("w_sb", [128, 2 * NCH * 128], BF16)
    stg = p.sb("stg", [128, 2 * T], BF16)
    vt_sb = p.sb("vt_sb", [128, T], BF16)

    for e in [nc.vector, nc.scalar, nc.tensor, nc.gpsimd, nc.sync]:
        e.wait_ge(dma_c.h, dma_c.v)
    v_ap = emit_ln(p, xfm, xn_sb, T, cst)
    nc.tensor.wait_ge(v_ap.h, v_ap.v)

    ct = col_tiles(T)
    dw = [p.sem("dw0"), p.sem("dw1")]
    t_p = p.sem("t_p")
    v_e = p.sem("v_e")
    dwb = [dw[0].v, dw[1].v]
    veb = v_e.v
    s_o = [p.sem("s_o0"), p.sem("s_o1")]
    t_tr = p.sem("t_tr")
    v_tr = p.sem("v_tr")
    s_v = p.sem("s_v")

    def load_panel(src, oc, par):
        dst = w_sb[:, par * NCH * 128:(par + 1) * NCH * 128].rearrange(
            "p (kc m) -> p kc m", m=128)
        if dw[par].v > 0:
            nc.gpsimd.wait_ge(t_p.h, t_p.v)
        dw[par].inc(nc.gpsimd.dma_start(out=dst, in_=src[oc]), 16)

    load_panel(wq, 0, 0)
    load_panel(wq, 1, 1)
    with (
        p.psum("pP0", [128, T], F32) as pP0,
        p.psum("pP1", [128, T], F32) as pP1,
        p.psum("pTv0", [128, 128], BF16) as pTv0,
        p.psum("pTv1", [128, 128], BF16) as pTv1,
    ):
        pP = [pP0, pP1]
        for pi in range(48):
            par = pi % 2
            is_q = pi < 32
            is_v = (not is_q) and ((pi - 32) % 2 == 1)
            nc.tensor.wait_ge(dw[par].h, dwb[par] + 16 * (pi // 2 + 1))
            if pi >= 2:
                nc.tensor.wait_ge(v_e.h, veb + pi - 1)
            ins = None
            for kc in range(NCH):
                w = w_sb[:, par * NCH * 128 + kc * 128:par * NCH * 128 + (kc + 1) * 128]
                for (t0, t1) in ct:
                    ins = nc.tensor.matmul(pP[par][:, t0:t1], w,
                                           xn_sb[:, kc * T + t0:kc * T + t1],
                                           start=(kc == 0), stop=(kc == NCH - 1),
                                           skip_group_check=True)
            t_p.inc(ins)
            if pi + 2 < 48:
                src, oc = (wq, pi + 2) if pi + 2 < 32 else (wkv, pi + 2 - 32)
                load_panel(src, oc, par)
            nc.vector.wait_ge(t_p.h, t_p.v)
            if pi >= 2:
                nc.vector.wait_ge(s_o[par].h, s_o[par].v)
            if is_v:
                v_e.inc(nc.vector.tensor_copy(vt_sb[:, :], pP[par][:, :]))
                g = (pi - 32) // 2
                nc.tensor.wait_ge(v_e.h, v_e.v)
                pTv = [pTv0, pTv1]
                for it in range(T // 128):
                    tpar = it % 2
                    if it >= 2:
                        nc.tensor.wait_ge(v_tr.h, v_tr.v - 1)
                    t_tr.inc(nc.tensor.transpose(pTv[tpar][:, :],
                                                 vt_sb[:, it * 128:(it + 1) * 128],
                                                 cst["identb"][:, :]))
                    nc.vector.wait_ge(t_tr.h, t_tr.v)
                    v_tr.inc(nc.vector.tensor_copy(
                        stg[:, par * T + it * 128:par * T + (it + 1) * 128],
                        pTv[tpar][:, :]))
                nc.sync.wait_ge(v_tr.h, v_tr.v)
                s_o[par].inc(nc.sync.dma_start(
                    out=xv.rearrange("(it p) (g d) -> it p g d", p=128, d=128)
                    [:, :, g, :].rearrange("it p d -> p it d") if False else
                    xv.rearrange("(it p) q -> p it q", p=128)
                    [:, :, g * 128:(g + 1) * 128],
                    in_=stg[:, par * T:(par + 1) * T].rearrange(
                        "p (it d) -> p it d", d=128)), 16)
                if vprep is not None:
                    for rc in (2 * (g // 2), 2 * (g // 2) + 1):
                        s_o[par].inc(nc.sync.dma_start(
                            out=vprep[g % 2, rc],
                            in_=stg[:, par * T:(par + 1) * T]), 16)
            else:
                scl = SC128 if is_q else 1.0
                v_e.inc(nc.vector.tensor_scalar(stg[:, par * T:(par + 1) * T],
                                                pP[par][:, :], scl, None, ALU.mult))
                nc.sync.wait_ge(v_e.h, v_e.v)
                dst = xq if is_q else xk
                row = pi if is_q else (pi - 32) // 2
                s_o[par].inc(nc.sync.dma_start(
                    out=fm_chunk(dst, row, T), in_=stg[:, par * T:(par + 1) * T]), 16)
                if (not is_q) and kprep is not None:
                    g = row
                    for rc in (2 * (g // 2), 2 * (g // 2) + 1):
                        s_o[par].inc(nc.sync.dma_start(
                            out=kprep[g % 2, rc],
                            in_=stg[:, par * T:(par + 1) * T]), 16)
        nc.sync.wait_ge(s_o[0].h, s_o[0].v)
        nc.sync.wait_ge(s_o[1].h, s_o[1].v)
    return p.done()


def build_gqa_b(T=TPC, p=None):
    TK = 2 * T
    p = p or LP("gqa_b")
    nc = p.nc
    xq = p.inp("xq", (D, T), BF16)
    kex = p._io.get("kex") if p._io is not None else None
    vex = p._io.get("vex") if p._io is not None else None
    if kex is None:
        xkh = p.inp("xkh", (32, 128, TK), BF16)
        xvh = p.inp("xvh", (32, TK, 128), BF16)
    vmask = p.inp("vmask", (TK, T), BF16)
    c_ones = p.inp("c_onesbk", (128, 1), BF16)
    c_ones1f = p.inp("c_ones1fk", (1, 128))
    xctx = p.out("xctx", (D, T), BF16)

    dma = p.sem("dma")
    if kex is not None:
        dkv = [p.sem("d_in0"), p.sem("d_in1")]
        dge = nc.sync
    else:
        dkv = [p.sem("dkv0"), p.sem("dkv1")]
        dge = nc.gpsimd
    dkvb = [dkv[0].v, dkv[1].v]
    q_sb = p.sb("q_sb", [128, NCH * T], BF16)
    kh_sb = p.sb("kh_sb", [128, 2 * TK], BF16)
    vh_sb = p.sb("vh_sb", [128, 2 * (TK // 128) * 128], BF16)
    m_sb = p.sb("m_sb", [128, (TK // 128) * T], BF16)
    onesk = p.sb("onesk", [128, 1], BF16)
    ones1f = p.sb("ones1fk", [1, 128], F32)
    pT_sb = p.sb("pT_sb", [128, 2 * 512], BF16)
    zrow = p.sb("zrow", [1, 512], F32)
    zbc = p.sb("zbc", [128, 2 * 512], F32)
    ctx_sb = p.sb("ctx_sb", [128, 2 * 512], BF16)

    load_fm(p, nc.sync, q_sb, xq, NCH, T, dma, splits=4)
    dma.inc(nc.sync.dma_start(out=onesk[:, :], in_=c_ones), 16)
    dma.inc(nc.sync.dma_start(out=ones1f[:, :], in_=c_ones1f), 16)
    dma.inc(nc.sync.dma_start(
        out=m_sb[:, :].rearrange("p (c t) -> p c t", t=T),
        in_=vmask.rearrange("(c p) t -> p c t", p=128)), 16)

    if kex is not None:
        cr = p.sem("coll_r")
        dge.wait_ge(cr.h, cr.v)

    def load_head(h, par):
        if kex is not None:
            bq = (h % 16) // 4
            j = 1 if h >= 16 else 0
            dkv[par].inc(dge.dma_start(
                out=kh_sb[:, par * TK:(par + 1) * TK].rearrange(
                    "p (s t) -> p s t", s=2),
                in_=kex[j, 2 * bq:2 * bq + 2].rearrange("s f t -> f s t")), 16)
            for s_ in range(2):
                dkv[par].inc(dge.dma_start(
                    out=vh_sb[:, par * (TK // 128) * 128:
                              (par + 1) * (TK // 128) * 128]
                    .rearrange("p (s c q) -> p s c q", s=2, q=128)[:, s_],
                    in_=vex[j, 2 * bq + s_].rearrange(
                        "pp (c q) -> pp c q", q=128)), 16)
            return
        dkv[par].inc(nc.gpsimd.dma_start(
            out=kh_sb[:, par * TK:(par + 1) * TK], in_=xkh[h]), 16)
        dkv[par].inc(nc.gpsimd.dma_start(
            out=vh_sb[:, par * (TK // 128) * 128:(par + 1) * (TK // 128) * 128]
            .rearrange("p (c q) -> p c q", q=128),
            in_=xvh[h].rearrange("(c p) q -> p c q", p=128)), 16)

    DKV_N = 48 if kex is not None else 32
    load_head(0, 0)
    load_head(1, 1)
    for e in [nc.vector, nc.scalar, nc.tensor, nc.sync]:
        e.wait_ge(dma.h, dma.v)

    t_s = p.sem("t_s")
    a_e = p.sem("a_e")
    v_m = p.sem("v_m")
    t_c = p.sem("t_c")
    t_z = p.sem("t_z")
    v_z = p.sem("v_z")
    t_b = p.sem("t_b")
    v_c = p.sem("v_c")
    s_c = [p.sem("s_c0"), p.sem("s_c1")]
    nqt = T // 512
    nkt = TK // 128

    with (
        p.psum("pS0", [128, 512], F32) as pS0,
        p.psum("pS1", [128, 512], F32) as pS1,
        p.psum("pC0", [128, 512], F32) as pC0,
        p.psum("pC1", [128, 512], F32) as pC1,
        p.psum("pZ0", [1, 512], F32) as pZ0,
        p.psum("pZ1", [1, 512], F32) as pZ1,
        p.psum("pB0", [128, 512], F32) as pB0,
        p.psum("pB1", [128, 512], F32) as pB1,
    ):
        pS = [pS0, pS1]
        pC = [pC0, pC1]
        pZ = [pZ0, pZ1]
        pB = [pB0, pB1]
        qi = 0
        for h in range(32):
            hpar = h % 2
            nc.tensor.wait_ge(dkv[hpar].h,
                              dkvb[hpar] + DKV_N * (h // 2 + 1))
            kT_h = kh_sb[:, hpar * TK:(hpar + 1) * TK]
            vbase = hpar * nkt * 128
            for qt in range(nqt):
                par = qi % 2
                qT_t = q_sb[:, h * T + qt * 512:h * T + (qt + 1) * 512]
                for kt in range(nkt):
                    kpar = kt % 2
                    if qi > 0 or kt >= 2:
                        nc.tensor.wait_ge(a_e.h, a_e.v - 1)
                    t_s.inc(nc.tensor.matmul(pS[kpar][:, :],
                                             kT_h[:, kt * 128:(kt + 1) * 128],
                                             qT_t, start=True, stop=True,
                                             skip_group_check=True))
                    nc.scalar.wait_ge(t_s.h, t_s.v)
                    if qi > 0 or kt >= 2:
                        nc.scalar.wait_ge(v_m.h, v_m.v - 1)
                        nc.scalar.wait_ge(t_z.h, t_z.v - 1)
                    a_e.inc(nc.scalar.activation(
                        pT_sb[:, kpar * 512:(kpar + 1) * 512],
                        pS[kpar][:, :], AF.Exp))
                    nc.vector.wait_ge(a_e.h, a_e.v)
                    mk = m_sb[:, kt * T + qt * 512:kt * T + (qt + 1) * 512]
                    v_m.inc(nc.vector.tensor_tensor(
                        pT_sb[:, kpar * 512:(kpar + 1) * 512],
                        pT_sb[:, kpar * 512:(kpar + 1) * 512], mk, ALU.mult))
                    nc.tensor.wait_ge(v_m.h, v_m.v)
                    if kt == 0 and qi >= 2:
                        nc.tensor.wait_ge(v_c.h, v_c.v - 1)
                        nc.tensor.wait_ge(v_z.h, v_z.v - 1)
                    t_c.inc(nc.tensor.matmul(
                        pC[par][:, :], vh_sb[:, vbase + kt * 128:vbase + (kt + 1) * 128],
                        pT_sb[:, kpar * 512:(kpar + 1) * 512],
                        start=(kt == 0), stop=(kt == nkt - 1),
                        skip_group_check=True))
                    t_z.inc(nc.tensor.matmul(
                        pZ[par][:, :], onesk[:, :],
                        pT_sb[:, kpar * 512:(kpar + 1) * 512],
                        start=(kt == 0), stop=(kt == nkt - 1),
                        skip_group_check=True))
                nc.vector.wait_ge(t_z.h, t_z.v)
                if qi >= 2:
                    nc.vector.wait_ge(t_b.h, t_b.v)
                v_z.inc(nc.vector.reciprocal(zrow[:, :], pZ[par][:, :]))
                nc.tensor.wait_ge(v_z.h, v_z.v)
                t_b.inc(nc.tensor.matmul(pB[par][:, :], ones1f[:, :], zrow[:, :],
                                         start=True, stop=True))
                nc.vector.wait_ge(t_b.h, t_b.v)
                p.g(nc.vector, nc.vector.tensor_copy(
                    zbc[:, par * 512:(par + 1) * 512], pB[par][:, :]))
                nc.vector.wait_ge(t_c.h, t_c.v)
                nc.vector.wait_ge(s_c[par].h, s_c[par].v)
                v_c.inc(nc.vector.tensor_tensor(
                    ctx_sb[:, par * 512:(par + 1) * 512], pC[par][:, :],
                    zbc[:, par * 512:(par + 1) * 512], ALU.mult))
                ins = nc.sync.dma_start(
                    out=fm_chunk(xctx, h, T)[:, qt * 512:(qt + 1) * 512],
                    in_=ctx_sb[:, par * 512:(par + 1) * 512])
                ins._wait_ge(v_c.h, v_c.v)
                s_c[par].inc(ins, 16)
                qi += 1
            if h + 2 < 32:
                dge.wait_ge(t_z.h, t_z.v)
                load_head(h + 2, hpar)
        nc.sync.wait_ge(s_c[0].h, s_c[0].v)
        nc.sync.wait_ge(s_c[1].h, s_c[1].v)
    return p.done()


def build_dense_out(T=TPC, p=None):
    p = p or LP("dense_out")
    nc = p.nc
    xfm = p.inp("xfm", (D, T))
    xin = p.inp("xin", (D, T), BF16)
    w = p.inp("w", (NCH, 128, NCH, 128), BF16)
    bias = p.inp("bias", (D, 1))
    xout = p.out("xout", (D, T))

    dma = p.sem("dma")
    dw = [p.sem("dw0"), p.sem("dw1")]
    dwb0 = [dw[0].v, dw[1].v]
    xin_sb = p.sb("xin_sb", [128, NCH * T], BF16)
    w_sb = p.sb("w_sb", [128, 2 * NCH * 128], BF16)
    b_sb = p.sb("b_sb", [128, NCH], F32)

    load_fm(p, nc.sync, xin_sb, xin, NCH, T, dma, splits=4)
    with nc.allow_non_contiguous_dma(reason="4KB bias, one-time"):
        dma.inc(nc.sync.dma_start(
            out=b_sb[:, :].rearrange("p (c o) -> p c o", o=1),
            in_=bias.rearrange("(c p) o -> p c o", p=128)), 16)
    for e in [nc.vector, nc.tensor, nc.gpsimd, nc.sync]:
        e.wait_ge(dma.h, dma.v)

    def load_panel(oc, par):
        dst = w_sb[:, par * NCH * 128:(par + 1) * NCH * 128].rearrange(
            "p (kc m) -> p kc m", m=128)
        dw[par].inc(nc.gpsimd.dma_start(out=dst, in_=w[oc]), 16)

    load_panel(0, 0)
    load_panel(1, 1)
    ct = col_tiles(T)
    t_d = p.sem("t_d")
    with (
        p.psum("pD0", [128, T], F32) as pD0,
        p.psum("pD1", [128, T], F32) as pD1,
    ):
        pD = [pD0, pD1]
        v_res = p.sem("res_vx")
        vrb = v_res.v
        dwb = dwb0

        def delta_cb(oc):
            par = oc % 2
            if oc >= 2:
                nc.tensor.wait_ge(v_res.h, vrb + oc - 1)
            nc.tensor.wait_ge(dw[par].h, dwb[par] + 16 * (oc // 2 + 1))
            ins = None
            for kc in range(NCH):
                wv = w_sb[:, par * NCH * 128 + kc * 128:par * NCH * 128 + (kc + 1) * 128]
                for (t0, t1) in ct:
                    ins = nc.tensor.matmul(pD[par][:, t0:t1], wv,
                                           xin_sb[:, kc * T + t0:kc * T + t1],
                                           start=(kc == 0), stop=(kc == NCH - 1),
                                           skip_group_check=True)
            t_d.inc(ins)
            if oc + 2 < NCH:
                nc.gpsimd.wait_ge(t_d.h, t_d.v)
                load_panel(oc + 2, par)
            return pD[par][:, :], t_d

        d_rp = [p.sem("res_dx0"), p.sem("res_dx1")]
        v_x = v_res
        s_outp = [p.sem("res_st0"), p.sem("res_st1")]
        with p.sbuf("res_xst", [128, 2 * T], F32) as xst:
            for oc in range(NCH):
                par = oc % 2
                if oc >= 2:
                    nc.sync.wait_ge(s_outp[par].h, s_outp[par].v)
                    nc.sync.wait_ge(d_rp[par].h, d_rp[par].v)
                d_rp[par].inc(nc.sync.dma_start(
                    out=xst[:, par * T:(par + 1) * T],
                    in_=fm_chunk(xfm, oc, T)), 16)
                psum_ap, t_sem = delta_cb(oc)
                nc.vector.wait_ge(t_sem.h, t_sem.v)
                nc.vector.wait_ge(d_rp[par].h, d_rp[par].v)
                p.g(nc.vector, nc.vector.tensor_scalar(
                    xst[:, par * T:(par + 1) * T], xst[:, par * T:(par + 1) * T],
                    b_sb[:, oc:oc + 1], None, ALU.add))
                v_x.inc(nc.vector.tensor_tensor(
                    xst[:, par * T:(par + 1) * T], xst[:, par * T:(par + 1) * T],
                    psum_ap, ALU.add))
                ins = nc.sync.dma_start(out=fm_chunk(xout, oc, T),
                                        in_=xst[:, par * T:(par + 1) * T])
                ins._wait_ge(v_x.h, v_x.v)
                s_outp[par].inc(ins, 16)
            nc.sync.wait_ge(s_outp[0].h, s_outp[0].v)
            nc.sync.wait_ge(s_outp[1].h, s_outp[1].v)
    return p.done()


def gqa_inmaps_a(inputs, xs):
    hc = host_const_arrays()
    base = {
        "wq": _bf(blockify(np.asarray(inputs["L1_attn_a_Wq"]))),
        "wkv": _bf(blockify(np.asarray(inputs["L1_attn_a_Wkv"]))),
        "c_onesb": hc["onesb"], "c_ones1f": hc["ones1f"], "c_identb": hc["identb"],
    }
    return [dict(base, xfm=xs[c]) for c in range(NCORES)]


def gqa_inmaps_b(ra, T=TPC):
    hc = host_const_arrays()
    TK = 2 * T
    xk_full = {}
    xv_full = {}
    for b in range(4):
        xk_full[b] = np.concatenate([ra[2 * b]["xk"], ra[2 * b + 1]["xk"]], axis=1)
        xv_full[b] = np.concatenate([ra[2 * b]["xv"], ra[2 * b + 1]["xv"]], axis=0)
    maps = []
    for c in range(NCORES):
        beta, half = c // 2, c % 2
        xkh = np.empty((32, 128, TK), BF)
        xvh = np.empty((32, TK, 128), BF)
        for eta in range(32):
            bq = (eta % 16) // 4
            gq = 2 * beta + (1 if eta >= 16 else 0)
            xkh[eta] = xk_full[bq][gq * 128:(gq + 1) * 128, :]
            xvh[eta] = xv_full[bq][:, gq * 128:(gq + 1) * 128]
        qoff = half * T
        kpos = np.arange(TK)[:, None]
        qpos = qoff + np.arange(T)[None, :]
        vm = (kpos <= qpos).astype(np.float32)
        maps.append({
            "xq": ra[c]["xq"], "xkh": xkh, "xvh": xvh, "vmask": _bf(vm),
            "c_onesbk": hc["onesb"], "c_ones1fk": hc["ones1f"],
        })
    return maps


def gqa_inmaps_c(inputs, xs, rb):
    base = {
        "w": _bf(blockify(np.asarray(inputs["L1_attn_a_Wo"]))),
        "bias": _f32(np.asarray(inputs["L1_attn_a_bo"]).reshape(D, 1)),
    }
    return [dict(base, xfm=xs[c], xin=rb[c]["xctx"]) for c in range(NCORES)]



def build_mamba_a(T=TPC, p=None):
    T2 = T + 4
    p = p or LP("mamba_a")
    nc = p.nc
    xfm = p.inp("xfm", (D, T2))
    wproj = p.inp("wproj", (67, 128, NCH, 128), BF16)
    mupt = p.inp("mupt", (128, 67))
    lnw = p.inp("lnw", (128, NCH))
    wconv = p.inp("wconv", (128, 34 * 4))
    dtb = p.inp("dtb", (128, 1))
    alog = p.inp("alog", (128, 1))
    zs = p.out("zs", (D, T), BF16)
    xbcs = p.out("xbcs", (34 * 128, T), BF16)
    dts = p.out("dts", (64, T), BF16)
    dtas = p.out("dtas", (64, T), BF16)

    cst, dma_c = load_consts(p, [
        ("onesb", (128, 1), BF16), ("ones1f", (1, 128), F32),
    ])
    dmw = p.sem("dmw")
    dmr = p.sem("dmr")
    xn_sb = p.sb("xn_sb", [128, NCH * T2], BF16)
    w_sb = p.sb("w_sb", [128, 2 * NCH * 128], BF16)
    mup_sb = p.sb("mup_sb", [128, 67], F32)
    lnw_sb = p.sb("lnw_sb", [128, NCH], F32)
    wc_sb = p.sb("wc_sb", [128, 34 * 4], F32)
    dtb_sb = p.sb("dtb_sb", [128, 1], F32)
    al_sb = p.sb("al_sb", [128, 4], F32)
    stg = p.sb("stg", [128, 2 * T2], F32)
    stb = p.sb("stb", [128, 2 * T], BF16)
    cacc = p.sb("cacc", [128, 2 * T], F32)

    dmr.inc(nc.sync.dma_start(out=mup_sb[:, :], in_=mupt), 16)
    dmr.inc(nc.sync.dma_start(out=lnw_sb[:, :], in_=lnw), 16)
    dmr.inc(nc.sync.dma_start(out=wc_sb[:, :], in_=wconv), 16)
    dmr.inc(nc.sync.dma_start(out=dtb_sb[:, :], in_=dtb), 16)
    dmr.inc(nc.sync.dma_start(out=al_sb[:, 0:1], in_=alog), 16)
    for e in [nc.vector, nc.scalar, nc.tensor, nc.gpsimd, nc.sync]:
        e.wait_ge(dma_c.h, dma_c.v)
        e.wait_ge(dmr.h, dmr.v)

    a_pre = p.sem("a_pre")
    a_pre.inc(nc.scalar.activation(al_sb[:, 1:2], al_sb[:, 0:1], AF.Exp))
    v_pre = p.sem("v_pre")
    nc.vector.wait_ge(a_pre.h, a_pre.v)
    v_pre.inc(nc.vector.tensor_scalar(al_sb[:, 1:2], al_sb[:, 1:2], -1.0, None,
                                      ALU.mult))

    v_ap = emit_ln(p, xfm, xn_sb, T2, cst, mode="mamba", lnw_col=lnw_sb)
    nc.tensor.wait_ge(v_ap.h, v_ap.v)

    ct = col_tiles(T2)
    dw = [p.sem("dw0"), p.sem("dw1")]
    t_p = p.sem("t_p")
    v_e = p.sem("v_e")
    dwb = [dw[0].v, dw[1].v]
    veb = v_e.v
    v_cv = p.sem("v_cv")
    a_si = p.sem("a_si")
    s_o = [p.sem("s_o0"), p.sem("s_o1")]

    def load_panel(pi, par):
        dst = w_sb[:, par * NCH * 128:(par + 1) * NCH * 128].rearrange(
            "p (kc m) -> p kc m", m=128)
        dw[par].inc(nc.gpsimd.dma_start(out=dst, in_=wproj[pi]), 16)

    load_panel(0, 0)
    load_panel(1, 1)
    with (
        p.psum("pP0", [128, T2], F32) as pP0,
        p.psum("pP1", [128, T2], F32) as pP1,
    ):
        pP = [pP0, pP1]
        for pi in range(67):
            par = pi % 2
            nc.tensor.wait_ge(dw[par].h, dwb[par] + 16 * (pi // 2 + 1))
            if pi >= 2:
                nc.tensor.wait_ge(v_e.h, veb + pi - 1)
            ins = None
            for kc in range(NCH):
                w = w_sb[:, par * NCH * 128 + kc * 128:par * NCH * 128 + (kc + 1) * 128]
                for (t0, t1) in ct:
                    ins = nc.tensor.matmul(pP[par][:, t0:t1], w,
                                           xn_sb[:, kc * T2 + t0:kc * T2 + t1],
                                           start=(kc == 0), stop=(kc == NCH - 1),
                                           skip_group_check=True)
            t_p.inc(ins)
            if pi + 2 < 67:
                nc.gpsimd.wait_ge(t_p.h, t_p.v)
                load_panel(pi + 2, par)
            nc.vector.wait_ge(t_p.h, t_p.v)
            if pi >= 2:
                nc.vector.wait_ge(s_o[par].h, s_o[par].v)
            if pi < 32:
                v_e.inc(nc.vector.tensor_scalar(
                    stb[:, par * T:(par + 1) * T], pP[par][:, 4:T2],
                    mup_sb[:, pi:pi + 1], None, ALU.mult))
                ins = nc.sync.dma_start(out=fm_chunk(zs, pi, T),
                                        in_=stb[:, par * T:(par + 1) * T])
                ins._wait_ge(v_e.h, v_e.v)
                s_o[par].inc(ins, 16)
            elif pi < 66:
                cp = pi - 32
                p.g(nc.vector, nc.vector.tensor_scalar(
                    stg[:, par * T2:(par + 1) * T2], pP[par][:, :],
                    mup_sb[:, pi:pi + 1], None, ALU.mult))
                sg = stg[:, par * T2:(par + 1) * T2]
                ca = cacc[:, par * T:(par + 1) * T]
                p.g(nc.vector, nc.vector.tensor_scalar(
                    ca, sg[:, 1:1 + T], wc_sb[:, 4 * cp:4 * cp + 1], None, ALU.mult))
                for k in range(1, 4):
                    p.g(nc.vector, nc.vector.scalar_tensor_tensor(
                        ca, sg[:, 1 + k:1 + k + T], wc_sb[:, 4 * cp + k:4 * cp + k + 1],
                        ca, ALU.mult, ALU.add))
                v_cv.inc(nc.vector.engine_nop())
                nc.scalar.wait_ge(v_cv.h, v_cv.v)
                a_si.inc(nc.scalar.activation(stb[:, par * T:(par + 1) * T], ca,
                                              AF.Silu))
                ins = nc.sync.dma_start(out=fm_chunk(xbcs, cp, T),
                                        in_=stb[:, par * T:(par + 1) * T])
                ins._wait_ge(a_si.h, a_si.v)
                s_o[par].inc(ins, 16)
                v_e.inc(nc.vector.engine_nop())
            else:
                sg = stg[:, par * T2:par * T2 + T]
                p.g(nc.vector, nc.vector.tensor_scalar(
                    sg, pP[par][:, 4:T2], mup_sb[:, pi:pi + 1], None, ALU.mult))
                p.g(nc.vector, nc.vector.tensor_scalar(
                    sg, sg, dtb_sb[:, 0:1], None, ALU.add))
                ab = cacc[:, 0:T]
                lnb = cacc[:, T:2 * T]
                va = p.sem("v_sp")
                va.inc(nc.vector.engine_nop())
                nc.scalar.wait_ge(va.h, va.v)
                asp = p.sem("a_sp")
                p.g(nc.scalar, nc.scalar.activation(ab, sg, AF.Abs))
                asp.inc(nc.scalar.activation(ab, ab, AF.Exp, scale=-1.0))
                nc.vector.wait_ge(asp.h, asp.v)
                va.inc(nc.vector.tensor_scalar(ab, ab, 1.0, None, ALU.add))
                nc.scalar.wait_ge(va.h, va.v)
                asp.inc(nc.scalar.activation(lnb, ab, AF.Ln))
                nc.vector.wait_ge(asp.h, asp.v)
                p.g(nc.vector, nc.vector.tensor_scalar(sg, sg, 0.0, None, ALU.max))
                p.g(nc.vector, nc.vector.tensor_tensor(sg, sg, lnb, ALU.add))
                nc.vector.wait_ge(s_o[1 - par].h, s_o[1 - par].v)
                nc.vector.wait_ge(v_pre.h, v_pre.v)
                p.g(nc.vector, nc.vector.tensor_copy(stb[:, 0:T], sg))
                v_dt = p.sem("v_dt")
                v_dt.inc(nc.vector.tensor_scalar(stb[:, T:2 * T], sg,
                                                 al_sb[:, 1:2], None, ALU.mult))
                ins = nc.sync.dma_start(out=dts, in_=stb[0:64, 0:T])
                ins._wait_ge(v_dt.h, v_dt.v)
                s_o[par].inc(ins, 16)
                ins = nc.sync.dma_start(out=dtas, in_=stb[0:64, T:2 * T])
                ins._wait_ge(v_dt.h, v_dt.v)
                s_o[par].inc(ins, 16)
                v_e.inc(nc.vector.engine_nop())
        nc.sync.wait_ge(s_o[0].h, s_o[0].v)
        nc.sync.wait_ge(s_o[1].h, s_o[1].v)
    return p.done()


def mamba_inmaps_a(inputs, xs, x_raw, T=TPC):
    hc = host_const_arrays()
    ip = np.asarray(inputs["L0_attn_a_in_proj_w"])
    ippad = np.zeros((4096, 67 * 128), np.float32)
    ippad[:, :8512] = ip
    mup = np.asarray(inputs["L0_attn_a_mup_vector"])
    muppad = np.zeros(67 * 128, np.float32)
    muppad[:8512] = mup
    cw = np.asarray(inputs["L0_attn_a_conv_w"])
    dtb = np.zeros((128, 1), np.float32)
    dtb[:64, 0] = np.asarray(inputs["L0_attn_a_dt_bias"])
    alog = np.zeros((128, 1), np.float32)
    alog[:64, 0] = np.asarray(inputs["L0_attn_a_A_log"])
    base = {
        "wproj": _bf(blockify(ippad)),
        "mupt": _f32(muppad.reshape(67, 128).T),
        "lnw": _f32(np.asarray(inputs["L0_attn_a_ln_w"]).reshape(NCH, 128).T),
        "wconv": _f32(cw.reshape(34, 128, 4).transpose(1, 0, 2).reshape(128, 34 * 4)),
        "dtb": dtb, "alog": alog,
        "c_onesb": hc["onesb"], "c_ones1f": hc["ones1f"],
    }
    maps = []
    for c in range(NCORES):
        b, h = c // 2, c % 2
        lead = np.zeros((4096, 4), np.float32)
        if h == 1:
            lead = _f32(x_raw[b, T - 4:T, :].T)
        xf2 = np.concatenate([lead, xs[c]], axis=1)
        maps.append(dict(base, xfm=_f32(xf2)))
    return maps


def emit_dense_residual(p, xin_sb, w_ap, xfm, xout, T, bias_sb=None, ready_sem=None):
    nc = p.nc
    dw = [p.sem("do_dw0"), p.sem("do_dw1")]
    dwb0 = [dw[0].v, dw[1].v]
    w_sb = p.sb("do_w_sb", [128, 2 * NCH * 128], BF16)
    t_d = p.sem("do_t_d")

    def load_panel(oc, par):
        dst = w_sb[:, par * NCH * 128:(par + 1) * NCH * 128].rearrange(
            "p (kc m) -> p kc m", m=128)
        dw[par].inc(nc.gpsimd.dma_start(out=dst, in_=w_ap[oc]), 16)

    if ready_sem is not None:
        nc.gpsimd.wait_ge(ready_sem.h, ready_sem.v)
    load_panel(0, 0)
    load_panel(1, 1)
    ct = col_tiles(T)
    with (
        p.psum("do_pD0", [128, T], F32) as pD0,
        p.psum("do_pD1", [128, T], F32) as pD1,
    ):
        pD = [pD0, pD1]
        v_x = p.sem("do_res_vx")
        vxb = v_x.v
        dwb = dwb0
        d_rp = [p.sem("do_res_dx0"), p.sem("do_res_dx1")]
        s_outp = [p.sem("do_res_st0"), p.sem("do_res_st1")]
        with p.sbuf("do_res_xst", [128, 2 * T], F32) as xst:
            for oc in range(NCH):
                par = oc % 2
                if oc >= 2:
                    nc.sync.wait_ge(s_outp[par].h, s_outp[par].v)
                    nc.sync.wait_ge(d_rp[par].h, d_rp[par].v)
                d_rp[par].inc(nc.sync.dma_start(
                    out=xst[:, par * T:(par + 1) * T],
                    in_=fm_chunk(xfm, oc, T)), 16)
                if oc >= 2:
                    nc.tensor.wait_ge(v_x.h, vxb + oc - 1)
                nc.tensor.wait_ge(dw[par].h, dwb[par] + 16 * (oc // 2 + 1))
                if ready_sem is not None and oc == 0:
                    nc.tensor.wait_ge(ready_sem.h, ready_sem.v)
                ins = None
                for kc in range(NCH):
                    wv = w_sb[:, par * NCH * 128 + kc * 128:
                              par * NCH * 128 + (kc + 1) * 128]
                    for (t0, t1) in ct:
                        ins = nc.tensor.matmul(pD[par][:, t0:t1], wv,
                                               xin_sb[:, kc * T + t0:kc * T + t1],
                                               start=(kc == 0), stop=(kc == NCH - 1),
                                               skip_group_check=True)
                t_d.inc(ins)
                if oc + 2 < NCH:
                    nc.gpsimd.wait_ge(t_d.h, t_d.v)
                    load_panel(oc + 2, par)
                nc.vector.wait_ge(t_d.h, t_d.v)
                nc.vector.wait_ge(d_rp[par].h, d_rp[par].v)
                if bias_sb is not None:
                    p.g(nc.vector, nc.vector.tensor_scalar(
                        xst[:, par * T:(par + 1) * T], xst[:, par * T:(par + 1) * T],
                        bias_sb[:, oc:oc + 1], None, ALU.add))
                v_x.inc(nc.vector.tensor_tensor(
                    xst[:, par * T:(par + 1) * T], xst[:, par * T:(par + 1) * T],
                    pD[par][:, :], ALU.add))
                ins = nc.sync.dma_start(out=fm_chunk(xout, oc, T),
                                        in_=xst[:, par * T:(par + 1) * T])
                ins._wait_ge(v_x.h, v_x.v)
                s_outp[par].inc(ins, 16)
            nc.sync.wait_ge(s_outp[0].h, s_outp[0].v)
            nc.sync.wait_ge(s_outp[1].h, s_outp[1].v)


def build_mamba_b(T=TPC, p=None):
    p = p or LP("mamba_b")
    nc = p.nc
    NCk = 8
    xbcs = p.inp("xbcs", (34 * 128, T), BF16)
    dts = p.inp("dts", (64, T), BF16)
    dtas = p.inp("dtas", (64, T), BF16)
    drow = p.inp("drow", (1, 64))
    ys = p.out("ys", (T, D), BF16)
    hout = p.out("hout", (128, D))
    elg = p.out("elg", (T, 64))

    cst, dma_c = load_consts(p, [
        ("onesb", (128, 1), BF16), ("ones1f", (1, 128), F32),
        ("identf", (128, 128), F32), ("ltri", (128, 128), BF16),
    ])
    drow_sb = p.sb("drow_sb", [1, 64], F32)
    dma_c.inc(nc.sync.dma_start(out=drow_sb[:, :], in_=drow), 16)

    xs_sb = p.sb("xs_sb", [128, 2 * D], BF16)
    xh_sb = p.sb("xh_sb", [128, D], BF16)
    xd_sb = p.sb("xd_sb", [128, D], BF16)
    xc_sb = p.sb("xc_sb", [128, D], BF16)
    bt_sb = p.sb("bt_sb", [128, 2 * 128], BF16)
    bf_sb = p.sb("bf_sb", [128, 2 * 128], BF16)
    cf_sb = p.sb("cf_sb", [128, 2 * 128], BF16)
    dtt_sb = p.sb("dtt_sb", [128, 2 * 64], BF16)
    dta_sb = p.sb("dta_sb", [128, 2 * 64], BF16)
    L_sb = p.sb("L_sb", [128, 64], F32)
    eL_sb = p.sb("eL_sb", [128, 64], F32)
    eLb_sb = p.sb("eLb_sb", [128, 64], F32)
    w2_sb = p.sb("w2_sb", [128, 64], BF16)
    arg_sb = p.sb("arg_sb", [128, 64], F32)
    rows_sb = p.sb("rows_sb", [1, 256], F32)
    cbm_sb = p.sb("cbm_sb", [128, 128], BF16)
    mc_sb = p.sb("mc_sb", [128, 2 * 128], F32)
    mm_sb = p.sb("mm_sb", [128, 2 * 128], BF16)
    dbc_sb = p.sb("dbc_sb", [128, 64], F32)
    elg_st = p.sb("elg_st", [128, 64], F32)
    h_sb = p.sb("h_sb", [128, D], F32)
    hb_sb = p.sb("hb_sb", [128, D], BF16)
    hs_sb = p.sb("hs_sb", [128, 512], F32)
    y_st = p.sb("y_st", [128, 2 * D], BF16)
    t1_sb = p.sb("t1_sb", [128, 512], F32)

    for e in [nc.vector, nc.scalar, nc.tensor, nc.gpsimd, nc.sync]:
        e.wait_ge(dma_c.h, dma_c.v)

    d_in = [p.sem("d_in0"), p.sem("d_in1")]
    t_L = p.sem("t_L")
    v_L = p.sem("v_L")
    a_L = p.sem("a_L")
    t_bc2 = p.sem("t_bc2")
    v_x2 = p.sem("v_x2")
    t_cb = p.sem("t_cb")
    v_cb = p.sem("v_cb")
    t_lb = p.sem("t_lb")
    v_mc = p.sem("v_mc")
    a_m = p.sem("a_m")
    v_mm = p.sem("v_mm")
    t_y = p.sem("t_y")
    v_y = p.sem("v_y")
    s_yp = [p.sem("s_y0"), p.sem("s_y1")]
    t_u = p.sem("t_u")
    v_h = p.sem("v_h")
    v_cr = p.sem("v_cr")
    a_eg = p.sem("a_eg")
    s_eg = p.sem("s_eg")
    vz = p.sem("vz")

    vz.inc(nc.vector.memset(h_sb[:, :], 0.0))
    vz.inc(nc.vector.memset(hb_sb[:, :], 0.0))
    vz.inc(nc.vector.memset(rows_sb[:, :], 0.0))
    with p.psum("pI", [128, 64], F32) as pI:
        tI = p.sem("tI")
        tI.inc(nc.tensor.matmul(pI[:, :], cst["ones1f"][:, :], drow_sb[:, :],
                                start=True, stop=True))
        nc.vector.wait_ge(tI.h, tI.v)
        vz.inc(nc.vector.tensor_copy(dbc_sb[:, :], pI[:, :]))
    for e in [nc.vector, nc.scalar, nc.tensor, nc.sync]:
        e.wait_ge(vz.h, vz.v)

    dvw = {"d": 64}
    with (
        p.psum("pMISC", [128, 512], F32) as pM,
        p.psum("pCB", [128, 128], F32) as pCB,
        p.psum("pLB0", [128, 128], F32) as pLB0,
        p.psum("pLB1", [128, 128], F32) as pLB1,
        p.psum("pYA0", [128, 512], F32) as pYA0,
        p.psum("pYA1", [128, 512], F32) as pYA1,
        p.psum("pYB0", [128, 512], F32) as pYB0,
        p.psum("pYB1", [128, 512], F32) as pYB1,
    ):
        pLB = [pLB0, pLB1]
        pYA = [pYA0, pYA1]
        pYB = [pYB0, pYB1]
        for sc in range(NCk):
            par = sc % 2
            cols = slice(sc * 128, (sc + 1) * 128)
            if sc >= 2:
                nc.sync.wait_ge(t_u.h, t_u.v)
                for dd in d_in:
                    nc.sync.wait_ge(dd.h, dd.v)
            d_in[par].inc(nc.sync.dma_start(
                out=xs_sb[:, par * D:(par + 1) * D],
                in_=xbcs[0:4096, cols], transpose=True), 16)
            d_in[par].inc(nc.sync.dma_start(
                out=bt_sb[:, par * 128:(par + 1) * 128],
                in_=xbcs[4096:4224, cols], transpose=True), 16)
            d_in[par].inc(nc.sync.dma_start(
                out=dtt_sb[:, par * 64:(par + 1) * 64],
                in_=dts[:, cols], transpose=True), 16)
            d_in[par].inc(nc.sync.dma_start(
                out=dta_sb[:, par * 64:(par + 1) * 64],
                in_=dtas[:, cols], transpose=True), 16)
            d_in[par].inc(nc.sync.dma_start(
                out=bf_sb[:, par * 128:(par + 1) * 128],
                in_=fm_chunk(xbcs, 32, T)[:, cols]), 16)
            d_in[par].inc(nc.sync.dma_start(
                out=cf_sb[:, par * 128:(par + 1) * 128],
                in_=fm_chunk(xbcs, 33, T)[:, cols]), 16)

            nc.tensor.wait_ge(d_in[par].h, d_in[par].v)
            if sc >= 1:
                nc.tensor.wait_ge(a_eg.h, a_eg.v)
            nc.tensor.matmul(pM[:, 0:64], cst["ltri"][:, :],
                             dta_sb[:, par * 64:(par + 1) * 64],
                             start=True, stop=True, skip_group_check=True)
            t_L.inc(nc.tensor.matmul(pM[0:1, 64:128], cst["onesb"][:, :],
                                     dta_sb[:, par * 64:(par + 1) * 64],
                                     start=True, stop=True, skip_group_check=True))
            nc.vector.wait_ge(t_L.h, t_L.v)
            p.g(nc.vector, nc.vector.tensor_copy(L_sb[:, :], pM[:, 0:64]))
            p.g(nc.vector, nc.vector.tensor_copy(rows_sb[:, 64:128], pM[0:1, 64:128]))
            v_L.inc(nc.vector.engine_nop())
            nc.scalar.wait_ge(v_L.h, v_L.v)
            a_L.inc(nc.scalar.activation(eL_sb[:, :], L_sb[:, :], AF.Exp))
            nc.tensor.wait_ge(v_L.h, v_L.v)
            nc.tensor.matmul(pM[:, 128:192], cst["ones1f"][:, :], rows_sb[:, 0:64],
                             start=True, stop=True, skip_group_check=True)
            t_bc2.inc(nc.tensor.matmul(pM[:, 192:256], cst["ones1f"][:, :],
                                       rows_sb[:, 64:128], start=True, stop=True,
                                       skip_group_check=True))
            nc.vector.wait_ge(t_bc2.h, t_bc2.v)
            p.g(nc.vector, nc.vector.tensor_tensor(
                rows_sb[:, 0:64], rows_sb[:, 0:64], rows_sb[:, 64:128], ALU.add))
            if sc >= 1:
                nc.vector.wait_ge(s_eg.h, s_eg.v)
            p.g(nc.vector, nc.vector.tensor_tensor(
                elg_st[:, :], L_sb[:, :], pM[:, 128:192], ALU.add))
            v_cr.inc(nc.vector.tensor_tensor(
                arg_sb[:, :], pM[:, 192:256], L_sb[:, :], ALU.subtract))
            nc.scalar.wait_ge(v_cr.h, v_cr.v)
            a_eg.inc(nc.scalar.activation(elg_st[:, :], elg_st[:, :], AF.Exp))
            nc.scalar.wait_ge(a_eg.h, a_eg.v)
            p.g(nc.scalar, nc.scalar.activation(w2_sb[:, :], arg_sb[:, :], AF.Exp))
            a_eg.inc(nc.scalar.activation(eLb_sb[:, :], pM[:, 192:256], AF.Exp))
            ins = nc.sync.dma_start(out=elg.rearrange("(c p) r -> c p r", p=128)[sc],
                                    in_=elg_st[:, :])
            ins._wait_ge(a_eg.h, a_eg.v - 1)
            s_eg.inc(ins, 16)

            dt3 = dtt_sb[:, par * 64:(par + 1) * 64].rearrange(
                "p (r o) -> p r o", o=1).broadcast_to([128, 64, 64])
            xs3 = xs_sb[:, par * D:(par + 1) * D].rearrange("p (r d) -> p r d", **dvw)
            xh3 = xh_sb[:, :].rearrange("p (r d) -> p r d", **dvw)
            xd3 = xd_sb[:, :].rearrange("p (r d) -> p r d", **dvw)
            xc3 = xc_sb[:, :].rearrange("p (r d) -> p r d", **dvw)
            db3 = dbc_sb[:, :].rearrange("p (r o) -> p r o", o=1).broadcast_to(
                [128, 64, 64])
            p.g(nc.vector, nc.vector.tensor_tensor(xh3, xs3, dt3, ALU.mult))
            p.g(nc.vector, nc.vector.tensor_tensor(xd3, xs3, db3, ALU.mult))
            nc.vector.wait_ge(a_eg.h, a_eg.v)
            w23 = w2_sb[:, :].rearrange("p (r o) -> p r o", o=1).broadcast_to(
                [128, 64, 64])
            v_x2.inc(nc.vector.tensor_tensor(xc3, xh3, w23, ALU.mult))

            if sc >= 1:
                nc.tensor.wait_ge(v_cb.h, v_cb.v)
            t_cb.inc(nc.tensor.matmul(pCB[:, :], bf_sb[:, par * 128:(par + 1) * 128],
                                      cf_sb[:, par * 128:(par + 1) * 128],
                                      start=True, stop=True, skip_group_check=True))
            nc.vector.wait_ge(t_cb.h, t_cb.v)
            v_cb.inc(nc.vector.tensor_tensor(cbm_sb[:, :], pCB[:, :],
                                             cst["ltri"][:, :], ALU.mult))

            for sl in range(8):
                spar = sl % 2
                gsl = sc * 8 + sl
                if sl == 0 and sc >= 2:
                    nc.vector.wait_ge(s_yp[par].h, s_yp[par].v)
                for rr in range(8):
                    r = sl * 8 + rr
                    rpar = r % 2
                    if r >= 2 or sc >= 1:
                        nc.tensor.wait_ge(v_mc.h, v_mc.v - 1)
                    t_lb.inc(nc.tensor.transpose(
                        pLB[rpar][:, :], L_sb[:, r:r + 1].broadcast_to([128, 128]),
                        cst["identf"][:, :]))
                    nc.vector.wait_ge(t_lb.h, t_lb.v)
                    if r >= 2 or sc >= 1:
                        nc.vector.wait_ge(a_m.h, a_m.v - 1)
                    v_mc.inc(nc.vector.tensor_scalar(
                        mc_sb[:, rpar * 128:(rpar + 1) * 128], pLB[rpar][:, :],
                        L_sb[:, r:r + 1], 0.0, ALU.subtract, ALU.min))
                    nc.scalar.wait_ge(v_mc.h, v_mc.v)
                    if r >= 2 or sc >= 1:
                        nc.scalar.wait_ge(v_mm.h, v_mm.v - 1)
                    a_m.inc(nc.scalar.activation(
                        mc_sb[:, rpar * 128:(rpar + 1) * 128],
                        mc_sb[:, rpar * 128:(rpar + 1) * 128], AF.Exp))
                    nc.vector.wait_ge(a_m.h, a_m.v)
                    nc.vector.wait_ge(v_cb.h, v_cb.v)
                    if r >= 2 or sc >= 1:
                        nc.vector.wait_ge(t_y.h, t_y.v - 1)
                    v_mm.inc(nc.vector.tensor_tensor(
                        mm_sb[:, rpar * 128:(rpar + 1) * 128],
                        mc_sb[:, rpar * 128:(rpar + 1) * 128], cbm_sb[:, :],
                        ALU.mult))
                    nc.tensor.wait_ge(v_mm.h, v_mm.v)
                    if rr == 0:
                        nc.tensor.wait_ge(v_x2.h, v_x2.v)
                        if gsl >= 2:
                            nc.tensor.wait_ge(v_y.h, v_y.v - 1)
                            nc.tensor.wait_ge(v_h.h, v_h.v - 1)
                    t_y.inc(nc.tensor.matmul(
                        pYA[spar][:, rr * 64:(rr + 1) * 64],
                        mm_sb[:, rpar * 128:(rpar + 1) * 128],
                        xh_sb[:, r * 64:(r + 1) * 64],
                        start=True, stop=True, skip_group_check=True))
                t_y.inc(nc.tensor.matmul(
                    pYB[spar][:, :], cf_sb[:, par * 128:(par + 1) * 128],
                    hb_sb[:, sl * 512:(sl + 1) * 512],
                    start=True, stop=True, skip_group_check=True))
                nc.vector.wait_ge(t_y.h, t_y.v)
                eview = eL_sb[:, sl * 8:(sl + 1) * 8].rearrange(
                    "p (r o) -> p r o", o=1).broadcast_to([128, 8, 64])
                t13 = t1_sb[:, :].rearrange("p (r d) -> p r d", **dvw)
                pb3 = pYB[spar][:, :].rearrange("p (r d) -> p r d", **dvw)
                p.g(nc.vector, nc.vector.tensor_tensor(t13, pb3, eview, ALU.mult))
                p.g(nc.vector, nc.vector.tensor_tensor(
                    t1_sb[:, :], t1_sb[:, :], xd_sb[:, sl * 512:(sl + 1) * 512],
                    ALU.add))
                v_y.inc(nc.vector.tensor_tensor(
                    y_st[:, par * D + sl * 512:par * D + (sl + 1) * 512],
                    pYA[spar][:, :], t1_sb[:, :], ALU.add))
                nc.tensor.wait_ge(v_y.h, v_y.v)
                t_u.inc(nc.tensor.matmul(
                    pYB[spar][:, :], bt_sb[:, par * 128:(par + 1) * 128],
                    xc_sb[:, sl * 512:(sl + 1) * 512],
                    start=True, stop=True, skip_group_check=True))
                nc.vector.wait_ge(t_u.h, t_u.v)
                ebview = eLb_sb[:, sl * 8:(sl + 1) * 8].rearrange(
                    "p (r o) -> p r o", o=1).broadcast_to([128, 8, 64])
                hs3 = hs_sb[:, :].rearrange("p (r d) -> p r d", **dvw)
                h3 = h_sb[:, sl * 512:(sl + 1) * 512].rearrange(
                    "p (r d) -> p r d", **dvw)
                p.g(nc.vector, nc.vector.tensor_tensor(hs3, h3, ebview, ALU.mult))
                p.g(nc.vector, nc.vector.tensor_tensor(
                    h_sb[:, sl * 512:(sl + 1) * 512], hs_sb[:, :],
                    pYB[spar][:, :], ALU.add))
                v_h.inc(nc.vector.tensor_copy(
                    hb_sb[:, sl * 512:(sl + 1) * 512],
                    h_sb[:, sl * 512:(sl + 1) * 512]))
            ins = nc.sync.dma_start(
                out=ys.rearrange("(c p) d -> c p d", p=128)[sc],
                in_=y_st[:, par * D:(par + 1) * D])
            ins._wait_ge(v_y.h, v_y.v)
            s_yp[par].inc(ins, 16)
        s_h = p.sem("s_h")
        nc.sync.wait_ge(v_h.h, v_h.v)
        s_h.inc(nc.sync.dma_start(out=hout, in_=h_sb[:, :]), 16)
        nc.sync.wait_ge(s_h.h, s_h.v)
        nc.sync.wait_ge(s_yp[0].h, s_yp[0].v)
        nc.sync.wait_ge(s_yp[1].h, s_yp[1].v)
        nc.sync.wait_ge(s_eg.h, s_eg.v)
    return p.done()


def mamba_inmaps_b(inputs, ra, T=TPC):
    hc = host_const_arrays()
    ltri = np.triu(np.ones((128, 128), np.float32))
    drow = np.asarray(inputs["L0_attn_a_D"]).reshape(1, 64)
    base = {
        "c_onesb": hc["onesb"], "c_ones1f": hc["ones1f"],
        "c_identf": hc["identf"], "c_ltri": _bf(ltri),
        "drow": _f32(drow),
    }
    return [dict(base, xbcs=ra[c]["xbcs"], dts=ra[c]["dts"], dtas=ra[c]["dtas"])
            for c in range(NCORES)]


def build_mamba_c(T=TPC, p=None):
    p = p or LP("mamba_c")
    nc = p.nc
    NCk = 8
    xfm = p.inp("xfm", (D, T))
    ys = p.inp("ys", (T, D), BF16)
    elg = p.inp("elg", (T, 64))
    hin = p.inp("hin", (128, D))
    zsp = p.inp("zsp", (D, T), BF16)
    xbcs = p.inp("xbcs", (34 * 128, T), BF16)
    nwrow = p.inp("nwrow", (1, D))
    wout = p.inp("wout", (NCH, 128, NCH, 128), BF16)
    xout = p.out("xout", (D, T))

    cst, dma_c = load_consts(p, [
        ("ones1f", (1, 128), F32), ("identb", (128, 128), BF16),
    ])
    nw_sb = p.sb("nw_sb", [1, D], F32)
    dma_c.inc(nc.sync.dma_start(out=nw_sb[:, :], in_=nwrow), 16)
    hb_sb = p.sb("hb_sb", [128, D], BF16)
    nwb_sb = p.sb("nwb_sb", [128, D], BF16)
    y_sb = p.sb("y_sb", [128, 2 * D], BF16)
    z_sb = p.sb("z_sb", [128, 2 * D], BF16)
    eg_sb = p.sb("eg_sb", [128, 2 * 64], F32)
    ct_sb = p.sb("ct_sb", [128, 2 * 128], BF16)
    ych_sb = p.sb("ych_sb", [128, D], F32)
    yg_sb = p.sb("yg_sb", [128, D], F32)
    sq_sb = p.sb("sq_sb", [128, D], BF16)
    acc_sb = p.sb("acc_sb", [128, 8], F32)
    ygn_sb = p.sb("ygn_sb", [128, D], BF16)
    ygf_sb = p.sb("ygf_sb", [128, NCH * T], BF16)

    d_h = p.sem("d_h")
    d_in = [p.sem("d_in0"), p.sem("d_in1")]
    d_hb = p.sem("d_hb")
    v_hb = p.sem("v_hb")
    t_nw = p.sem("t_nw")
    v_nw = p.sem("v_nw")
    t_cr = p.sem("t_cr")
    v_y1 = p.sem("v_y1")
    a_z = p.sem("a_z")
    v_yg = p.sem("v_yg")
    a_sq = p.sem("a_sq")
    v_rc = p.sem("v_rc")
    a_rs = p.sem("a_rs")
    v_gn = p.sem("v_gn")
    t_tr = p.sem("t_tr")
    v_tr = p.sem("v_tr")

    with p.sbuf("hin_f", [128, D], F32) as hin_f:
        d_hb.inc(nc.sync.dma_start(out=hin_f[:, :], in_=hin), 16)
        for e in [nc.vector, nc.scalar, nc.tensor, nc.gpsimd, nc.sync]:
            e.wait_ge(dma_c.h, dma_c.v)
        nc.vector.wait_ge(d_hb.h, d_hb.v)
        v_hb.inc(nc.vector.tensor_copy(hb_sb[:, :], hin_f[:, :]))

        with p.psum("pNW", [128, 512], F32) as pNW:
            for i in range(8):
                if i >= 1:
                    nc.tensor.wait_ge(v_nw.h, v_nw.v)
                t_nw.inc(nc.tensor.matmul(pNW[:, :], cst["ones1f"][:, :],
                                          nw_sb[:, i * 512:(i + 1) * 512],
                                          start=True, stop=True,
                                          skip_group_check=True))
                nc.vector.wait_ge(t_nw.h, t_nw.v)
                v_nw.inc(nc.vector.tensor_copy(nwb_sb[:, i * 512:(i + 1) * 512],
                                               pNW[:, :]))

    with (
        p.psum("pC0", [128, 512], F32) as pC0,
        p.psum("pC1", [128, 512], F32) as pC1,
        p.psum("pT0", [128, 128], BF16) as pT0,
        p.psum("pT1", [128, 128], BF16) as pT1,
    ):
        pC = [pC0, pC1]
        pT = [pT0, pT1]
        nc.tensor.wait_ge(v_nw.h, v_nw.v)
        for sc in range(NCk):
            par = sc % 2
            cols = slice(sc * 128, (sc + 1) * 128)
            if sc >= 2:
                nc.sync.wait_ge(v_y1.h, v_y1.v)
                for dd in d_in:
                    nc.sync.wait_ge(dd.h, dd.v)
            d_in[par].inc(nc.sync.dma_start(
                out=y_sb[:, par * D:(par + 1) * D],
                in_=ys.rearrange("(c p) d -> c p d", p=128)[sc]), 16)
            d_in[par].inc(nc.sync.dma_start(
                out=z_sb[:, par * D:(par + 1) * D],
                in_=zsp[:, cols], transpose=True), 16)
            d_in[par].inc(nc.sync.dma_start(
                out=eg_sb[:, par * 64:(par + 1) * 64],
                in_=elg.rearrange("(c p) r -> c p r", p=128)[sc]), 16)
            d_in[par].inc(nc.sync.dma_start(
                out=ct_sb[:, par * 128:(par + 1) * 128],
                in_=fm_chunk(xbcs, 33, T)[:, cols]), 16)
            nc.tensor.wait_ge(d_in[par].h, d_in[par].v)
            nc.tensor.wait_ge(v_hb.h, v_hb.v)
            for sl in range(8):
                spar = sl % 2
                gsl = sc * 8 + sl
                if gsl >= 2:
                    nc.tensor.wait_ge(v_y1.h, v_y1.v - 1)
                t_cr.inc(nc.tensor.matmul(
                    pC[spar][:, :], ct_sb[:, par * 128:(par + 1) * 128],
                    hb_sb[:, sl * 512:(sl + 1) * 512],
                    start=True, stop=True, skip_group_check=True))
                nc.vector.wait_ge(t_cr.h, t_cr.v)
                egv = eg_sb[:, par * 64 + sl * 8:par * 64 + (sl + 1) * 8].rearrange(
                    "p (r o) -> p r o", o=1).broadcast_to([128, 8, 64])
                yv = ych_sb[:, sl * 512:(sl + 1) * 512].rearrange(
                    "p (r d) -> p r d", d=64)
                pc3 = pC[spar][:, :].rearrange("p (r d) -> p r d", d=64)
                p.g(nc.vector, nc.vector.tensor_tensor(yv, pc3, egv, ALU.mult))
                v_y1.inc(nc.vector.tensor_tensor(
                    ych_sb[:, sl * 512:(sl + 1) * 512],
                    ych_sb[:, sl * 512:(sl + 1) * 512],
                    y_sb[:, par * D + sl * 512:par * D + (sl + 1) * 512], ALU.add))
            nc.scalar.wait_ge(d_in[par].h, d_in[par].v)
            if sc >= 1:
                nc.scalar.wait_ge(v_yg.h, v_yg.v)
            a_z.inc(nc.scalar.activation(sq_sb[:, :],
                                         z_sb[:, par * D:(par + 1) * D], AF.Silu))
            nc.vector.wait_ge(a_z.h, a_z.v)
            nc.vector.wait_ge(v_y1.h, v_y1.v)
            v_yg.inc(nc.vector.tensor_tensor(yg_sb[:, :], ych_sb[:, :], sq_sb[:, :],
                                             ALU.mult))
            nc.scalar.wait_ge(v_yg.h, v_yg.v)
            a_sq.inc(nc.scalar.activation(sq_sb[:, :], yg_sb[:, :], AF.Square,
                                          accum_out=acc_sb[:, 0:1]))
            nc.vector.wait_ge(a_sq.h, a_sq.v)
            p.g(nc.vector, nc.vector.tensor_scalar(acc_sb[:, 1:2], acc_sb[:, 0:1],
                                                   1.0 / D, 1e-5, ALU.mult, ALU.add))
            v_rc.inc(nc.vector.engine_nop())
            nc.scalar.wait_ge(v_rc.h, v_rc.v)
            p.g(nc.scalar, nc.scalar.activation(acc_sb[:, 2:3], acc_sb[:, 1:2], AF.Ln))
            a_rs.inc(nc.scalar.activation(acc_sb[:, 2:3], acc_sb[:, 2:3], AF.Exp,
                                          scale=-0.5))
            nc.vector.wait_ge(a_rs.h, a_rs.v)
            p.g(nc.vector, nc.vector.tensor_scalar(ygn_sb[:, :], yg_sb[:, :],
                                                   acc_sb[:, 2:3], None, ALU.mult))
            nc.vector.wait_ge(v_nw.h, v_nw.v)
            v_gn.inc(nc.vector.tensor_tensor(ygn_sb[:, :], ygn_sb[:, :],
                                             nwb_sb[:, :], ALU.mult))
            nc.tensor.wait_ge(v_gn.h, v_gn.v)
            for cc in range(NCH):
                tpar = cc % 2
                if sc * NCH + cc >= 2:
                    nc.tensor.wait_ge(v_tr.h, v_tr.v - 1)
                t_tr.inc(nc.tensor.transpose(pT[tpar][:, :],
                                             ygn_sb[:, cc * 128:(cc + 1) * 128],
                                             cst["identb"][:, :]))
                nc.vector.wait_ge(t_tr.h, t_tr.v)
                v_tr.inc(nc.vector.tensor_copy(
                    ygf_sb[:, cc * T + sc * 128:cc * T + (sc + 1) * 128],
                    pT[tpar][:, :]))

    emit_dense_residual(p, ygf_sb, wout, xfm, xout, T, ready_sem=p.sem("v_tr"))
    return p.done()


def mamba_inmaps_c(inputs, xs, ra, rb, T=TPC):
    hc = host_const_arrays()
    base = {
        "nwrow": _f32(np.asarray(inputs["L0_attn_a_norm_w"]).reshape(1, D)),
        "wout": _bf(blockify(np.asarray(inputs["L0_attn_a_out_proj_w"]))),
        "c_ones1f": hc["ones1f"], "c_identb": hc["identb"],
    }
    maps = []
    for c in range(NCORES):
        b, h = c // 2, c % 2
        hin = np.zeros((128, D), np.float32) if h == 0 else _f32(rb[c - 1]["hout"])
        maps.append(dict(base, xfm=xs[c], ys=rb[c]["ys"], elg=rb[c]["elg"],
                         hin=hin, zsp=ra[c]["zs"], xbcs=ra[c]["xbcs"]))
    return maps



def build_gla_a(T=TPC, p=None):
    p = p or LP("gla_a")
    nc = p.nc
    xfm = p.inp("xfm", (D, T))
    wq = p.inp("wq", (NCH, 128, NCH, 128), BF16)
    wk = p.inp("wk", (8, 128, NCH, 128), BF16)
    wv = p.inp("wv", (8, 128, NCH, 128), BF16)
    xq = p.out("xq", (D, T), BF16)
    xk = p.out("xk", (1024, T), BF16)
    xv = p.out("xv", (1024, T), BF16)
    ktp = p._io.get("ktp") if p._io is not None else None
    vtp = p._io.get("vtp") if p._io is not None else None

    cst, dma_c = load_consts(p, [
        ("onesb", (128, 1), BF16), ("ones1f", (1, 128), F32),
    ])
    xn_sb = p.sb("xn_sb", [128, NCH * T], BF16)
    w_sb = p.sb("w_sb", [128, 2 * NCH * 128], BF16)
    stg = p.sb("stg", [128, 2 * T], BF16)
    for e in [nc.vector, nc.scalar, nc.tensor, nc.gpsimd, nc.sync]:
        e.wait_ge(dma_c.h, dma_c.v)
    v_ap = emit_ln(p, xfm, xn_sb, T, cst)
    nc.tensor.wait_ge(v_ap.h, v_ap.v)

    ct = col_tiles(T)
    dw = [p.sem("dw0"), p.sem("dw1")]
    t_p = p.sem("t_p")
    v_e = p.sem("v_e")
    dwb = [dw[0].v, dw[1].v]
    veb = v_e.v
    s_o = [p.sem("s_o0"), p.sem("s_o1")]

    panels = [(wq, i, xq, i) for i in range(NCH)] + \
             [(wk, i, xk, i) for i in range(8)] + \
             [(wv, i, xv, i) for i in range(8)]

    def load_panel(pi, par):
        src, oc, _, _ = panels[pi]
        dst = w_sb[:, par * NCH * 128:(par + 1) * NCH * 128].rearrange(
            "p (kc m) -> p kc m", m=128)
        dw[par].inc(nc.gpsimd.dma_start(out=dst, in_=src[oc]), 16)

    load_panel(0, 0)
    load_panel(1, 1)
    with (
        p.psum("pP0", [128, T], F32) as pP0,
        p.psum("pP1", [128, T], F32) as pP1,
    ):
        pP = [pP0, pP1]
        for pi in range(len(panels)):
            par = pi % 2
            nc.tensor.wait_ge(dw[par].h, dwb[par] + 16 * (pi // 2 + 1))
            if pi >= 2:
                nc.tensor.wait_ge(v_e.h, veb + pi - 1)
            ins = None
            for kc in range(NCH):
                w = w_sb[:, par * NCH * 128 + kc * 128:par * NCH * 128 + (kc + 1) * 128]
                for (t0, t1) in ct:
                    ins = nc.tensor.matmul(pP[par][:, t0:t1], w,
                                           xn_sb[:, kc * T + t0:kc * T + t1],
                                           start=(kc == 0), stop=(kc == NCH - 1),
                                           skip_group_check=True)
            t_p.inc(ins)
            if pi + 2 < len(panels):
                nc.gpsimd.wait_ge(t_p.h, t_p.v)
                load_panel(pi + 2, par)
            nc.vector.wait_ge(t_p.h, t_p.v)
            if pi >= 2:
                nc.vector.wait_ge(s_o[par].h, s_o[par].v)
            v_e.inc(nc.vector.tensor_copy(stg[:, par * T:(par + 1) * T],
                                          pP[par][:, :]))
            _, _, dst, row = panels[pi]
            ins = nc.sync.dma_start(out=fm_chunk(dst, row, T),
                                    in_=stg[:, par * T:(par + 1) * T])
            ins._wait_ge(v_e.h, v_e.v)
            s_o[par].inc(ins, 16)
            if ktp is not None and pi >= 32:
                tp = ktp if pi < 40 else vtp
                ins = nc.sync.dma_start(
                    out=tp.rearrange("(g pp) v -> g pp v", pp=128)[row],
                    in_=stg[:, par * T + T - 64:par * T + T])
                ins._wait_ge(v_e.h, v_e.v)
                s_o[par].inc(ins, 16)
        nc.sync.wait_ge(s_o[0].h, s_o[0].v)
        nc.sync.wait_ge(s_o[1].h, s_o[1].v)
    return p.done()


def build_gla_b(T=TPC, skip=(), p=None):
    p = p or LP("gla_b")
    nc = p.nc
    NCk = T // 128
    GATE = 1.0 / 16.0
    xq = p.inp("xq", (D, T), BF16)
    xk = p.inp("xk", (1024, T), BF16)
    xv = p.inp("xv", (1024, T), BF16)
    ktail = p.inp("ktail", (1024, 128), BF16)
    vtail = p.inp("vtail", (1024, 128), BF16)
    qcos = p.inp("qcos", (T, 128))
    qsin = p.inp("qsin", (T, 128))
    kcos = p.inp("kcos", (T + 64, 128))
    ksin = p.inp("ksin", (T + 64, 128))
    swm = p.inp("swm", (3, 128, 64))
    obase = p.out("obase", (T, D), BF16)
    qgs = p.out("qgs", (D, T), BF16)
    sloc = p.out("sloc", (1024, 128))

    cst, dma_c = load_consts(p, [
        ("onesb", (128, 1), BF16), ("ones1f", (1, 128), F32),
        ("identb", (128, 128), BF16), ("identf", (128, 128), F32),
        ("ltrib", (128, 128), BF16),
    ])
    swm_sb = p.sb("swm_sb", [128, 192], F32)
    dma_c.inc(nc.sync.dma_start(
        out=swm_sb[:, :].rearrange("p (a b) -> p a b", b=64),
        in_=swm.rearrange("a p b -> p a b")), 16)

    q_sb = p.sb("q_sb", [128, 2 * D], BF16)
    k_sb = p.sb("k_sb", [128, 2 * 1024], BF16)
    v_sb = p.sb("v_sb", [128, 2 * 1024], BF16)
    sq_sb = p.sb("sq_sb", [128, D], BF16)
    sk_sb = p.sb("sk_sb", [128, 2 * 1024], BF16)
    cs_sb = p.sb("cs_sb", [128, 4 * 128], F32)
    qs_sb = p.sb("qs_sb", [128, D], F32)
    rp_sb = p.sb("rp_sb", [128, D], F32)
    zs_sb = p.sb("zs_sb", [128, 64], F32)
    ks_sb = p.sb("ks_sb", [128, 1024], F32)
    gs_sb = p.sb("gs_sb", [128, 1024], F32)
    g1_sb = p.sb("g1_sb", [128, 1024], F32)
    gsb_sb = p.sb("gsb_sb", [128, 1024], BF16)
    eG_sb = p.sb("eG_sb", [128, 1024], BF16)
    eGi_sb = p.sb("eGi_sb", [128, 1024], BF16)
    qt_sb = p.sb("qt_sb", [128, D], BF16)
    kt_sb = p.sb("kt_sb", [128, 1024], BF16)
    qtT_sb = p.sb("qtT_sb", [128, D], BF16)
    qgT_sb = p.sb("qgT_sb", [128, D], BF16)
    ktT_sb = p.sb("ktT_sb", [128, 1024], BF16)
    sqT_sb = p.sb("sqT_sb", [128, D], BF16)
    skT_sb = p.sb("skT_sb", [128, 2 * 1024], BF16)
    vaug_sb = p.sb("vaug_sb", [128, 2 * 8 * 136], BF16)
    wv_sb = p.sb("wv_sb", [128, 8 * 136], BF16)
    crow_sb = p.sb("crow_sb", [1, 2 * 1024], F32)
    ec_sb = p.sb("ec_sb", [128, 16], F32)
    ecl_sb = p.sb("ecl_sb", [128, 16], F32)
    S_sb = p.sb("S_sb", [128, 1024], F32)
    Sb_sb = p.sb("Sb_sb", [128, 1024], BF16)
    Ss_sb = p.sb("Ss_sb", [128, 128], F32)
    am_sb = p.sb("am_sb", [128, 2 * 128], BF16)
    pw_sb = p.sb("pw_sb", [128, 2 * 64], BF16)
    psm_sb = p.sb("psm_sb", [128, 2 * 64], F32)
    ost_sb = p.sb("ost_sb", [128, 2 * D], BF16)
    zc_sb = p.sb("zc_sb", [128, 4], F32)

    for e in [nc.vector, nc.scalar, nc.tensor, nc.gpsimd, nc.sync]:
        e.wait_ge(dma_c.h, dma_c.v)

    d_in = [p.sem("d_in0"), p.sem("d_in1")]
    d_cs = p.sem("d_cs")
    d_tl = p.sem("d_tl")
    v_sm = p.sem("v_sm")
    a_sm = p.sem("a_sm")
    v_rp = p.sem("v_rp")
    t_G = p.sem("t_G")
    v_G = p.sem("v_G")
    a_G = p.sem("a_G")
    v_til = p.sem("v_til")
    t_tr = p.sem("t_tr")
    v_tr = p.sem("v_tr")
    t_A = p.sem("t_A")
    v_A = p.sem("v_A")
    t_o = p.sem("t_o")
    v_o = p.sem("v_o")
    s_ob = [p.sem("s_ob0"), p.sem("s_ob1")]
    s_qg = [p.sem("s_qg0"), p.sem("s_qg1")]
    t_u = p.sem("t_u")
    v_S = p.sem("v_S")
    t_sw = p.sem("t_sw")
    v_swm = p.sem("v_swm")
    a_sw = p.sem("a_sw")
    t_pv = p.sem("t_pv")
    v_sw = p.sem("v_sw")
    v_va = p.sem("v_va")
    vz = p.sem("vz")

    def rope(dst3, src3, c1, c2, s1, s2, t3, vsem):
        p.g(nc.vector, nc.vector.tensor_tensor(dst3[:, :, 0:64], src3[:, :, 64:128],
                                               s1, ALU.mult))
        p.g(nc.vector, nc.vector.tensor_scalar(dst3[:, :, 0:64], dst3[:, :, 0:64],
                                               -1.0, None, ALU.mult))
        p.g(nc.vector, nc.vector.tensor_tensor(t3[:, :, 0:64], src3[:, :, 0:64],
                                               c1, ALU.mult))
        p.g(nc.vector, nc.vector.tensor_tensor(dst3[:, :, 0:64], dst3[:, :, 0:64],
                                               t3[:, :, 0:64], ALU.add))
        p.g(nc.vector, nc.vector.tensor_tensor(dst3[:, :, 64:128], src3[:, :, 0:64],
                                               s2, ALU.mult))
        p.g(nc.vector, nc.vector.tensor_tensor(t3[:, :, 64:128], src3[:, :, 64:128],
                                               c2, ALU.mult))
        vsem.inc(nc.vector.tensor_tensor(dst3[:, :, 64:128], dst3[:, :, 64:128],
                                         t3[:, :, 64:128], ALU.add))

    def csview(lo, hi, n, rows=128):
        return cs_sb[0:rows, lo:hi].rearrange("p (h d) -> p h d", h=1).broadcast_to(
            [rows, n, 64])

    vz.inc(nc.vector.memset(S_sb[:, :], 0.0))
    vz.inc(nc.vector.memset(Sb_sb[:, :], 0.0))
    vz.inc(nc.vector.memset(crow_sb[:, :], 0.0))
    vz.inc(nc.vector.memset(vaug_sb[:, :], 1.0))
    d_tl.inc(nc.sync.dma_start(out=k_sb[:, 1024:2048], in_=ktail,
                               transpose=True), 16)
    d_tl.inc(nc.sync.dma_start(out=v_sb[:, 1024:2048], in_=vtail,
                               transpose=True), 16)
    d_tl.inc(nc.sync.dma_start(out=cs_sb[64:128, 256:384], in_=kcos[0:64, :]), 16)
    d_tl.inc(nc.sync.dma_start(out=cs_sb[64:128, 384:512], in_=ksin[0:64, :]), 16)
    nc.vector.wait_ge(d_tl.h, d_tl.v)
    nc.vector.wait_ge(vz.h, vz.v)
    rope(sk_sb[64:128, 1024:2048].rearrange("p (h d) -> p h d", d=128),
         k_sb[64:128, 1024:2048].rearrange("p (h d) -> p h d", d=128),
         cs_sb[64:128, 256:320].rearrange("p (h d) -> p h d", h=1)
         .broadcast_to([64, 8, 64]),
         cs_sb[64:128, 320:384].rearrange("p (h d) -> p h d", h=1)
         .broadcast_to([64, 8, 64]),
         cs_sb[64:128, 384:448].rearrange("p (h d) -> p h d", h=1)
         .broadcast_to([64, 8, 64]),
         cs_sb[64:128, 448:512].rearrange("p (h d) -> p h d", h=1)
         .broadcast_to([64, 8, 64]),
         rp_sb[64:128, 0:1024].rearrange("p (h d) -> p h d", d=128), v_rp)
    va1 = vaug_sb[:, 8 * 136:2 * 8 * 136].rearrange("p (g w) -> p g w", w=136)
    v_rp.inc(nc.vector.tensor_copy(
        va1[64:128, :, 0:128],
        v_sb[64:128, 1024:2048].rearrange("p (g d) -> p g d", d=128)))
    v_rp.inc(nc.vector.memset(sk_sb[0:64, 1024:2048], 0.0))
    with p.psum("pTT0", [128, 128], BF16) as pTT0, \
         p.psum("pTT1", [128, 128], BF16) as pTT1:
        pT2i = [pTT0, pTT1]
        nc.tensor.wait_ge(v_rp.h, v_rp.v)
        for g in range(8):
            tp = g % 2
            if g >= 2:
                nc.tensor.wait_ge(v_tr.h, v_tr.v - 1)
            t_tr.inc(nc.tensor.transpose(pT2i[tp][:, :],
                                         sk_sb[:, 1024 + g * 128:
                                               1024 + (g + 1) * 128],
                                         cst["identb"][:, :]))
            nc.vector.wait_ge(t_tr.h, t_tr.v)
            v_tr.inc(nc.vector.tensor_copy(
                skT_sb[:, 1024 + g * 128:1024 + (g + 1) * 128], pT2i[tp][:, :]))
    nc.tensor.wait_ge(v_tr.h, v_tr.v)

    for sc in range(NCk):
        par = sc % 2
        opp = 1 - par
        cols = slice(sc * 128, (sc + 1) * 128)
        if sc >= 2:
            nc.sync.wait_ge(t_pv.h, t_pv.v)
            nc.sync.wait_ge(t_u.h, t_u.v)
            for dd in d_in:
                nc.sync.wait_ge(dd.h, dd.v)
            nc.sync.wait_ge(d_cs.h, d_cs.v)
        if sc == 0:
            nc.sync.wait_ge(v_rp.h, v_rp.v)
        d_in[par].inc(nc.sync.dma_start(out=q_sb[:, par * D:(par + 1) * D],
                                        in_=xq[:, cols], transpose=True), 16)
        d_in[par].inc(nc.sync.dma_start(out=k_sb[:, par * 1024:(par + 1) * 1024],
                                        in_=xk[:, cols], transpose=True), 16)
        d_in[par].inc(nc.sync.dma_start(out=v_sb[:, par * 1024:(par + 1) * 1024],
                                        in_=xv[:, cols], transpose=True), 16)
        d_cs.inc(nc.sync.dma_start(out=cs_sb[:, 0:128], in_=qcos[cols, :]), 16)
        d_cs.inc(nc.sync.dma_start(out=cs_sb[:, 128:256], in_=qsin[cols, :]), 16)
        d_cs.inc(nc.sync.dma_start(
            out=cs_sb[:, 256:384], in_=kcos[64 + sc * 128:64 + (sc + 1) * 128, :]), 16)
        d_cs.inc(nc.sync.dma_start(
            out=cs_sb[:, 384:512], in_=ksin[64 + sc * 128:64 + (sc + 1) * 128, :]), 16)

        nc.scalar.wait_ge(d_in[par].h, d_in[par].v)
        if sc >= 1:
            nc.scalar.wait_ge(v_til.h, v_til.v)
        a_sm.inc(nc.scalar.activation(qs_sb[:, :], q_sb[:, par * D:(par + 1) * D],
                                      AF.Exp))
        a_sm.inc(nc.scalar.activation(ks_sb[:, :],
                                      k_sb[:, par * 1024:(par + 1) * 1024], AF.Exp))
        p.g(nc.scalar, nc.scalar.activation(
            gs_sb[:, :], k_sb[:, par * 1024:(par + 1) * 1024], AF.Abs))
        a_sm.inc(nc.scalar.activation(gs_sb[:, :], gs_sb[:, :], AF.Exp, scale=-1.0))
        nc.vector.wait_ge(a_sm.h, a_sm.v)
        q3 = qs_sb[:, :].rearrange("p (h d) -> p h d", d=128)
        p.g(nc.vector, nc.vector.reduce_sum(
            zs_sb[:, 0:32].rearrange("p (h o) -> p h o", o=1), q3, axis=AX.X))
        p.g(nc.vector, nc.vector.reciprocal(zs_sb[:, 0:32], zs_sb[:, 0:32]))
        p.g(nc.vector, nc.vector.tensor_tensor(
            q3, q3, zs_sb[:, 0:32].rearrange("p (h o) -> p h o", o=1)
            .broadcast_to([128, 32, 128]), ALU.mult))
        k3 = ks_sb[:, :].rearrange("p (h d) -> p h d", d=128)
        p.g(nc.vector, nc.vector.reduce_sum(
            zs_sb[:, 32:40].rearrange("p (h o) -> p h o", o=1), k3, axis=AX.X))
        p.g(nc.vector, nc.vector.reciprocal(zs_sb[:, 32:40], zs_sb[:, 32:40]))
        p.g(nc.vector, nc.vector.tensor_tensor(
            k3, k3, zs_sb[:, 32:40].rearrange("p (h o) -> p h o", o=1)
            .broadcast_to([128, 8, 128]), ALU.mult))
        v_sm.inc(nc.vector.tensor_scalar(gs_sb[:, :], gs_sb[:, :], 1.0, None,
                                         ALU.add))
        nc.scalar.wait_ge(v_sm.h, v_sm.v)
        a_sm.inc(nc.scalar.activation(g1_sb[:, :], gs_sb[:, :], AF.Ln))
        nc.vector.wait_ge(a_sm.h, a_sm.v)
        p.g(nc.vector, nc.vector.tensor_scalar(
            gs_sb[:, :], k_sb[:, par * 1024:(par + 1) * 1024], 0.0, None, ALU.min))
        p.g(nc.vector, nc.vector.tensor_tensor(gs_sb[:, :], gs_sb[:, :],
                                               g1_sb[:, :], ALU.subtract))
        p.g(nc.vector, nc.vector.tensor_scalar(gs_sb[:, :], gs_sb[:, :], GATE,
                                               None, ALU.mult))
        v_sm.inc(nc.vector.tensor_copy(gsb_sb[:, :], gs_sb[:, :]))
        nc.vector.wait_ge(d_cs.h, d_cs.v)
        rope(sq_sb[:, :].rearrange("p (h d) -> p h d", d=128),
             q_sb[:, par * D:(par + 1) * D].rearrange("p (h d) -> p h d", d=128),
             csview(0, 64, 32), csview(64, 128, 32),
             csview(128, 192, 32), csview(192, 256, 32),
             rp_sb[:, :].rearrange("p (h d) -> p h d", d=128), v_rp)
        rope(sk_sb[:, par * 1024:(par + 1) * 1024].rearrange("p (h d) -> p h d",
                                                             d=128),
             k_sb[:, par * 1024:(par + 1) * 1024].rearrange("p (h d) -> p h d",
                                                            d=128),
             csview(256, 320, 8), csview(320, 384, 8),
             csview(384, 448, 8), csview(448, 512, 8),
             rp_sb[:, 0:1024].rearrange("p (h d) -> p h d", d=128), v_rp)
        with p.psum(f"pG{sc}", [128, 1024], F32) as pG, \
             p.psum(f"pR{sc}", [1, 1024], F32) as pR, \
             p.psum(f"pE{sc}", [128, 16], F32) as pE:
            nc.tensor.wait_ge(v_sm.h, v_sm.v)
            ins = None
            for (t0, t1) in [(0, 512), (512, 1024)]:
                nc.tensor.matmul(pG[:, t0:t1], cst["ltrib"][:, :],
                                 gsb_sb[:, t0:t1], start=True, stop=True,
                                 skip_group_check=True)
                nc.tensor.matmul(pR[:, t0:t1], cst["onesb"][:, :],
                                 gsb_sb[:, t0:t1], start=True, stop=True,
                                 skip_group_check=True)
            for g in range(8):
                ins = nc.tensor.transpose(pE[:, g:g + 1],
                                          crow_sb[:, g * 128:(g + 1) * 128],
                                          cst["identf"][0:1, 0:1])
            t_G.inc(ins)
            nc.vector.wait_ge(t_G.h, t_G.v)
            p.g(nc.vector, nc.vector.tensor_copy(ec_sb[:, 0:8], pE[:, 0:8]))
            p.g(nc.vector, nc.vector.tensor_copy(crow_sb[:, 1024:2048], pR[:, :]))
            p.g(nc.vector, nc.vector.tensor_tensor(
                crow_sb[:, 0:1024], crow_sb[:, 0:1024], crow_sb[:, 1024:2048],
                ALU.add))
            v_G.inc(nc.vector.engine_nop())
            nc.tensor.wait_ge(v_G.h, v_G.v)
            ins = None
            for g in range(8):
                ins = nc.tensor.transpose(pE[:, 8 + g:9 + g],
                                          crow_sb[:, 1024 + g * 128:
                                                  1024 + (g + 1) * 128],
                                          cst["identf"][0:1, 0:1])
            t_G.inc(ins)
            nc.vector.wait_ge(t_G.h, t_G.v)
            v_G.inc(nc.vector.tensor_copy(ecl_sb[:, 0:8], pE[:, 8:16]))
            nc.scalar.wait_ge(v_G.h, v_G.v)
            p.g(nc.scalar, nc.scalar.activation(ec_sb[:, 8:16], ec_sb[:, 0:8],
                                                AF.Exp))
            p.g(nc.scalar, nc.scalar.activation(ecl_sb[:, 8:16], ecl_sb[:, 0:8],
                                                AF.Exp))
            a_G.inc(nc.scalar.activation(eG_sb[:, :], pG[:, :], AF.Exp))
            a_G.inc(nc.scalar.activation(eGi_sb[:, :], pG[:, :], AF.Exp,
                                         scale=-1.0))
        nc.vector.wait_ge(a_G.h, a_G.v)
        e4 = eG_sb[:, :].rearrange("p (g d) -> p g d", d=128).unsqueeze(
            2).broadcast_to([128, 8, 4, 128])
        p.g(nc.vector, nc.vector.tensor_tensor(
            qt_sb[:, :].rearrange("p (g r d) -> p g r d", g=8, r=4),
            qs_sb[:, :].rearrange("p (g r d) -> p g r d", g=8, r=4), e4, ALU.mult))
        v_til.inc(nc.vector.tensor_tensor(
            kt_sb[:, :].rearrange("p (g d) -> p g d", d=128),
            ks_sb[:, :].rearrange("p (g d) -> p g d", d=128),
            eGi_sb[:, :].rearrange("p (g d) -> p g d", d=128), ALU.mult))
        with p.psum(f"pTa{sc}", [128, 128], BF16) as pTa, \
             p.psum(f"pTb{sc}", [128, 128], BF16) as pTb:
            pT2 = [pTa, pTb]
            nc.tensor.wait_ge(v_til.h, v_til.v)
            nc.tensor.wait_ge(v_rp.h, v_rp.v)
            if sc >= 1:
                nc.vector.wait_ge(s_qg[opp].h, s_qg[opp].v)
            plan = ([(kt_sb, hh, ktT_sb, hh, None) for hh in range(8)] +
                    [(qt_sb, hh, qtT_sb, hh, hh // 4) for hh in range(32)] +
                    [(sq_sb, hh, sqT_sb, hh, None) for hh in range(32)] +
                    [(sk_sb, 8 * par + hh, skT_sb, 8 * par + hh, None)
                     for hh in range(8)])
            for i, (src, si, dst, di, gq) in enumerate(plan):
                tp = i % 2
                if i >= 2:
                    nc.tensor.wait_ge(v_tr.h, v_tr.v - 1)
                t_tr.inc(nc.tensor.transpose(pT2[tp][:, :],
                                             src[:, si * 128:(si + 1) * 128],
                                             cst["identb"][:, :]))
                nc.vector.wait_ge(t_tr.h, t_tr.v)
                if gq is not None:
                    p.g(nc.vector, nc.vector.tensor_copy(
                        dst[:, di * 128:(di + 1) * 128], pT2[tp][:, :]))
                    v_tr.inc(nc.vector.tensor_scalar(
                        qgT_sb[:, di * 128:(di + 1) * 128], pT2[tp][:, :],
                        ec_sb[:, 8 + gq:9 + gq], None, ALU.mult))
                else:
                    v_tr.inc(nc.vector.tensor_copy(
                        dst[:, di * 128:(di + 1) * 128], pT2[tp][:, :]))
        if sc >= 2:
            nc.sync.wait_ge(s_qg[par].h, s_qg[par].v)
        ins = nc.sync.dma_start(
            out=qgs.rearrange("(c p) t -> p c t", p=128)[:, :, cols],
            in_=qgT_sb[:, :].rearrange("p (c t) -> p c t", t=128))
        ins._wait_ge(v_tr.h, v_tr.v)
        s_qg[par].inc(ins, 16)
        nc.vector.wait_ge(d_in[par].h, d_in[par].v)
        if sc >= 1:
            nc.vector.wait_ge(t_pv.h, t_pv.v)
        va3 = vaug_sb[:, par * 8 * 136:(par + 1) * 8 * 136].rearrange(
            "p (g w) -> p g w", w=136)
        p.g(nc.vector, nc.vector.tensor_copy(
            va3[:, :, 0:128],
            v_sb[:, par * 1024:(par + 1) * 1024].rearrange("p (g d) -> p g d",
                                                           d=128)))
        v_va.inc(nc.vector.tensor_copy(
            wv_sb[0:64, :], vaug_sb[0:64, par * 8 * 136:(par + 1) * 8 * 136]))
        v_va.inc(nc.vector.tensor_copy(
            wv_sb[64:128, :], vaug_sb[64:128, opp * 8 * 136:(opp + 1) * 8 * 136]))
        with p.psum(f"pA{sc}", [128, 128], F32) as pA, \
             p.psum(f"pO0{sc}", [128, 128], F32) as pO0, \
             p.psum(f"pO1{sc}", [128, 128], F32) as pO1, \
             p.psum(f"pU{sc}", [128, 128], F32) as pU:
            pO = [pO0, pO1]
            nc.tensor.wait_ge(v_tr.h, v_tr.v)
            for h in range(32):
                g = h // 4
                hp = h % 2
                if h >= 1 or sc >= 1:
                    nc.tensor.wait_ge(v_A.h, v_A.v)
                t_A.inc(nc.tensor.matmul(pA[:, :], ktT_sb[:, g * 128:(g + 1) * 128],
                                         qtT_sb[:, h * 128:(h + 1) * 128],
                                         start=True, stop=True,
                                         skip_group_check=True))
                nc.vector.wait_ge(t_A.h, t_A.v)
                if h >= 2 or sc >= 1:
                    nc.vector.wait_ge(t_o.h, t_o.v - 2)
                v_A.inc(nc.vector.tensor_tensor(
                    am_sb[:, hp * 128:(hp + 1) * 128], pA[:, :], cst["ltrib"][:, :],
                    ALU.mult))
                nc.tensor.wait_ge(v_A.h, v_A.v)
                if h >= 2 or sc >= 1:
                    nc.tensor.wait_ge(v_o.h, v_o.v - 1)
                t_o.inc(nc.tensor.matmul(pO[hp][:, :],
                                         am_sb[:, hp * 128:(hp + 1) * 128],
                                         v_sb[:, par * 1024 + g * 128:
                                              par * 1024 + (g + 1) * 128],
                                         start=True, stop=False,
                                         skip_group_check=True))
                t_o.inc(nc.tensor.matmul(pO[hp][:, :],
                                         qtT_sb[:, h * 128:(h + 1) * 128],
                                         Sb_sb[:, g * 128:(g + 1) * 128],
                                         start=False, stop=True,
                                         skip_group_check=True))
                nc.vector.wait_ge(t_o.h, t_o.v)
                if sc >= 2 and h == 0:
                    nc.vector.wait_ge(s_ob[par].h, s_ob[par].v)
                v_o.inc(nc.vector.tensor_scalar(
                    ost_sb[:, par * D + h * 128:par * D + (h + 1) * 128],
                    pO[hp][:, :], 0.5, None, ALU.mult))
            for g in range(8):
                if g >= 1 or sc >= 1:
                    nc.tensor.wait_ge(v_S.h, v_S.v)
                t_u.inc(nc.tensor.matmul(
                    pU[:, :], kt_sb[:, g * 128:(g + 1) * 128],
                    v_sb[:, par * 1024 + g * 128:par * 1024 + (g + 1) * 128],
                    start=True, stop=True, skip_group_check=True))
                nc.vector.wait_ge(t_u.h, t_u.v)
                p.g(nc.vector, nc.vector.tensor_tensor(
                    Ss_sb[:, :], S_sb[:, g * 128:(g + 1) * 128], pU[:, :], ALU.add))
                p.g(nc.vector, nc.vector.tensor_scalar(
                    S_sb[:, g * 128:(g + 1) * 128], Ss_sb[:, :],
                    ecl_sb[:, 8 + g:9 + g], None, ALU.mult))
                v_S.inc(nc.vector.tensor_copy(
                    Sb_sb[:, g * 128:(g + 1) * 128],
                    S_sb[:, g * 128:(g + 1) * 128]))
        with p.psum(f"pS0{sc}", [128, 64], F32) as pS0, \
             p.psum(f"pS1{sc}", [128, 64], F32) as pS1, \
             p.psum(f"pW0{sc}", [128, 136], F32) as pW0, \
             p.psum(f"pW1{sc}", [128, 136], F32) as pW1:
            pS = [pS0, pS1]
            pW = [pW0, pW1]
            qi = 0
            nc.tensor.wait_ge(v_va.h, v_va.v)
            for h in range(0 if 4 in skip else 32):
                g = h // 4
                for blk in range(2):
                    sp = qi % 2
                    first = (sc == 0 and blk == 0)
                    klow = skT_sb[:, par * 1024 + g * 128:par * 1024 + g * 128 + 64]
                    if blk == 0:
                        khigh = skT_sb[:, opp * 1024 + g * 128 + 64:
                                       opp * 1024 + (g + 1) * 128]
                        vwin = wv_sb[:, g * 136:(g + 1) * 136]
                    else:
                        khigh = skT_sb[:, par * 1024 + g * 128 + 64:
                                       par * 1024 + (g + 1) * 128]
                        vwin = vaug_sb[:, par * 8 * 136 + g * 136:
                                       par * 8 * 136 + (g + 1) * 136]
                    qsl = sqT_sb[:, h * 128 + blk * 64:h * 128 + (blk + 1) * 64]
                    if qi >= 2:
                        nc.tensor.wait_ge(v_swm.h, v_swm.v)
                    nc.tensor.matmul(pS[sp][0:64, :], klow, qsl,
                                     start=True, stop=True, skip_group_check=True)
                    t_sw.inc(nc.tensor.matmul(pS[sp][64:128, :], khigh, qsl,
                                              start=True, stop=True,
                                              skip_group_check=True,
                                              tile_position=(0, 64)))
                    if 7 in skip:
                        qi += 1
                        continue
                    nc.vector.wait_ge(t_sw.h, t_sw.v)
                    if qi >= 2:
                        nc.vector.wait_ge(a_sw.h, max(a_sw.v - 1, 0))
                    mvar = 2 if first else blk
                    v_swm.inc(nc.vector.tensor_tensor(
                        psm_sb[:, sp * 64:(sp + 1) * 64], pS[sp][:, :],
                        swm_sb[:, mvar * 64:(mvar + 1) * 64], ALU.add))
                    if 8 in skip:
                        qi += 1
                        continue
                    nc.scalar.wait_ge(v_swm.h, v_swm.v)
                    if qi >= 2:
                        nc.scalar.wait_ge(t_pv.h, max(t_pv.v - 1, 0))
                    a_sw.inc(nc.scalar.activation(pw_sb[:, sp * 64:(sp + 1) * 64],
                                                  psm_sb[:, sp * 64:(sp + 1) * 64],
                                                  AF.Exp))
                    if 5 in skip:
                        qi += 1
                        continue
                    nc.tensor.wait_ge(a_sw.h, a_sw.v)
                    if qi >= 2:
                        nc.tensor.wait_ge(v_sw.h, max(v_sw.v - 1, 0))
                    br = slice(blk * 64, (blk + 1) * 64)
                    tp_kw = {} if blk == 0 else {"tile_position": (0, 64)}
                    t_pv.inc(nc.tensor.matmul(pW[sp][br, :],
                                              pw_sb[:, sp * 64:(sp + 1) * 64],
                                              vwin, start=True, stop=True,
                                              skip_group_check=True, **tp_kw))
                    if 6 in skip:
                        qi += 1
                        continue
                    nc.vector.wait_ge(t_pv.h, t_pv.v)
                    p.g(nc.vector, nc.vector.tensor_scalar(
                        zc_sb[br, sp:sp + 1], pW[sp][br, 128:129], 2.0, None,
                        ALU.mult))
                    p.g(nc.vector, nc.vector.reciprocal(zc_sb[br, sp:sp + 1],
                                                        zc_sb[br, sp:sp + 1]))
                    v_sw.inc(nc.vector.scalar_tensor_tensor(
                        ost_sb[:, par * D + h * 128:par * D + (h + 1) * 128]
                        [br, :],
                        pW[sp][br, 0:128], zc_sb[br, sp:sp + 1],
                        ost_sb[:, par * D + h * 128:par * D + (h + 1) * 128]
                        [br, :], ALU.mult, ALU.add))
                    qi += 1
        ins = nc.sync.dma_start(
            out=obase.rearrange("(c p) d -> c p d", p=128)[sc],
            in_=ost_sb[:, par * D:(par + 1) * D])
        ins._wait_ge(v_sw.h, v_sw.v)
        s_ob[par].inc(ins, 16)

    s_S = p.sem("s_S")
    nc.sync.wait_ge(v_S.h, v_S.v)
    s_S.inc(nc.sync.dma_start(out=sloc.rearrange("(g p) v -> p g v", p=128),
                              in_=S_sb[:, :].rearrange("p (g v) -> p g v", v=128)),
            16)
    nc.sync.wait_ge(s_S.h, s_S.v)
    nc.sync.wait_ge(s_ob[0].h, s_ob[0].v)
    nc.sync.wait_ge(s_ob[1].h, s_ob[1].v)
    nc.sync.wait_ge(s_qg[0].h, s_qg[0].v)
    nc.sync.wait_ge(s_qg[1].h, s_qg[1].v)
    return p.done()


def build_gla_c(T=TPC, p=None):
    p = p or LP("gla_c")
    nc = p.nc
    NCk = T // 128
    xfm = p.inp("xfm", (D, T))
    ob = p.inp("ob", (T, D), BF16)
    qgs = p.inp("qgs", (D, T), BF16)
    sin = p.inp("sin", (1024, 128))
    wo = p.inp("wo", (NCH, 128, NCH, 128), BF16)
    xout = p.out("xout", (D, T))

    cst, dma_c = load_consts(p, [("identb", (128, 128), BF16)])
    sin_sb = p.sb("sin_sb", [128, 8 * 128], BF16)
    dma_s = p.sem("dma_s")
    dma_s.inc(nc.gpsimd.dma_start(
        out=sin_sb[:, :].rearrange("p (g v) -> p g v", v=128),
        in_=sin.rearrange("(g p) v -> p g v", p=128)), 16)
    o_sb = p.sb("o_sb", [128, 2 * D], BF16)
    qg_sb = p.sb("qg_sb", [128, 2 * D], BF16)
    oc_sb = p.sb("oc_sb", [128, D], BF16)
    of_sb = p.sb("of_sb", [128, NCH * T], BF16)

    d_in = [p.sem("d_in0"), p.sem("d_in1")]
    t_c = p.sem("t_c")
    v_c = p.sem("v_c")
    t_tr = p.sem("t_tr")
    v_tr = p.sem("v_tr")
    for e in [nc.vector, nc.scalar, nc.tensor, nc.gpsimd, nc.sync]:
        e.wait_ge(dma_c.h, dma_c.v)
        e.wait_ge(dma_s.h, dma_s.v)

    with (
        p.psum("pC0", [128, 128], F32) as pC0,
        p.psum("pC1", [128, 128], F32) as pC1,
        p.psum("pT0", [128, 128], BF16) as pT0,
        p.psum("pT1", [128, 128], BF16) as pT1,
    ):
        pC = [pC0, pC1]
        pT = [pT0, pT1]
        for sc in range(NCk):
            par = sc % 2
            cols = slice(sc * 128, (sc + 1) * 128)
            if sc >= 2:
                nc.sync.wait_ge(v_c.h, v_c.v)
                for dd in d_in:
                    nc.sync.wait_ge(dd.h, dd.v)
            d_in[par].inc(nc.sync.dma_start(
                out=o_sb[:, par * D:(par + 1) * D],
                in_=ob.rearrange("(c p) d -> c p d", p=128)[sc]), 16)
            d_in[par].inc(nc.sync.dma_start(
                out=qg_sb[:, par * D:(par + 1) * D].rearrange(
                    "p (c t) -> p c t", t=128),
                in_=qgs.rearrange("(c p) t -> p c t", p=128)[:, :, cols]), 16)
            nc.tensor.wait_ge(d_in[par].h, d_in[par].v)
            for h in range(32):
                g = h // 4
                hp = h % 2
                if h >= 2 or sc >= 1:
                    nc.tensor.wait_ge(v_c.h, v_c.v - 1)
                t_c.inc(nc.tensor.matmul(
                    pC[hp][:, :], qg_sb[:, par * D + h * 128:par * D + (h + 1) * 128],
                    sin_sb[:, g * 128:(g + 1) * 128],
                    start=True, stop=True, skip_group_check=True))
                nc.vector.wait_ge(t_c.h, t_c.v)
                v_c.inc(nc.vector.scalar_tensor_tensor(
                    oc_sb[:, h * 128:(h + 1) * 128], pC[hp][:, :], 0.5,
                    o_sb[:, par * D + h * 128:par * D + (h + 1) * 128],
                    ALU.mult, ALU.add))
            nc.tensor.wait_ge(v_c.h, v_c.v)
            for cc in range(NCH):
                tp = cc % 2
                if sc * NCH + cc >= 2:
                    nc.tensor.wait_ge(v_tr.h, v_tr.v - 1)
                t_tr.inc(nc.tensor.transpose(pT[tp][:, :],
                                             oc_sb[:, cc * 128:(cc + 1) * 128],
                                             cst["identb"][:, :]))
                nc.vector.wait_ge(t_tr.h, t_tr.v)
                v_tr.inc(nc.vector.tensor_copy(
                    of_sb[:, cc * T + sc * 128:cc * T + (sc + 1) * 128],
                    pT[tp][:, :]))

    emit_dense_residual(p, of_sb, wo, xfm, xout, T, ready_sem=p.sem("v_tr"))
    return p.done()


def gla_inmaps_a(inputs, xs):
    hc = host_const_arrays()
    base = {
        "wq": _bf(blockify(np.asarray(inputs["L1_attn_b_Wq"]))),
        "wk": _bf(blockify(np.asarray(inputs["L1_attn_b_Wk"]))),
        "wv": _bf(blockify(np.asarray(inputs["L1_attn_b_Wv"]))),
        "c_onesb": hc["onesb"], "c_ones1f": hc["ones1f"],
    }
    return [dict(base, xfm=xs[c]) for c in range(NCORES)]


def _rope_tables(T=TPC):
    half = np.arange(64, dtype=np.float32)
    inv = 1.0 / (10000.0 ** (2.0 * half / 128.0))
    out = {}
    for h in range(2):
        qpos = np.arange(h * T, (h + 1) * T, dtype=np.float32)
        kpos = np.arange(h * T - 64, (h + 1) * T, dtype=np.float32)
        fq = qpos[:, None] * inv[None, :]
        fk = kpos[:, None] * inv[None, :]
        out[h] = (
            _f32(np.concatenate([np.cos(fq), np.cos(fq)], 1) / np.sqrt(128.0)),
            _f32(np.concatenate([np.sin(fq), np.sin(fq)], 1) / np.sqrt(128.0)),
            _f32(np.concatenate([np.cos(fk), np.cos(fk)], 1)),
            _f32(np.concatenate([np.sin(fk), np.sin(fk)], 1)),
        )
    return out


def _swa_masks(is_odd):
    i = np.arange(64)[:, None]
    j = np.arange(64)[None, :]
    neg = np.float32(-1e30)
    m0 = np.zeros((128, 64), np.float32)
    m0[0:64] = np.where(i <= j, 0.0, neg)
    m0[64:128] = np.where(i >= j, 0.0, neg)
    m1 = np.zeros((128, 64), np.float32)
    m1[0:64] = np.where(i >= j, 0.0, neg)
    m1[64:128] = np.where(i <= j, 0.0, neg)
    m2 = m0.copy()
    if not is_odd:
        m2[64:128] = neg
    return np.stack([m0, m1, m2])


def gla_inmaps_b(ra, T=TPC):
    hc = host_const_arrays()
    rt = _rope_tables(T)
    maps = []
    for c in range(NCORES):
        b, h = c // 2, c % 2
        ktail = np.zeros((1024, 128), BF)
        vtail = np.zeros((1024, 128), BF)
        if h == 1:
            ktail[:, 64:128] = ra[c - 1]["xk"][:, T - 64:T]
            vtail[:, 64:128] = ra[c - 1]["xv"][:, T - 64:T]
        qc, qs_, kc, ks_ = rt[h]
        maps.append({
            "xq": ra[c]["xq"], "xk": ra[c]["xk"], "xv": ra[c]["xv"],
            "ktail": ktail, "vtail": vtail,
            "qcos": qc, "qsin": qs_, "kcos": kc, "ksin": ks_,
            "swm": _swa_masks(h == 1),
            "c_onesb": hc["onesb"], "c_ones1f": hc["ones1f"],
            "c_identb": hc["identb"], "c_identf": hc["identf"],
            "c_ltrib": _bf(np.triu(np.ones((128, 128), np.float32))),
        })
    return maps


def gla_inmaps_c(inputs, xs, rb):
    hc = host_const_arrays()
    base = {
        "wo": _bf(blockify(np.asarray(inputs["L1_attn_b_Wo"]))),
        "c_identb": hc["identb"],
    }
    maps = []
    for c in range(NCORES):
        b, h = c // 2, c % 2
        sin = np.zeros((1024, 128), np.float32) if h == 0 else _f32(rb[c - 1]["sloc"])
        maps.append(dict(base, xfm=xs[c], ob=rb[c]["obase"], qgs=rb[c]["qgs"],
                         sin=sin))
    return maps


# version 11
# speedup vs baseline: 1.1090x; 1.1036x over previous
import contextlib
import numpy as np
import ml_dtypes
import concourse.bass as bass
import concourse.mybir as mybir
from concourse import aot_env
from concourse.bass_utils import run_bass_kernel_spmd

F32 = mybir.dt.float32
BF16 = mybir.dt.bfloat16
AF = mybir.ActivationFunctionType
ALU = mybir.AluOpType
AX = mybir.AxisListType
BF = ml_dtypes.bfloat16

D = 4096
NCH = 32
TPC = 1024
NCORES = 8



class Sem:
    def __init__(self, nc, es, name):
        self.h = es.enter_context(nc.semaphore(name))
        self.v = 0

    def inc(self, inst, n=1):
        self.v += n
        inst.then_inc(self.h, n)
        return inst


SEM_H = {"ln_dx0", "ln_dx1", "res_dx0", "res_dx1", "res_st0", "res_st1",
         "do_res_dx0", "do_res_dx1", "do_res_st0", "do_res_st1",
         "d_st0", "d_st1", "dmab", "d_in0", "d_in1", "s_h", "s_eg",
         "s_y0", "s_y1", "d_hb", "d_h", "d_cs", "d_tl", "s_S",
         "s_ob0", "s_ob1", "s_qg0", "s_qg1", "dmr", "dmw", "s_o0", "s_o1",
         "s_c0", "s_c1", "dma", "s_v"}
SEM_S = {"dmaw", "dw_up0", "dw_up1", "dmq0", "dmq1", "dw0", "dw1", "dma_s",
         "do_dw0", "do_dw1", "dkv0", "dkv1", "dmq"}
SEM_SHARED_PRE = ("g_", "fence", "dma_c", "coll_")


class LP:

    def __init__(self, name):
        self.name = name
        self.nc = bass.Bass("TRN2", target_bir_lowering=False)
        self.es = contextlib.ExitStack()
        self._sems = {}
        self.mega = False
        self._alias = None
        self._pool_idx = None
        self._io = None
        self._scopes = []
        self._name_counts = {}
        self._cst_cache = None

    def begin_section(self, io):
        self._alias = {}
        self._pool_idx = {"h": 0, "s": 0, "c": 0}
        self._io = io
        self._scopes.append(contextlib.ExitStack())

    def end_section(self):
        self._scopes.pop().close()
        nc = self.nc
        f = self.sem("fence")
        for s in self._sems.values():
            if s is not f and s.v > 0:
                nc.vector.wait_ge(s.h, s.v)
        f.inc(nc.vector.engine_nop())
        for e in [nc.tensor, nc.scalar, nc.gpsimd, nc.sync, nc.vector]:
            e.wait_ge(f.h, f.v)
        self._io = None
        self._alias = None

    def _uniq(self, name):
        n = self._name_counts.get(name, 0)
        self._name_counts[name] = n + 1
        return name if n == 0 else f"{name}_u{n}"

    def sem(self, name):
        if self._alias is not None and not name.startswith(SEM_SHARED_PRE):
            if name not in self._alias:
                t = "h" if name in SEM_H else ("s" if name in SEM_S else "c")
                i = self._pool_idx[t]
                self._pool_idx[t] += 1
                self._alias[name] = f"{t}P{i}"
            name = self._alias[name]
        if name not in self._sems:
            self._sems[name] = Sem(self.nc, self.es, name)
        return self._sems[name]

    def sbuf(self, name, shape, dtype):
        return self.nc.sbuf_tensor(self._uniq(name), shape, dtype)

    def psum(self, name, shape, dtype):
        return self.nc.psum_tensor(self._uniq(name), shape, dtype)

    def sb(self, name, shape, dtype):
        st = self._scopes[-1] if self._scopes else self.es
        return st.enter_context(self.nc.sbuf_tensor(self._uniq(name), shape,
                                                    dtype))

    def g(self, eng, inst):
        nm = "g_" + str(eng.engine)
        s = self.sem(nm)
        s.inc(inst)
        eng.wait_ge(s.h, s.v)
        return inst

    def inp(self, name, shape, dtype=F32):
        if self._io is not None:
            return self._io[name]
        return self.nc.dram_tensor(name, list(shape), dtype, kind="ExternalInput").ap()

    def out(self, name, shape, dtype=F32):
        if self._io is not None:
            return self._io[name]
        return self.nc.dram_tensor(name, list(shape), dtype, kind="ExternalOutput").ap()

    def done(self):
        if self.mega:
            return self.nc
        self.es.close()
        return self.nc


def fm_chunk(hbm, c, T):
    return hbm.rearrange("(c p) t -> c p t", p=128)[c]


def load_fm(p, eng, dst_sb, src_hbm, nch, T, sem, splits=4):
    src = src_hbm.rearrange("(c p) t -> p c t", p=128)
    bounds = [round(i * nch / splits) for i in range(splits + 1)]
    for i in range(splits):
        cs, ce = bounds[i], bounds[i + 1]
        if cs == ce:
            continue
        d = dst_sb[:, cs * T:ce * T].rearrange("p (c t) -> p c t", t=T)
        sem.inc(eng.dma_start(out=d, in_=src[:, cs:ce, :]), 16)


def col_tiles(T, width=512):
    out = []
    t = 0
    while t < T:
        w = min(width, T - t)
        out.append((t, t + w))
        t += w
    return out


def load_consts(p, cst_specs):
    if p._cst_cache is not None:
        return ({key: p._cst_cache[(key, tuple(shape))]
                 for key, shape, dt in cst_specs}, p.sem("dma_c"))
    nc = p.nc
    dma = p.sem("dma_c")
    out = {}
    for key, shape, dt in cst_specs:
        ap = p.inp("c_" + key, shape, dt)
        sb = p.sb("cst_" + key, list(shape), dt)
        dma.inc(nc.sync.dma_start(out=sb[:], in_=ap), 16)
        out[key] = sb
    return out, dma


HOST_CONSTS = {}


def host_const_arrays():
    if HOST_CONSTS:
        return HOST_CONSTS
    HOST_CONSTS["onesb"] = np.ones((128, 1), BF)
    HOST_CONSTS["ones1f"] = np.ones((1, 128), np.float32)
    HOST_CONSTS["identf"] = np.eye(128, dtype=np.float32)
    HOST_CONSTS["identb"] = np.eye(128, dtype=BF)
    return HOST_CONSTS



def emit_ln(p, x_hbm, xn_sb, T, cst, mode="plain", lnw_col=None):
    nc = p.nc
    d_xp = [p.sem("ln_dx0"), p.sem("ln_dx1")]
    v_cast = p.sem("ln_cast")
    a_sq = p.sem("ln_sq")
    t_st = p.sem("ln_stats")
    v_rows = p.sem("ln_rows")
    a_row = p.sem("ln_rowa")
    v_row2 = p.sem("ln_rowv2")
    t_bc = p.sem("ln_bc")
    v_bc = p.sem("ln_bcv")
    v_ap = p.sem("ln_apply")

    ct = col_tiles(T)
    with (
        p.sbuf("ln_xst", [128, 2 * T], F32) as xst,
        p.sbuf("ln_xbt", [128, 2 * T], BF16) as xbt,
        p.sbuf("ln_sq", [128, 2 * T], BF16) as sq,
        p.sbuf("ln_rows", [1, (6 if mode == "mamba" else 5) * T], F32) as rows,
        p.sbuf("ln_mbc", [128, T], F32) as mbc,
        p.sbuf("ln_rbc", [128, T], F32) as rbc,
        p.psum("ln_s1", [1, T], F32) as pS1,
        p.psum("ln_s2", [1, T], F32) as pS2,
    ):
        for c in range(NCH):
            par = c % 2
            if c >= 2:
                nc.sync.wait_ge(v_cast.h, v_cast.v - 1)
                nc.sync.wait_ge(d_xp[par].h, d_xp[par].v)
            d_xp[par].inc(nc.sync.dma_start(out=xst[:, par * T:(par + 1) * T],
                                            in_=fm_chunk(x_hbm, c, T)), 16)
            nc.vector.wait_ge(d_xp[par].h, d_xp[par].v)
            if c >= 2:
                nc.vector.wait_ge(t_st.h, t_st.v - 1)
            v_cast.inc(nc.vector.tensor_copy(xbt[:, par * T:(par + 1) * T],
                                             xst[:, par * T:(par + 1) * T]))
            nc.scalar.wait_ge(v_cast.h, v_cast.v)
            if c >= 2:
                nc.scalar.wait_ge(t_st.h, t_st.v - 1)
            a_sq.inc(nc.scalar.activation(
                sq[:, par * T:(par + 1) * T], xbt[:, par * T:(par + 1) * T],
                AF.Square))
            first, last = (c == 0), (c == NCH - 1)
            nc.tensor.wait_ge(v_cast.h, v_cast.v)
            for (t0, t1) in ct:
                nc.tensor.matmul(pS1[:, t0:t1], cst["onesb"][:, :],
                                 xbt[:, par * T + t0:par * T + t1],
                                 start=first, stop=last, skip_group_check=True)
            nc.tensor.wait_ge(a_sq.h, a_sq.v)
            ins = None
            for (t0, t1) in ct:
                ins = nc.tensor.matmul(pS2[:, t0:t1], cst["onesb"][:, :],
                                       sq[:, par * T + t0:par * T + t1],
                                       start=first, stop=last, skip_group_check=True)
            t_st.inc(ins)

        m_row = rows[:, 0 * T:1 * T]
        e2_row = rows[:, 1 * T:2 * T]
        var_row = rows[:, 2 * T:3 * T]
        r_row = rows[:, 3 * T:4 * T]
        vpe_row = rows[:, 4 * T:5 * T]
        nc.vector.wait_ge(t_st.h, t_st.v)
        p.g(nc.vector, nc.vector.tensor_scalar(m_row, pS1[:, :], 1.0 / D, None, ALU.mult))
        p.g(nc.vector, nc.vector.tensor_scalar(e2_row, pS2[:, :], 1.0 / D, None, ALU.mult))
        p.g(nc.vector, nc.vector.tensor_tensor(var_row, m_row, m_row, ALU.mult))
        p.g(nc.vector, nc.vector.tensor_tensor(var_row, e2_row, var_row, ALU.subtract))
        v_rows.inc(nc.vector.tensor_scalar(vpe_row, var_row, 1e-5, None, ALU.add))
        nc.scalar.wait_ge(v_rows.h, v_rows.v)
        p.g(nc.scalar, nc.scalar.activation(r_row, vpe_row, AF.Ln))
        a_row.inc(nc.scalar.activation(r_row, r_row, AF.Exp, scale=-0.5))
        if mode == "mamba":
            r2 = rows[:, 5 * T:6 * T]
            tmp = vpe_row
            nc.vector.wait_ge(a_row.h, a_row.v)
            p.g(nc.vector, nc.vector.tensor_tensor(tmp, r_row, r_row, ALU.mult))
            p.g(nc.vector, nc.vector.tensor_tensor(tmp, var_row, tmp, ALU.mult))
            v_row2.inc(nc.vector.tensor_scalar(tmp, tmp, 1e-5, None, ALU.add))
            nc.scalar.wait_ge(v_row2.h, v_row2.v)
            p.g(nc.scalar, nc.scalar.activation(r2, tmp, AF.Ln))
            a_row.inc(nc.scalar.activation(r2, r2, AF.Exp, scale=-0.5))
            nc.vector.wait_ge(a_row.h, a_row.v)
            v_row2.inc(nc.vector.tensor_tensor(r_row, r_row, r2, ALU.mult))
            nc.tensor.wait_ge(v_row2.h, v_row2.v)
        else:
            nc.tensor.wait_ge(a_row.h, a_row.v)

        with p.psum("ln_pbc", [128, 512], F32) as pBC, \
             p.psum("ln_pbc2", [128, 512], F32) as pBC2:
            for (t0, t1) in ct:
                w_ = t1 - t0
                if t0 > 0:
                    nc.tensor.wait_ge(v_bc.h, v_bc.v)
                nc.tensor.matmul(pBC[:, 0:w_], cst["ones1f"][:, :], m_row[:, t0:t1],
                                 start=True, stop=True)
                t_bc.inc(nc.tensor.matmul(pBC2[:, 0:w_], cst["ones1f"][:, :],
                                          r_row[:, t0:t1], start=True, stop=True))
                nc.vector.wait_ge(t_bc.h, t_bc.v)
                p.g(nc.vector, nc.vector.tensor_copy(mbc[:, t0:t1], pBC[:, 0:w_]))
                v_bc.inc(nc.vector.tensor_copy(rbc[:, t0:t1], pBC2[:, 0:w_]))

        nc.sync.wait_ge(v_cast.h, v_cast.v)
        nc.gpsimd.wait_ge(v_bc.h, v_bc.v)
        nc.vector.wait_ge(v_bc.h, v_bc.v)
        for c in range(NCH):
            par = c % 2
            nc.sync.wait_ge(d_xp[par].h, d_xp[par].v)
            if c >= 2:
                nc.sync.wait_ge(v_ap.h, v_ap.v - 3)
            ins = nc.sync.dma_start(out=xst[:, par * T:(par + 1) * T],
                                    in_=fm_chunk(x_hbm, c, T))
            if c >= 2:
                ins._wait_ge(v_ap.h, v_ap.v - 3)
            d_xp[par].inc(ins, 16)
            eng = nc.vector
            eng.wait_ge(d_xp[par].h, d_xp[par].v)
            xnc = xn_sb[:, c * T:(c + 1) * T]
            v_ap.inc(eng.tensor_tensor(xnc, xst[:, par * T:(par + 1) * T],
                                       mbc[:, :], ALU.subtract))
            eng.wait_ge(v_ap.h, v_ap.v)
            if mode == "mamba":
                ins = eng.scalar_tensor_tensor(
                    xnc, xnc, lnw_col[:, c:c + 1], rbc[:, :], ALU.mult, ALU.mult)
            else:
                ins = eng.tensor_tensor(xnc, xnc, rbc[:, :], ALU.mult)
            v_ap.inc(ins)
    return v_ap


def emit_residual_stream(p, x_hbm, xout_hbm, T, delta_cb, scale=1.0, n_oc=NCH,
                         ready=None):
    nc = p.nc
    d_rp = [p.sem("res_dx0"), p.sem("res_dx1")]
    v_x = p.sem("res_vx")
    s_outp = [p.sem("res_st0"), p.sem("res_st1")]
    with p.sbuf("res_xst", [128, 2 * T], F32) as xst:
        for oc in range(n_oc):
            par = oc % 2
            if oc >= 2:
                nc.sync.wait_ge(s_outp[par].h, s_outp[par].v)
                nc.sync.wait_ge(d_rp[par].h, d_rp[par].v)
            elif ready is not None:
                nc.sync.wait_ge(ready.h, ready.v)
            d_rp[par].inc(nc.sync.dma_start(out=xst[:, par * T:(par + 1) * T],
                                            in_=fm_chunk(x_hbm, oc, T)), 16)
            psum_ap, t_sem = delta_cb(oc)
            nc.vector.wait_ge(t_sem.h, t_sem.v)
            nc.vector.wait_ge(d_rp[par].h, d_rp[par].v)
            v_x.inc(nc.vector.scalar_tensor_tensor(
                xst[:, par * T:(par + 1) * T], psum_ap, scale,
                xst[:, par * T:(par + 1) * T], ALU.mult, ALU.add))
            ins = nc.sync.dma_start(out=fm_chunk(xout_hbm, oc, T),
                                    in_=xst[:, par * T:(par + 1) * T])
            ins._wait_ge(v_x.h, v_x.v)
            s_outp[par].inc(ins, 16)
        nc.sync.wait_ge(s_outp[0].h, s_outp[0].v)
        nc.sync.wait_ge(s_outp[1].h, s_outp[1].v)
    return v_x



def build_ffn(which, T=TPC, debug=False, p=None):
    R = 16 if which == 0 else 8
    E = 4
    ER = E * R
    SCALE = 4.0 if which == 0 else 1.0
    p = p or LP(f"ffn{which}")
    nc = p.nc
    xfm = p.inp("xfm", (D, T))
    wrt = p.inp("wrt", (D, E))
    wa = p.inp("wa", (D, ER))
    wb = p.inp("wb", (ER, D))
    if which == 1:
        wup = p.inp("wup", (NCH, 128, NCH, 128), BF16)
        ebias = p.inp("ebias", (1, E))
    xout = p.out("xout", (D, T))

    cst, dma_c = load_consts(p, [
        ("onesb", (128, 1), BF16), ("ones1f", (1, 128), F32),
        ("identf", (128, 128), F32),
        ("expe", (E, ER), BF16),
    ])
    dmaw = p.sem("dmaw")

    nT = T // 128
    xn_sb = p.sb("xn_sb", [128, NCH * T], BF16)
    wrt_sb = p.sb("wrt_sb", [128, NCH * E], BF16)
    wa_sb = p.sb("wa_sb", [128, NCH * ER], BF16)
    wb_sb = p.sb("wb_sb", [ER, D], BF16)
    lgT_sb = p.sb("lgT", [128, nT * E], F32)
    g_m1 = p.sb("g_m1", [128, nT], F32)
    g_m2 = p.sb("g_m2", [128, nT], F32)
    g_ex = p.sb("g_ex", [128, nT * E], F32)
    g_kp = p.sb("g_kp", [128, nT * E], F32)
    g_l2 = p.sb("g_l2", [128, nT * E], F32)
    g_z = p.sb("g_z", [128, nT], F32)
    g_eb = p.sb("g_eb", [128, E], F32)
    gT_sb = p.sb("gT", [E, T], BF16)
    lg_row = p.sb("lgrow", [E, T], F32)
    gexp_sb = p.sb("gexp", [ER, T], F32)
    midg_sb = p.sb("midg", [ER, T], BF16)
    if which == 1:
        wup_sb = p.sb("wup_sb", [128, 2 * NCH * 128], BF16)
        ebias_sb = p.sb("ebias_sb", [1, E], F32)
        dma_c.inc(nc.sync.dma_start(out=ebias_sb[:, :], in_=ebias), 16)

    dmaw.inc(nc.gpsimd.dma_start(
        out=wrt_sb[:, :].rearrange("p (c e) -> p c e", e=E),
        in_=wrt.rearrange("(c p) e -> p c e", p=128)), 16)
    dmaw.inc(nc.gpsimd.dma_start(
        out=wa_sb[:, :].rearrange("p (c j) -> p c j", j=ER),
        in_=wa.rearrange("(c p) j -> p c j", p=128)), 16)
    dmaw.inc(nc.gpsimd.dma_start(out=wb_sb[:, :], in_=wb), 16)

    for e in [nc.vector, nc.scalar, nc.tensor, nc.gpsimd, nc.sync]:
        e.wait_ge(dma_c.h, dma_c.v)

    v_ap = emit_ln(p, xfm, xn_sb, T, cst)
    nc.tensor.wait_ge(v_ap.h, v_ap.v)
    nc.tensor.wait_ge(dmaw.h, dmaw.v)

    ct = col_tiles(T)
    t_lg = p.sem("t_lg")
    t_tr = p.sem("t_tr")
    v_g = p.sem("v_g")
    t_gx = p.sem("t_gx")
    t_d = p.sem("t_d")

    with (
        p.psum("pL", [E, T], F32) as pL,
        p.psum("pM", [ER, T], F32) as pM,
    ):
        for c in range(NCH):
            xnc3 = xn_sb[:, c * T:(c + 1) * T]
            first, last = (c == 0), (c == NCH - 1)
            ins = None
            for (t0, t1) in ct:
                nc.tensor.matmul(pL[:, t0:t1], wrt_sb[:, c * E:(c + 1) * E],
                                 xnc3[:, t0:t1], start=first, stop=last,
                                 skip_group_check=True)
                ins = nc.tensor.matmul(pM[:, t0:t1], wa_sb[:, c * ER:(c + 1) * ER],
                                       xnc3[:, t0:t1], start=first, stop=last,
                                       skip_group_check=True)
            if last:
                t_lg.inc(ins)

        nc.vector.wait_ge(t_lg.h, t_lg.v)
        v_lg = p.sem("v_lg")
        v_lg.inc(nc.vector.tensor_copy(lg_row[:, :], pL[:, :]))

        with p.psum("pTr", [128, nT * E], F32) as pTr:
            nc.tensor.wait_ge(v_lg.h, v_lg.v)
            ins = None
            for i in range(nT):
                ins = nc.tensor.transpose(pTr[:, i * E:(i + 1) * E],
                                          lg_row[:, i * 128:(i + 1) * 128],
                                          cst["identf"][0:E, 0:E])
            t_tr.inc(ins)
            nc.vector.wait_ge(t_tr.h, t_tr.v)
            v_tr = p.sem("v_tr")
            v_tr.inc(nc.vector.tensor_copy(lgT_sb[:, :], pTr[:, :]))

        L3 = lgT_sb[:, :].rearrange("p (n e) -> p n e", e=E)
        m1 = g_m1[:, :].rearrange("p (n o) -> p n o", o=1)
        m2 = g_m2[:, :].rearrange("p (n o) -> p n o", o=1)
        ex = g_ex[:, :].rearrange("p (n e) -> p n e", e=E)
        kp = g_kp[:, :].rearrange("p (n e) -> p n e", e=E)
        l2 = g_l2[:, :].rearrange("p (n e) -> p n e", e=E)
        zrow = g_z[:, :].rearrange("p (n o) -> p n o", o=1)
        nc.vector.wait_ge(v_tr.h, v_tr.v)
        if which == 0:
            p.g(nc.vector, nc.vector.reduce_max(m1, L3, axis=AX.X))
            p.g(nc.vector, nc.vector.tensor_tensor(ex, L3,
                m1.broadcast_to([128, nT, E]), ALU.is_equal))
            p.g(nc.vector, nc.vector.scalar_tensor_tensor(l2, ex, -1e30, L3,
                ALU.mult, ALU.add))
            p.g(nc.vector, nc.vector.reduce_max(m2, l2, axis=AX.X))
            nc.vector.tensor_tensor(ex, L3, m1.broadcast_to([128, nT, E]), ALU.subtract)
            va = p.sem("v_gpre")
            va.inc(nc.vector.tensor_tensor(kp, L3, m2.broadcast_to([128, nT, E]),
                                           ALU.is_ge))
            nc.scalar.wait_ge(va.h, va.v)
            ae = p.sem("a_gexp")
            ae.inc(nc.scalar.activation(ex, ex, AF.Exp))
            nc.vector.wait_ge(ae.h, ae.v)
            p.g(nc.vector, nc.vector.tensor_tensor(ex, ex, kp, ALU.mult))
            p.g(nc.vector, nc.vector.reduce_sum(zrow, ex, axis=AX.X))
            p.g(nc.vector, nc.vector.reciprocal(zrow, zrow))
            v_g.inc(nc.vector.tensor_tensor(ex, ex, zrow.broadcast_to([128, nT, E]),
                                            ALU.mult))
        else:
            with p.psum("pEB", [128, E], F32) as pEB:
                t_eb = p.sem("t_eb")
                nc.tensor.wait_ge(p.sem("v_tr").h, p.sem("v_tr").v)
                t_eb.inc(nc.tensor.matmul(pEB[:, :], cst["ones1f"][:, :],
                                          ebias_sb[:, :], start=True, stop=True))
                nc.vector.wait_ge(t_eb.h, t_eb.v)
                p.g(nc.vector, nc.vector.tensor_copy(g_eb[:, :], pEB[:, :]))
            p.g(nc.vector, nc.vector.tensor_tensor(
                L3, L3,
                g_eb[:, :].rearrange("p (n e) -> p n e", n=1).broadcast_to([128, nT, E]),
                ALU.add))
            p.g(nc.vector, nc.vector.reduce_max(m1, L3, axis=AX.X))
            v_g.inc(nc.vector.tensor_tensor(ex, L3, m1.broadcast_to([128, nT, E]),
                                            ALU.is_ge))

        with p.psum("pTg", [E, T], F32) as pTg:
            nc.tensor.wait_ge(v_g.h, v_g.v)
            ins = None
            for i in range(nT):
                ins = nc.tensor.transpose(pTg[:, i * 128:(i + 1) * 128],
                                          ex[:, i, :], cst["identf"][:, :])
            t_g2 = p.sem("t_g2")
            t_g2.inc(ins)
            nc.vector.wait_ge(t_g2.h, t_g2.v)
            v_g2 = p.sem("v_g2")
            v_g2.inc(nc.vector.tensor_copy(gT_sb[:, :], pTg[:, :]))

        with p.psum("pGX", [ER, T], F32) as pGX:
            nc.tensor.wait_ge(v_g2.h, v_g2.v)
            ins = None
            for (t0, t1) in ct:
                ins = nc.tensor.matmul(pGX[:, t0:t1], cst["expe"][:, :],
                                       gT_sb[:, t0:t1], start=True, stop=True)
            t_gx.inc(ins)
            nc.vector.wait_ge(t_gx.h, t_gx.v)
            nc.vector.wait_ge(t_lg.h, t_lg.v)
            p.g(nc.vector, nc.vector.tensor_copy(gexp_sb[:, :], pGX[:, :]))
            v_m2 = p.sem("v_m2")
            v_m2.inc(nc.vector.tensor_tensor(midg_sb[:, :], pM[:, :], gexp_sb[:, :],
                                             ALU.mult))

    v_m2 = p.sem("v_m2")
    nc.tensor.wait_ge(v_m2.h, v_m2.v)
    if which == 1:
        dwp = [p.sem("dw_up0"), p.sem("dw_up1")]
        dwpb0 = [dwp[0].v, dwp[1].v]

        def load_panel(oc, par):
            dst = wup_sb[:, par * NCH * 128:(par + 1) * NCH * 128].rearrange(
                "p (kc m) -> p kc m", m=128)
            dwp[par].inc(nc.gpsimd.dma_start(out=dst, in_=wup[oc]), 16)

        nc.gpsimd.wait_ge(v_ap.h, v_ap.v)
        load_panel(0, 0)
        load_panel(1, 1)

    with (
        p.psum("pD0", [128, T], F32) as pD0,
        p.psum("pD1", [128, T], F32) as pD1,
    ):
        pD = [pD0, pD1]
        v_res = p.sem("res_vx")
        vrb = v_res.v
        dwpb = dwpb0 if which == 1 else None

        def delta_cb(oc):
            par = oc % 2
            if oc >= 2:
                nc.tensor.wait_ge(v_res.h, vrb + oc - 1)
            if which == 1:
                nc.tensor.wait_ge(dwp[oc % 2].h, dwpb[oc % 2] + 16 * (oc // 2 + 1))
            ins = None
            for (t0, t1) in ct:
                ins = nc.tensor.matmul(pD[par][:, t0:t1],
                                       wb_sb[:, oc * 128:(oc + 1) * 128],
                                       midg_sb[:, t0:t1], start=True,
                                       stop=(which == 0), skip_group_check=True)
            if which == 1:
                for kc in range(NCH):
                    w = wup_sb[:, par * NCH * 128 + kc * 128:
                               par * NCH * 128 + (kc + 1) * 128]
                    for (t0, t1) in ct:
                        ins = nc.tensor.matmul(pD[par][:, t0:t1], w,
                                               xn_sb[:, kc * T + t0:kc * T + t1],
                                               start=False, stop=(kc == NCH - 1),
                                               skip_group_check=True)
            t_d.inc(ins)
            if which == 1 and oc + 2 < NCH:
                nc.gpsimd.wait_ge(t_d.h, t_d.v)
                load_panel(oc + 2, par)
            return pD[par][:, :], t_d

        emit_residual_stream(p, xfm, xout, T, delta_cb, scale=SCALE,
                             ready=v_ap)

    if debug:
        dbg = p.sem("dbg")
        d_xn = p.out("d_xn", (D, T), BF16)
        d_lg = p.out("d_lg", (E, T))
        d_g = p.out("d_g", (E, T), BF16)
        d_midg = p.out("d_midg", (ER, T), BF16)
        nc.sync.wait_ge(p.sem("res_st0").h, p.sem("res_st0").v)
        nc.sync.wait_ge(p.sem("res_st1").h, p.sem("res_st1").v)
        dbg.inc(nc.sync.dma_start(out=d_xn.rearrange("(c p) t -> p c t", p=128),
                                  in_=xn_sb[:, :].rearrange("p (c t) -> p c t", t=T)), 16)
        dbg.inc(nc.sync.dma_start(out=d_lg, in_=lg_row[:, :]), 16)
        dbg.inc(nc.sync.dma_start(out=d_g, in_=gT_sb[:, :]), 16)
        dbg.inc(nc.sync.dma_start(out=d_midg, in_=midg_sb[:, :]), 16)
        nc.sync.wait_ge(dbg.h, dbg.v)
    return p.done()



def _bf(x):
    return np.ascontiguousarray(np.asarray(x), dtype=BF)


def _f32(x):
    return np.ascontiguousarray(np.asarray(x), dtype=np.float32)


def blockify(w):
    K, M = w.shape
    b = w.reshape(K // 128, 128, M // 128, 128).transpose(2, 1, 0, 3)
    return np.ascontiguousarray(b)


def ffn_inmaps(which, inputs, xs):
    hc = host_const_arrays()
    if which == 0:
        A = np.asarray(inputs["L0_ffn_A"])
        Bm = np.asarray(inputs["L0_ffn_B"])
        R = 16
        base = {
            "wrt": _f32(np.asarray(inputs["L0_ffn_Wr"]).T),
            "wa": _f32(A.transpose(2, 0, 1).reshape(4096, 4 * R)),
            "wb": _f32(Bm.transpose(0, 2, 1).reshape(4 * R, 4096)),
        }
    else:
        A = np.asarray(inputs["L1_ffn_lora_A"])
        Bm = np.asarray(inputs["L1_ffn_lora_B"])
        R = 8
        base = {
            "wrt": _f32(np.asarray(inputs["L1_ffn_gate_w"]).T),
            "wa": _f32(A.transpose(2, 0, 1).reshape(4096, 4 * R)),
            "wb": _f32(Bm.transpose(0, 2, 1).reshape(4 * R, 4096)),
            "wup": _bf(blockify(np.asarray(inputs["L1_ffn_W_up"]).T)),
            "ebias": _f32(np.asarray(inputs["L1_ffn_expert_bias"]).reshape(1, 4)),
        }
    base["c_expe"] = _bf((np.arange(4)[:, None] == (np.arange(4 * R)[None, :] // R))
                         .astype(np.float32))
    base["c_onesb"] = hc["onesb"]
    base["c_ones1f"] = hc["ones1f"]
    base["c_identf"] = hc["identf"]
    return [dict(base, xfm=xs[c]) for c in range(NCORES)]


def shard_x(x):
    out = []
    for c in range(NCORES):
        b, h = c // 2, c % 2
        out.append(_f32(x[b, h * TPC:(h + 1) * TPC, :].T))
    return out


def unshard_x(xs):
    x = np.empty((4, 2048, 4096), np.float32)
    for c in range(NCORES):
        b, h = c // 2, c % 2
        x[b, h * TPC:(h + 1) * TPC, :] = xs[c].T
    return x


_prog_cache = {}


def get_prog(key, builder, *args):
    if key not in _prog_cache:
        _prog_cache[key] = builder(*args)
    return _prog_cache[key]


def run(nc, in_maps):
    res = run_bass_kernel_spmd(nc, in_maps, core_ids=list(range(NCORES)))
    return res.results



def kernel(**inputs):
    _ = inputs["L0_attn_a_residual"]
    x_full = np.asarray(inputs["L0_attn_a_hidden_states"], dtype=np.float32)
    nc = get_prog("mega", build_mega)
    res = run(nc, mega_inmaps(inputs, x_full))
    return unshard_x([r["xout"] for r in res])


def kernel_unfused(**inputs):
    x_full = np.asarray(inputs["L0_attn_a_hidden_states"], dtype=np.float32)
    xs = shard_x(x_full)
    ra = run(get_prog("mamba_a", build_mamba_a), mamba_inmaps_a(inputs, xs, x_full))
    rb = run(get_prog("mamba_b", build_mamba_b), mamba_inmaps_b(inputs, ra))
    rc = run(get_prog("mamba_c", build_mamba_c), mamba_inmaps_c(inputs, xs, ra, rb))
    xs = [r["xout"] for r in rc]
    res = run(get_prog("ffn0", build_ffn, 0), ffn_inmaps(0, inputs, xs))
    xs = [r["xout"] for r in res]
    res = run(get_prog("attnb0", build_attnb0), attnb0_inmaps(inputs, xs))
    xs = [r["xout"] for r in res]
    res = run(get_prog("ffn0", build_ffn, 0), ffn_inmaps(0, inputs, xs))
    xs = [r["xout"] for r in res]
    ra = run(get_prog("gqa_a", build_gqa_a), gqa_inmaps_a(inputs, xs))
    rb = run(get_prog("gqa_b", build_gqa_b), gqa_inmaps_b(ra))
    rc = run(get_prog("dense_out", build_dense_out), gqa_inmaps_c(inputs, xs, rb))
    xs = [r["xout"] for r in rc]
    res = run(get_prog("ffn1", build_ffn, 1), ffn_inmaps(1, inputs, xs))
    xs = [r["xout"] for r in res]
    ra = run(get_prog("gla_a", build_gla_a), gla_inmaps_a(inputs, xs))
    rb = run(get_prog("gla_b", build_gla_b), gla_inmaps_b(ra))
    rc = run(get_prog("gla_c", build_gla_c), gla_inmaps_c(inputs, xs, rb))
    xs = [r["xout"] for r in rc]
    res = run(get_prog("ffn1", build_ffn, 1), ffn_inmaps(1, inputs, xs))
    xs = [r["xout"] for r in res]
    return unshard_x(xs)



class _MegaDone(Exception):
    pass


def build_mega(T=TPC, upto=99):
    p = LP("mega")
    p.mega = True
    nc = p.nc
    _step = [0]

    def ck():
        _step[0] += 1
        if _step[0] > upto:
            raise _MegaDone()

    def scr(name, shape, dt=F32):
        return nc.dram_tensor(name, list(shape), dt, kind="Internal").ap()

    def ext(name, shape, dt=F32):
        return nc.dram_tensor(name, list(shape), dt, kind="ExternalInput").ap()

    dma_c = p.sem("dma_c")
    p._cst_cache = {}
    for key, shape, dt in [("onesb", (128, 1), BF16), ("ones1f", (1, 128), F32),
                           ("identf", (128, 128), F32),
                           ("identb", (128, 128), BF16),
                           ("expe", (4, 64), BF16), ("expe", (4, 32), BF16),
                           ("ltri", (128, 128), BF16),
                           ("ltrib", (128, 128), BF16)]:
        nm = f"c_{key}_{shape[0]}x{shape[1]}"
        ap = ext(nm, shape, dt)
        sb = p.sb("cst_" + nm, list(shape), dt)
        dma_c.inc(nc.sync.dma_start(out=sb[:], in_=ap), 16)
        p._cst_cache[(key, tuple(shape))] = sb
    psel_ap = ext("psel", (128, 8))
    psel_sb = p.sb("psel_sb", [128, 8], F32)
    dma_c.inc(nc.sync.dma_start(out=psel_sb[:, :], in_=psel_ap), 16)
    for e in [nc.vector, nc.scalar, nc.tensor, nc.gpsimd, nc.sync]:
        e.wait_ge(dma_c.h, dma_c.v)

    x1 = scr("x1", (D, T)); x2 = scr("x2", (D, T)); x3 = scr("x3", (D, T))
    x4 = scr("x4", (D, T)); x5 = scr("x5", (D, T)); x6 = scr("x6", (D, T))
    x7 = scr("x7", (D, T))
    zs = scr("zs", (D, T), BF16)
    xbcs = scr("xbcs", (34 * 128, T), BF16)
    dts = scr("dts", (64, T), BF16)
    dtas = scr("dtas", (64, T), BF16)
    ys = scr("ys", (T, D), BF16)
    hout = scr("hout", (128, D), F32)
    elg = scr("elg", (T, 64), F32)
    hout_all = scr("hout_all", (8 * 128, D), F32)
    hin_s = scr("hin_s", (128, D), F32)
    gxq = scr("gxq", (D, T), BF16)
    gxk = scr("gxk", (1024, T), BF16)
    gxv = scr("gxv", (T, 1024), BF16)
    kprep = scr("kprep", (2, 8, 128, T), BF16)
    vprep = scr("vprep", (2, 8, 128, T), BF16)
    kex = scr("kex", (2, 8, 128, T), BF16)
    vex = scr("vex", (2, 8, 128, T), BF16)
    xctx = scr("xctx", (D, T), BF16)
    lxq = scr("lxq", (D, T), BF16)
    lxk = scr("lxk", (1024, T), BF16)
    lxv = scr("lxv", (1024, T), BF16)
    ktp = scr("ktp", (1024, 64), BF16)
    vtp = scr("vtp", (1024, 64), BF16)
    ktail_all = scr("ktail_all", (8 * 1024, 64), BF16)
    vtail_all = scr("vtail_all", (8 * 1024, 64), BF16)
    ktail_s = scr("ktail_s", (1024, 128), BF16)
    vtail_s = scr("vtail_s", (1024, 128), BF16)
    obase = scr("obase", (T, D), BF16)
    qgs = scr("qgs", (D, T), BF16)
    sloc = scr("sloc", (1024, 128), F32)
    sloc_all = scr("sloc_all", (8 * 1024, 128), F32)
    sin_s = scr("sin_s", (1024, 128), F32)
    xout = nc.dram_tensor("xout", [D, T], F32, kind="ExternalOutput").ap()

    coll_r = p.sem("coll_r")

    def coll(out_ap, in_ap, kind="AllGather"):
        ctx = (aot_env.allow_ici_all_to_all() if kind == "AllToAll"
               else contextlib.nullcontext())
        with ctx, nc.named_scope("coll"):
            coll_r.inc(nc.gpsimd.collective_compute(
                kind, ALU.bypass, replica_groups=[list(range(8))],
                ins=[in_ap], outs=[out_ap]))

    _secn = [0]

    def sec(fn, io, *args, **kw):
        tag = f"s{_secn[0]:02d}_{getattr(fn, '__name__', 'f')}"
        _secn[0] += 1
        with nc.named_scope(tag):
            p.begin_section(io)
            fn(*args, p=p, **kw)
            p.end_section()

    def emit_select8(tag, slot_ap_fn, dst_view, ncols, dt, gdim=None,
                     zero_view=None):
        p.begin_section({})
        _sid, _ = nc.enter_named_scope("sel_" + tag, False)
        d_s = [p.sem("d_in0"), p.sem("d_in1")]
        v_s = p.sem("v_e")
        s_st = p.sem("s_o0")
        nc.sync.wait_ge(coll_r.h, coll_r.v)

        def v3(sb2d):
            return (sb2d if gdim is None
                    else sb2d.rearrange("p (g v) -> p g v", v=gdim))

        with p.sbuf(f"{tag}_acc", [128, ncols], F32) as acc, \
             p.sbuf(f"{tag}_acco", [128, ncols], dt) as acco, \
             p.sbuf(f"{tag}_sl", [128, 2 * ncols], dt) as sl:
            p.g(nc.vector, nc.vector.memset(acc[:, :], 0.0))
            for r in range(8):
                par = r % 2
                if r >= 2:
                    nc.sync.wait_ge(v_s.h, v_s.v - 1)
                d_s[par].inc(nc.sync.dma_start(
                    out=v3(sl[:, par * ncols:(par + 1) * ncols]),
                    in_=slot_ap_fn(r)), 16)
                nc.vector.wait_ge(d_s[par].h, d_s[par].v)
                v_s.inc(nc.vector.scalar_tensor_tensor(
                    acc[:, :], sl[:, par * ncols:(par + 1) * ncols],
                    psel_sb[:, r:r + 1], acc[:, :], ALU.mult, ALU.add))
                nc.vector.wait_ge(v_s.h, v_s.v)
            v_s.inc(nc.vector.tensor_copy(acco[:, :], acc[:, :]))
            ins = nc.sync.dma_start(out=dst_view, in_=v3(acco[:, :]))
            ins._wait_ge(v_s.h, v_s.v)
            s_st.inc(ins, 16)
            if zero_view is not None:
                p.g(nc.vector, nc.vector.memset(sl[:, 0:ncols], 0.0))
                v_s.inc(nc.vector.engine_nop())
                ins = nc.sync.dma_start(out=zero_view, in_=v3(sl[:, 0:ncols]))
                ins._wait_ge(v_s.h, v_s.v)
                s_st.inc(ins, 16)
            nc.sync.wait_ge(s_st.h, s_st.v)
        nc.leave_named_scope("sel_" + tag, _sid, False)
        p.end_section()

    mie = mega_static_ext()
    exts = {}

    def eio(tag, keys):
        io = {}
        for k in keys:
            nm = f"{tag}_{k}"
            if nm not in exts:
                shape, dt = mie[nm]
                exts[nm] = ext(nm, shape, dt)
            io[k] = exts[nm]
        return io

    x0s = ext("x0s", (D, T), F32)

    try:
        _build_mega_body(p, nc, T, ck, sec, coll, emit_select8, eio, exts, ext,
                         mie, locals())
    except _MegaDone:
        pass
    p.mega = False
    return p.done()


def _build_mega_body(p, nc, T, ck, sec, coll, emit_select8, eio, exts, ext,
                     mie, L):
    (x0s, x1, x2, x3, x4, x5, x6, x7, zs, xbcs, dts, dtas, ys, hout, elg,
     hout_all, hin_s, gxq, gxk, gxv, kprep, vprep, kex, vex, xctx, lxq, lxk,
     lxv, ktp, vtp, ktail_all, vtail_all, ktail_s, vtail_s, obase, qgs, sloc,
     sloc_all, sin_s, xout) = (
        L["x0s"], L["x1"], L["x2"], L["x3"], L["x4"], L["x5"], L["x6"],
        L["x7"], L["zs"], L["xbcs"], L["dts"], L["dtas"], L["ys"], L["hout"],
        L["elg"], L["hout_all"], L["hin_s"], L["gxq"], L["gxk"], L["gxv"],
        L["kprep"], L["vprep"], L["kex"], L["vex"], L["xctx"], L["lxq"],
        L["lxk"], L["lxv"], L["ktp"], L["vtp"], L["ktail_all"],
        L["vtail_all"], L["ktail_s"], L["vtail_s"], L["obase"], L["qgs"],
        L["sloc"], L["sloc_all"], L["sin_s"], L["xout"])

    io = eio("ma", ["xfm", "wproj", "mupt", "lnw", "wconv", "dtb", "alog"])
    io.update(zs=zs, xbcs=xbcs, dts=dts, dtas=dtas)
    sec(build_mamba_a, io)
    ck()
    io = eio("mb", ["drow"])
    io.update(xbcs=xbcs, dts=dts, dtas=dtas, ys=ys, hout=hout, elg=elg)
    sec(build_mamba_b, io)
    ck()
    coll(hout_all, hout[:, :])
    ck()
    emit_select8("hsel", lambda r: hout_all[r * 128:(r + 1) * 128, :],
                 hin_s[:, :], D, F32)
    ck()
    io = eio("mc", ["nwrow", "wout"])
    io.update(xfm=x0s, ys=ys, elg=elg, hin=hin_s, zsp=zs, xbcs=xbcs, xout=x1)
    sec(build_mamba_c, io)
    ck()
    io0 = eio("f0", ["wrt", "wa", "wb"])
    sec(build_ffn, dict(io0, xfm=x1, xout=x2), 0)
    ck()
    io = eio("ab", ["wq32", "bq", "wk", "bk", "wv", "bv", "bige"])
    io.update(xfm=x2, xout=x3)
    sec(build_attnb0, io)
    ck()
    sec(build_ffn, dict(io0, xfm=x3, xout=x4), 0)
    ck()
    io = eio("ga", ["wq", "wkv"])
    io.update(xfm=x4, xq=gxq, xk=gxk, xv=gxv, kprep=kprep, vprep=vprep)
    sec(build_gqa_a, io)
    ck()
    for j in range(2):
        coll(kex[j].rearrange("s f t -> (s f) t"),
             kprep[j].rearrange("s f t -> (s f) t"), kind="AllToAll")
        coll(vex[j].rearrange("s q t -> (s q) t"),
             vprep[j].rearrange("s q t -> (s q) t"), kind="AllToAll")
    ck()
    for nmm in ("c_onesbk", "c_ones1fk"):
        shape, dt = mie[nmm]
        exts[nmm] = ext(nmm, shape, dt)
    io = eio("gb", ["vmask"])
    io.update(xq=gxq, kex=kex, vex=vex, xctx=xctx,
              c_onesbk=exts["c_onesbk"], c_ones1fk=exts["c_ones1fk"])
    sec(build_gqa_b, io)
    ck()
    io = eio("gc", ["w", "bias"])
    io.update(xfm=x4, xin=xctx, xout=x5)
    sec(build_dense_out, io)
    ck()
    io1 = eio("f1", ["wrt", "wa", "wb", "wup", "ebias"])
    sec(build_ffn, dict(io1, xfm=x5, xout=x6), 1)
    ck()
    io = eio("la", ["wq", "wk", "wv"])
    io.update(xfm=x6, xq=lxq, xk=lxk, xv=lxv, ktp=ktp, vtp=vtp)
    sec(build_gla_a, io)
    ck()
    coll(ktail_all, ktp[:, :])
    coll(vtail_all, vtp[:, :])
    emit_select8(
        "ktsel", lambda r: ktail_all[r * 1024:(r + 1) * 1024]
        .rearrange("(g pp) v -> pp g v", pp=128),
        ktail_s.rearrange("(g pp) v -> pp g v", pp=128)[:, :, 64:128],
        512, BF16, gdim=64,
        zero_view=ktail_s.rearrange("(g pp) v -> pp g v", pp=128)[:, :, 0:64])
    emit_select8(
        "vtsel", lambda r: vtail_all[r * 1024:(r + 1) * 1024]
        .rearrange("(g pp) v -> pp g v", pp=128),
        vtail_s.rearrange("(g pp) v -> pp g v", pp=128)[:, :, 64:128],
        512, BF16, gdim=64,
        zero_view=vtail_s.rearrange("(g pp) v -> pp g v", pp=128)[:, :, 0:64])
    ck()
    io = eio("lb", ["qcos", "qsin", "kcos", "ksin", "swm"])
    io.update(xq=lxq, xk=lxk, xv=lxv, ktail=ktail_s, vtail=vtail_s,
              obase=obase, qgs=qgs, sloc=sloc)
    sec(build_gla_b, io)
    ck()
    coll(sloc_all, sloc[:, :])
    emit_select8(
        "ssel", lambda r: sloc_all[r * 1024:(r + 1) * 1024]
        .rearrange("(g pp) v -> pp g v", pp=128),
        sin_s.rearrange("(g pp) v -> pp g v", pp=128), 1024, F32, gdim=128)
    ck()
    io = eio("lc", ["wo"])
    io.update(xfm=x6, ob=obase, qgs=qgs, sin=sin_s, xout=x7)
    sec(build_gla_c, io)
    ck()
    sec(build_ffn, dict(io1, xfm=x7, xout=xout), 1)


def mega_static_ext():
    E0, ER0, E1, ER1 = 4, 64, 4, 32
    return {
        "ma_xfm": ((D, TPC + 4), F32), "ma_wproj": ((67, 128, NCH, 128), BF16),
        "ma_mupt": ((128, 67), F32), "ma_lnw": ((128, NCH), F32),
        "ma_wconv": ((128, 34 * 4), F32), "ma_dtb": ((128, 1), F32),
        "ma_alog": ((128, 1), F32),
        "mb_drow": ((1, 64), F32),
        "mc_nwrow": ((1, D), F32), "mc_wout": ((NCH, 128, NCH, 128), BF16),
        "f0_wrt": ((D, E0), F32), "f0_wa": ((D, ER0), F32),
        "f0_wb": ((ER0, D), F32),
        "ab_wq32": ((32, 128, NCH, 128), BF16), "ab_bq": ((128, 32), F32),
        "ab_wk": ((D, 128), BF16), "ab_bk": ((128, 1), F32),
        "ab_wv": ((D, 128), BF16), "ab_bv": ((128, 1), F32),
        "ab_bige": ((128, D), BF16),
        "ga_wq": ((NCH, 128, NCH, 128), BF16),
        "ga_wkv": ((16, 128, NCH, 128), BF16),
        "gb_vmask": ((2 * TPC, TPC), BF16),
        "gc_w": ((NCH, 128, NCH, 128), BF16), "gc_bias": ((D, 1), F32),
        "f1_wrt": ((D, E1), F32), "f1_wa": ((D, ER1), F32),
        "f1_wb": ((ER1, D), F32), "f1_wup": ((NCH, 128, NCH, 128), BF16),
        "f1_ebias": ((1, E1), F32),
        "la_wq": ((NCH, 128, NCH, 128), BF16), "la_wk": ((8, 128, NCH, 128), BF16),
        "la_wv": ((8, 128, NCH, 128), BF16),
        "lb_qcos": ((TPC, 128), F32), "lb_qsin": ((TPC, 128), F32),
        "lb_kcos": ((TPC + 64, 128), F32), "lb_ksin": ((TPC + 64, 128), F32),
        "lb_swm": ((3, 128, 64), F32),
        "lc_wo": ((NCH, 128, NCH, 128), BF16),
        "c_onesbk": ((128, 1), BF16), "c_ones1fk": ((1, 128), F32),
    }


def mega_inmaps(inputs, x_full):
    xs = shard_x(x_full)
    hc = host_const_arrays()
    ma = mamba_inmaps_a(inputs, xs, x_full)
    mc = mamba_inmaps_c(inputs, xs,
                        [dict(zs=None, xbcs=None)] * NCORES,
                        [dict(ys=None, elg=None, hout=np.zeros((128, D),
                                                              np.float32))]
                        * NCORES)
    f0 = ffn_inmaps(0, inputs, xs)
    f1 = ffn_inmaps(1, inputs, xs)
    ab = attnb0_inmaps(inputs, xs)
    ga = gqa_inmaps_a(inputs, xs)
    gc = gqa_inmaps_c(inputs, xs, [dict(xctx=None)] * NCORES)
    la = gla_inmaps_a(inputs, xs)
    rt = _rope_tables(TPC)
    consts = {
        "c_onesb_128x1": hc["onesb"], "c_ones1f_1x128": hc["ones1f"],
        "c_identf_128x128": hc["identf"], "c_identb_128x128": hc["identb"],
        "c_expe_4x64": _bf((np.arange(4)[:, None] ==
                            (np.arange(64)[None, :] // 16)).astype(np.float32)),
        "c_expe_4x32": _bf((np.arange(4)[:, None] ==
                            (np.arange(32)[None, :] // 8)).astype(np.float32)),
        "c_ltri_128x128": _bf(np.triu(np.ones((128, 128), np.float32))),
        "c_ltrib_128x128": _bf(np.triu(np.ones((128, 128), np.float32))),
        "c_onesbk": hc["onesb"], "c_ones1fk": hc["ones1f"],
        "mb_drow": _f32(np.asarray(inputs["L0_attn_a_D"]).reshape(1, 64)),
    }
    maps = []
    for c in range(NCORES):
        b, h = c // 2, c % 2
        m = dict(consts)
        m["x0s"] = _f32(xs[c])
        for k in ["xfm", "wproj", "mupt", "lnw", "wconv", "dtb", "alog"]:
            m[f"ma_{k}"] = ma[c][k]
        for k in ["nwrow", "wout"]:
            m[f"mc_{k}"] = mc[c][k]
        for k in ["wrt", "wa", "wb"]:
            m[f"f0_{k}"] = f0[c][k]
        for k in ["wrt", "wa", "wb", "wup", "ebias"]:
            m[f"f1_{k}"] = f1[c][k]
        for k in ["wq32", "bq", "wk", "bk", "wv", "bv", "bige"]:
            m[f"ab_{k}"] = ab[c][k]
        for k in ["wq", "wkv"]:
            m[f"ga_{k}"] = ga[c][k]
        for k in ["w", "bias"]:
            m[f"gc_{k}"] = gc[c][k]
        for k in ["wq", "wk", "wv"]:
            m[f"la_{k}"] = la[c][k]
        m["lc_wo"] = _bf(blockify(np.asarray(inputs["L1_attn_b_Wo"])))
        kpos = np.arange(2 * TPC)[:, None]
        qpos = h * TPC + np.arange(TPC)[None, :]
        m["gb_vmask"] = _bf((kpos <= qpos).astype(np.float32))
        qc, qs_, kc, ks_ = rt[h]
        m["lb_qcos"], m["lb_qsin"] = qc, qs_
        m["lb_kcos"], m["lb_ksin"] = kc, ks_
        m["lb_swm"] = _swa_masks(h == 1)
        psel = np.zeros((128, 8), np.float32)
        if h == 1:
            psel[:, c - 1] = 1.0
        m["psel"] = psel
        maps.append(m)
    return maps



SC128 = 1.0 / float(np.sqrt(128.0))


def build_attnb0(T=TPC, debug=False, skip_attn=0, p=None):
    p = p or LP("attnb0")
    nc = p.nc
    xfm = p.inp("xfm", (D, T))
    wq32 = p.inp("wq32", (32, 128, NCH, 128), BF16)
    bq = p.inp("bq", (128, 32))
    wk = p.inp("wk", (D, 128), BF16)
    bk = p.inp("bk", (128, 1))
    wv = p.inp("wv", (D, 128), BF16)
    bv = p.inp("bv", (128, 1))
    bige = p.inp("bige", (128, D), BF16)
    xout = p.out("xout", (D, T))

    cst, dma_c = load_consts(p, [
        ("onesb", (128, 1), BF16), ("ones1f", (1, 128), F32),
        ("identb", (128, 128), BF16),
    ])
    dmaw = p.sem("dmaw")
    dmq = p.sem("dmq")

    nt = T // 128
    xn_sb = p.sb("xn_sb", [128, NCH * T], BF16)
    wqsb = p.sb("wqsb", [128, NCH * 128], BF16)
    wk_sb = p.sb("wk_sb", [128, NCH * 128], BF16)
    wv_sb = p.sb("wv_sb", [128, NCH * 128], BF16)
    bige_sb = p.sb("bige_sb", [128, D], BF16)
    brows = p.sb("brows", [128, 35], F32)
    qsT_sb = p.sb("qsT_sb", [128, T], BF16)
    kT_sb = p.sb("kT_sb", [128, T], BF16)
    v2_sb = p.sb("v2_sb", [128, 2 * T], BF16)
    q_tok = p.sb("q_tok", [128, T], BF16)
    k_tok = p.sb("k_tok", [128, T], BF16)
    q_st = p.sb("q_st", [128, 8192], BF16)
    k_st = p.sb("k_st", [128, 8192], BF16)
    e4_sb = p.sb("e4_sb", [128, 2 * 512], BF16)
    rd_sb = p.sb("rd_sb", [128, 128], F32)
    oT_sb = p.sb("oT_sb", [128, T], BF16)

    with p.sbuf("wqs_f32", [128, NCH * 128], F32) as wqs, \
         p.sbuf("wq_pan", [128, 2 * NCH * 128], BF16) as wqp:
        dmqs = [p.sem(f"dmq{s_}") for s_ in range(2)]
        v_acc = p.sem("v_wqacc")
        NW = NCH * 128
        for h in range(32):
            par_h = h % 2
            sem = dmqs[par_h]
            if h >= 2:
                nc.gpsimd.wait_ge(v_acc.h, v_acc.v - 1)
            sem.inc(nc.gpsimd.dma_start(
                out=wqp[:, par_h * NW:(par_h + 1) * NW], in_=wq32[h]), 16)
            nc.vector.wait_ge(sem.h, sem.v)
            if h == 0:
                v_acc.inc(nc.vector.tensor_copy(wqs[:, :], wqp[:, 0:NW]))
            else:
                v_acc.inc(nc.vector.tensor_tensor(
                    wqs[:, :], wqs[:, :], wqp[:, par_h * NW:(par_h + 1) * NW],
                    ALU.add))
            nc.vector.wait_ge(v_acc.h, v_acc.v)
        for sem in dmqs:
            dmq.inc(nc.gpsimd.dma_start(out=wk_sb[0:1, 0:1], in_=wk[0:1, 0:1]), 16)             if False else None
        dmaw.inc(nc.gpsimd.dma_start(
            out=wk_sb[:, :].rearrange("p (c d) -> p c d", d=128),
            in_=wk.rearrange("(c p) d -> p c d", p=128)), 16)
        dmaw.inc(nc.gpsimd.dma_start(
            out=wv_sb[:, :].rearrange("p (c d) -> p c d", d=128),
            in_=wv.rearrange("(c p) d -> p c d", p=128)), 16)
        dmaw.inc(nc.gpsimd.dma_start(out=bige_sb[:, :], in_=bige), 16)
        dmab = p.sem("dmab")
        dmab.inc(nc.sync.dma_start(out=brows[:, 0:32], in_=bq), 16)
        dmab.inc(nc.sync.dma_start(out=brows[:, 33:34], in_=bk), 16)
        dmab.inc(nc.sync.dma_start(out=brows[:, 34:35], in_=bv), 16)

        for e in [nc.vector, nc.scalar, nc.tensor, nc.gpsimd, nc.sync]:
            e.wait_ge(dma_c.h, dma_c.v)

        v_w = p.sem("v_w")
        nc.vector.wait_ge(dmaw.h, dmaw.v)
        nc.vector.wait_ge(p.sem("dmab").h, p.sem("dmab").v)
        p.g(nc.vector, nc.vector.reduce_sum(brows[:, 32:33], brows[:, 0:32], axis=AX.X))
        v_w.inc(nc.vector.tensor_copy(wqsb[:, :], wqs[:, :]))
        nc.vector.wait_ge(v_w.h, v_w.v)

    for e in [nc.sync, nc.scalar, nc.tensor, nc.gpsimd]:
        e.wait_ge(p.sem("v_w").h, p.sem("v_w").v)
    v_ap = emit_ln(p, xfm, xn_sb, T, cst)

    bqs_col = brows[:, 32:33]
    bk_col = brows[:, 33:34]
    bv_col = brows[:, 34:35]
    ct = col_tiles(T)
    v_w = p.sem("v_w")
    t_q = p.sem("t_q")
    v_qe = p.sem("v_qe")
    t_tok = p.sem("t_tok")
    v_tok = p.sem("v_tok")
    d_stp = [p.sem("d_st0"), p.sem("d_st1")]
    t_op = p.sem("t_op")
    topb = t_op.v
    a_exp = p.sem("a_exp")
    t_nd = p.sem("t_nd")
    v_o = p.sem("v_o")
    t_d = p.sem("t_d")

    v2v = v2_sb[:, :].rearrange("p (t two) -> p t two", two=2)
    vm = p.sem("v_memset")
    vm.inc(nc.vector.memset(v2v[:, :, 1], 1.0))

    nc.tensor.wait_ge(v_ap.h, v_ap.v)
    nc.tensor.wait_ge(v_w.h, v_w.v)
    nc.tensor.wait_ge(vm.h, vm.v)

    with (
        p.psum("pQ", [128, T], F32) as pQ,
        p.psum("pK", [128, T], F32) as pK,
        p.psum("pV", [128, T], F32) as pV,
    ):
        for c in range(NCH):
            first, last = (c == 0), (c == NCH - 1)
            xnc = xn_sb[:, c * T:(c + 1) * T]
            ins = None
            for (t0, t1) in ct:
                nc.tensor.matmul(pQ[:, t0:t1], wqsb[:, c * 128:(c + 1) * 128],
                                 xnc[:, t0:t1], start=first, stop=last,
                                 skip_group_check=True)
                nc.tensor.matmul(pK[:, t0:t1], wk_sb[:, c * 128:(c + 1) * 128],
                                 xnc[:, t0:t1], start=first, stop=last,
                                 skip_group_check=True)
                ins = nc.tensor.matmul(pV[:, t0:t1], wv_sb[:, c * 128:(c + 1) * 128],
                                       xnc[:, t0:t1], start=first, stop=last,
                                       skip_group_check=True)
            if last:
                t_q.inc(ins)
        nc.vector.wait_ge(t_q.h, t_q.v)
        nc.vector.tensor_scalar(qsT_sb[:, :], pQ[:, :], bqs_col, SC128,
                                ALU.add, ALU.mult)
        nc.vector.tensor_scalar(kT_sb[:, :], pK[:, :], bk_col, None, ALU.add)
        v_qe.inc(nc.vector.tensor_scalar(v2v[:, :, 0], pV[:, :], bv_col, None,
                                         ALU.add))

    with p.psum("pT1a", [128, 128], BF16) as pT1a, \
         p.psum("pT1b", [128, 128], BF16) as pT1b:
        pT1 = [pT1a, pT1b]
        nc.tensor.wait_ge(v_qe.h, v_qe.v)
        for i in range(nt):
            par = i % 2
            if i >= 2:
                nc.tensor.wait_ge(v_tok.h, v_tok.v - 1)
            t_tok.inc(nc.tensor.transpose(pT1[par][:, :],
                                          qsT_sb[:, i * 128:(i + 1) * 128],
                                          cst["identb"][:, :]))
            nc.vector.wait_ge(t_tok.h, t_tok.v)
            v_tok.inc(nc.vector.tensor_copy(q_tok[:, i * 128:(i + 1) * 128],
                                            pT1[par][:, :]))
    with p.psum("pT2a", [128, 128], BF16) as pT2a, \
         p.psum("pT2b", [128, 128], BF16) as pT2b:
        pT2 = [pT2a, pT2b]
        nc.tensor.wait_ge(v_tok.h, v_tok.v)
        for i in range(nt):
            par = i % 2
            if i >= 2:
                nc.tensor.wait_ge(v_tok.h, v_tok.v - 1)
            t_tok.inc(nc.tensor.transpose(pT2[par][:, :],
                                          kT_sb[:, i * 128:(i + 1) * 128],
                                          cst["identb"][:, :]))
            nc.vector.wait_ge(t_tok.h, t_tok.v)
            v_tok.inc(nc.vector.tensor_copy(k_tok[:, i * 128:(i + 1) * 128],
                                            pT2[par][:, :]))

    with (
        p.psum("pOP0", [128, 512], F32) as pOP0,
        p.psum("pOP1", [128, 512], F32) as pOP1,
        p.psum("pND0", [128, 256], F32) as pND0,
        p.psum("pND1", [128, 256], F32) as pND1,
    ):
        pOP = [pOP0, pOP1]
        pND = [pND0, pND1]
        nc.tensor.wait_ge(v_tok.h, v_tok.v)
        nc.sync.wait_ge(v_tok.h, v_tok.v)
        if skip_attn & 1:
            vz = p.sem("v_zero")
            vz.inc(nc.vector.memset(oT_sb[:, :], 0.001))
            nc.tensor.wait_ge(vz.h, vz.v)
        for i in range(0 if (skip_attn & 1) else nt):
            par = i % 2
            for half in range(2):
                hh = 2 * i + half
                if hh >= 1:
                    nc.sync.wait_ge(t_op.h, topb + 16 * hh)
                    nc.sync.wait_ge(d_stp[0].h, d_stp[0].v)
                d_stp[0].inc(nc.sync.dma_start(
                    out=q_st[0:1, 0:8192].rearrange("p (g d) -> p g d", d=128),
                    in_=q_tok[64 * half:64 * half + 64, i * 128:(i + 1) * 128]), 16)
                d_stp[0].inc(nc.sync.dma_start(
                    out=k_st[0:1, 0:8192].rearrange("p (g d) -> p g d", d=128),
                    in_=k_tok[64 * half:64 * half + 64, i * 128:(i + 1) * 128]), 16)
                nc.tensor.wait_ge(d_stp[0].h, d_stp[0].v)
                for g in range(16):
                    gg = 16 * hh + g
                    if gg >= 2:
                        nc.tensor.wait_ge(a_exp.h, a_exp.v - 1)
                    ins = None
                    for u in range(4):
                        g2 = 4 * g + u
                        ins = nc.tensor.matmul(
                            pOP[gg % 2][:, u * 128:(u + 1) * 128],
                            k_st[0:1, g2 * 128:(g2 + 1) * 128],
                            q_st[0:1, g2 * 128:(g2 + 1) * 128],
                            start=True, stop=True, skip_group_check=True)
                    t_op.inc(ins)
                    nc.scalar.wait_ge(t_op.h, t_op.v)
                    a_exp.inc(nc.scalar.activation(
                        e4_sb[:, (gg % 2) * 512:(gg % 2 + 1) * 512],
                        pOP[gg % 2][:, :], AF.Exp))
                    nc.tensor.wait_ge(a_exp.h, a_exp.v)
                    ins = None
                    for u in range(4):
                        tloc = 64 * half + 4 * g + u
                        ins = nc.tensor.matmul(
                            pND[par][:, 2 * tloc:2 * tloc + 2],
                            e4_sb[:, (gg % 2) * 512 + u * 128:
                                  (gg % 2) * 512 + (u + 1) * 128],
                            v2_sb[:, 2 * (i * 128 + tloc):2 * (i * 128 + tloc) + 2],
                            start=True, stop=True, skip_group_check=True)
                    t_nd.inc(ins)
            nc.vector.wait_ge(t_nd.h, t_nd.v)
            pv = pND[par][:, :].rearrange("p (t two) -> p t two", two=2)
            p.g(nc.vector, nc.vector.reciprocal(rd_sb[:, :], pv[:, :, 1]))
            v_o.inc(nc.vector.tensor_tensor(oT_sb[:, i * 128:(i + 1) * 128],
                                            pv[:, :, 0], rd_sb[:, :], ALU.mult))
            nc.tensor.wait_ge(v_o.h, v_o.v)

    with (
        p.psum("pA0", [128, T], F32) as pA0,
        p.psum("pA1", [128, T], F32) as pA1,
    ):
        pA = [pA0, pA1]
        v_res = p.sem("res_vx")
        vrb996 = v_res.v

        def delta_cb(oc):
            par = oc % 2
            if oc >= 2:
                nc.tensor.wait_ge(v_res.h, vrb996 + oc - 1)
            ins = None
            for (t0, t1) in ct:
                ins = nc.tensor.matmul(pA[par][:, t0:t1],
                                       bige_sb[:, oc * 128:(oc + 1) * 128],
                                       oT_sb[:, t0:t1], start=True, stop=True,
                                       skip_group_check=True)
            t_d.inc(ins)
            return pA[par][:, :], t_d

        emit_residual_stream(p, xfm, xout, T, delta_cb, scale=1.0,
                             ready=v_ap)

    if debug:
        dbg = p.sem("dbg")
        d_qs = p.out("d_qs", (128, T), BF16)
        d_k = p.out("d_k", (128, T), BF16)
        d_o = p.out("d_o", (128, T), BF16)
        nc.sync.wait_ge(p.sem("res_st0").h, p.sem("res_st0").v)
        nc.sync.wait_ge(p.sem("res_st1").h, p.sem("res_st1").v)
        dbg.inc(nc.sync.dma_start(out=d_qs, in_=qsT_sb[:, :]), 16)
        dbg.inc(nc.sync.dma_start(out=d_k, in_=kT_sb[:, :]), 16)
        dbg.inc(nc.sync.dma_start(out=d_o, in_=oT_sb[:, :]), 16)
        nc.sync.wait_ge(dbg.h, dbg.v)
    return p.done()


def attnb0_inmaps(inputs, xs):
    hc = host_const_arrays()
    Wq = np.asarray(inputs["L0_attn_b_Wq"])
    d_idx = np.arange(128)
    m_idx = np.arange(4096)
    bige = (d_idx[:, None] == (m_idx[None, :] // 128) * 4 + (m_idx[None, :] % 128) // 32)
    base = {
        "wq32": _bf(Wq.reshape(32, 128, 128, 32).transpose(3, 1, 0, 2)),
        "bq": _f32(inputs["L0_attn_b_bq"]),
        "wk": _bf(np.asarray(inputs["L0_attn_b_Wk"]).reshape(4096, 128)),
        "bk": _f32(np.asarray(inputs["L0_attn_b_bk"]).reshape(128, 1)),
        "wv": _bf(np.asarray(inputs["L0_attn_b_Wv"]).reshape(4096, 128)),
        "bv": _f32(np.asarray(inputs["L0_attn_b_bv"]).reshape(128, 1)),
        "bige": _bf(bige.astype(np.float32)),
        "c_onesb": hc["onesb"], "c_ones1f": hc["ones1f"], "c_identb": hc["identb"],
    }
    return [dict(base, xfm=xs[c]) for c in range(NCORES)]



def build_gqa_a(T=TPC, p=None):
    p = p or LP("gqa_a")
    nc = p.nc
    xfm = p.inp("xfm", (D, T))
    wq = p.inp("wq", (NCH, 128, NCH, 128), BF16)
    wkv = p.inp("wkv", (16, 128, NCH, 128), BF16)
    xq = p.out("xq", (D, T), BF16)
    xk = p.out("xk", (8 * 128, T), BF16)
    xv = p.out("xv", (T, 8 * 128), BF16)
    kprep = p._io.get("kprep") if p._io is not None else None
    vprep = p._io.get("vprep") if p._io is not None else None

    cst, dma_c = load_consts(p, [
        ("onesb", (128, 1), BF16), ("ones1f", (1, 128), F32),
        ("identb", (128, 128), BF16),
    ])
    xn_sb = p.sb("xn_sb", [128, NCH * T], BF16)
    w_sb = p.sb("w_sb", [128, 2 * NCH * 128], BF16)
    stg = p.sb("stg", [128, 2 * T], BF16)
    vt_sb = p.sb("vt_sb", [128, T], BF16)

    for e in [nc.vector, nc.scalar, nc.tensor, nc.gpsimd, nc.sync]:
        e.wait_ge(dma_c.h, dma_c.v)
    v_ap = emit_ln(p, xfm, xn_sb, T, cst)
    nc.tensor.wait_ge(v_ap.h, v_ap.v)

    ct = col_tiles(T)
    dw = [p.sem("dw0"), p.sem("dw1")]
    t_p = p.sem("t_p")
    v_e = p.sem("v_e")
    dwb = [dw[0].v, dw[1].v]
    veb = v_e.v
    s_o = [p.sem("s_o0"), p.sem("s_o1")]
    t_tr = p.sem("t_tr")
    v_tr = p.sem("v_tr")
    s_v = p.sem("s_v")

    def load_panel(src, oc, par):
        dst = w_sb[:, par * NCH * 128:(par + 1) * NCH * 128].rearrange(
            "p (kc m) -> p kc m", m=128)
        if dw[par].v > 0:
            nc.gpsimd.wait_ge(t_p.h, t_p.v)
        dw[par].inc(nc.gpsimd.dma_start(out=dst, in_=src[oc]), 16)

    load_panel(wq, 0, 0)
    load_panel(wq, 1, 1)
    with (
        p.psum("pP0", [128, T], F32) as pP0,
        p.psum("pP1", [128, T], F32) as pP1,
        p.psum("pTv0", [128, 128], BF16) as pTv0,
        p.psum("pTv1", [128, 128], BF16) as pTv1,
    ):
        pP = [pP0, pP1]
        for pi in range(48):
            par = pi % 2
            is_q = pi < 32
            is_v = (not is_q) and ((pi - 32) % 2 == 1)
            nc.tensor.wait_ge(dw[par].h, dwb[par] + 16 * (pi // 2 + 1))
            if pi >= 2:
                nc.tensor.wait_ge(v_e.h, veb + pi - 1)
            ins = None
            for kc in range(NCH):
                w = w_sb[:, par * NCH * 128 + kc * 128:par * NCH * 128 + (kc + 1) * 128]
                for (t0, t1) in ct:
                    ins = nc.tensor.matmul(pP[par][:, t0:t1], w,
                                           xn_sb[:, kc * T + t0:kc * T + t1],
                                           start=(kc == 0), stop=(kc == NCH - 1),
                                           skip_group_check=True)
            t_p.inc(ins)
            if pi + 2 < 48:
                src, oc = (wq, pi + 2) if pi + 2 < 32 else (wkv, pi + 2 - 32)
                load_panel(src, oc, par)
            nc.vector.wait_ge(t_p.h, t_p.v)
            if pi >= 2:
                nc.vector.wait_ge(s_o[par].h, s_o[par].v)
            if is_v:
                v_e.inc(nc.vector.tensor_copy(vt_sb[:, :], pP[par][:, :]))
                g = (pi - 32) // 2
                nc.tensor.wait_ge(v_e.h, v_e.v)
                pTv = [pTv0, pTv1]
                for it in range(T // 128):
                    tpar = it % 2
                    if it >= 2:
                        nc.tensor.wait_ge(v_tr.h, v_tr.v - 1)
                    t_tr.inc(nc.tensor.transpose(pTv[tpar][:, :],
                                                 vt_sb[:, it * 128:(it + 1) * 128],
                                                 cst["identb"][:, :]))
                    nc.vector.wait_ge(t_tr.h, t_tr.v)
                    v_tr.inc(nc.vector.tensor_copy(
                        stg[:, par * T + it * 128:par * T + (it + 1) * 128],
                        pTv[tpar][:, :]))
                nc.sync.wait_ge(v_tr.h, v_tr.v)
                s_o[par].inc(nc.sync.dma_start(
                    out=xv.rearrange("(it p) (g d) -> it p g d", p=128, d=128)
                    [:, :, g, :].rearrange("it p d -> p it d") if False else
                    xv.rearrange("(it p) q -> p it q", p=128)
                    [:, :, g * 128:(g + 1) * 128],
                    in_=stg[:, par * T:(par + 1) * T].rearrange(
                        "p (it d) -> p it d", d=128)), 16)
                if vprep is not None:
                    for rc in (2 * (g // 2), 2 * (g // 2) + 1):
                        s_o[par].inc(nc.sync.dma_start(
                            out=vprep[g % 2, rc],
                            in_=stg[:, par * T:(par + 1) * T]), 16)
            else:
                scl = SC128 if is_q else 1.0
                v_e.inc(nc.vector.tensor_scalar(stg[:, par * T:(par + 1) * T],
                                                pP[par][:, :], scl, None, ALU.mult))
                nc.sync.wait_ge(v_e.h, v_e.v)
                dst = xq if is_q else xk
                row = pi if is_q else (pi - 32) // 2
                s_o[par].inc(nc.sync.dma_start(
                    out=fm_chunk(dst, row, T), in_=stg[:, par * T:(par + 1) * T]), 16)
                if (not is_q) and kprep is not None:
                    g = row
                    for rc in (2 * (g // 2), 2 * (g // 2) + 1):
                        s_o[par].inc(nc.sync.dma_start(
                            out=kprep[g % 2, rc],
                            in_=stg[:, par * T:(par + 1) * T]), 16)
        nc.sync.wait_ge(s_o[0].h, s_o[0].v)
        nc.sync.wait_ge(s_o[1].h, s_o[1].v)
    return p.done()


def build_gqa_b(T=TPC, p=None):
    TK = 2 * T
    p = p or LP("gqa_b")
    nc = p.nc
    xq = p.inp("xq", (D, T), BF16)
    kex = p._io.get("kex") if p._io is not None else None
    vex = p._io.get("vex") if p._io is not None else None
    if kex is None:
        xkh = p.inp("xkh", (32, 128, TK), BF16)
        xvh = p.inp("xvh", (32, TK, 128), BF16)
    vmask = p.inp("vmask", (TK, T), BF16)
    c_ones = p.inp("c_onesbk", (128, 1), BF16)
    c_ones1f = p.inp("c_ones1fk", (1, 128))
    xctx = p.out("xctx", (D, T), BF16)

    dma = p.sem("dma")
    if kex is not None:
        dkv = [p.sem("d_in0"), p.sem("d_in1")]
        dge = nc.sync
    else:
        dkv = [p.sem("dkv0"), p.sem("dkv1")]
        dge = nc.gpsimd
    dkvb = [dkv[0].v, dkv[1].v]
    q_sb = p.sb("q_sb", [128, NCH * T], BF16)
    kh_sb = p.sb("kh_sb", [128, 2 * TK], BF16)
    vh_sb = p.sb("vh_sb", [128, 2 * (TK // 128) * 128], BF16)
    m_sb = p.sb("m_sb", [128, (TK // 128) * T], BF16)
    onesk = p.sb("onesk", [128, 1], BF16)
    ones1f = p.sb("ones1fk", [1, 128], F32)
    pT_sb = p.sb("pT_sb", [128, 2 * 512], BF16)
    zacc = p.sb("zacc", [128, 2 * 512], F32)
    zbf = p.sb("zbf", [128, 2 * 512], BF16)
    zrow = p.sb("zrow", [1, 2 * 512], F32)
    zbc = p.sb("zbc", [128, 2 * 512], F32)
    ctx_sb = p.sb("ctx_sb", [128, 2 * 512], BF16)

    load_fm(p, nc.sync, q_sb, xq, NCH, T, dma, splits=4)
    dma.inc(nc.sync.dma_start(out=onesk[:, :], in_=c_ones), 16)
    dma.inc(nc.sync.dma_start(out=ones1f[:, :], in_=c_ones1f), 16)
    dma.inc(nc.sync.dma_start(
        out=m_sb[:, :].rearrange("p (c t) -> p c t", t=T),
        in_=vmask.rearrange("(c p) t -> p c t", p=128)), 16)

    if kex is not None:
        cr = p.sem("coll_r")
        dge.wait_ge(cr.h, cr.v)

    def load_head(h, par):
        if kex is not None:
            bq = (h % 16) // 4
            j = 1 if h >= 16 else 0
            dkv[par].inc(dge.dma_start(
                out=kh_sb[:, par * TK:(par + 1) * TK].rearrange(
                    "p (s t) -> p s t", s=2),
                in_=kex[j, 2 * bq:2 * bq + 2].rearrange("s f t -> f s t")), 16)
            for s_ in range(2):
                dkv[par].inc(dge.dma_start(
                    out=vh_sb[:, par * (TK // 128) * 128:
                              (par + 1) * (TK // 128) * 128]
                    .rearrange("p (s c q) -> p s c q", s=2, q=128)[:, s_],
                    in_=vex[j, 2 * bq + s_].rearrange(
                        "pp (c q) -> pp c q", q=128)), 16)
            return
        dkv[par].inc(nc.gpsimd.dma_start(
            out=kh_sb[:, par * TK:(par + 1) * TK], in_=xkh[h]), 16)
        dkv[par].inc(nc.gpsimd.dma_start(
            out=vh_sb[:, par * (TK // 128) * 128:(par + 1) * (TK // 128) * 128]
            .rearrange("p (c q) -> p c q", q=128),
            in_=xvh[h].rearrange("(c p) q -> p c q", p=128)), 16)

    DKV_N = 48 if kex is not None else 32
    load_head(0, 0)
    load_head(1, 1)
    for e in [nc.vector, nc.scalar, nc.tensor, nc.sync]:
        e.wait_ge(dma.h, dma.v)

    t_s = p.sem("t_s")
    a_e = p.sem("a_e")
    v_m = p.sem("v_m")
    t_c = p.sem("t_c")
    v_zc = p.sem("v_zc")
    t_zr = p.sem("t_zr")
    v_z = p.sem("v_z")
    t_b = p.sem("t_b")
    v_c = p.sem("v_c")
    s_c = [p.sem("s_c0"), p.sem("s_c1")]
    nqt = T // 512
    nkt = TK // 128

    with (
        p.psum("pS0", [128, 512], F32) as pS0,
        p.psum("pS1", [128, 512], F32) as pS1,
        p.psum("pC0", [128, 512], F32) as pC0,
        p.psum("pC1", [128, 512], F32) as pC1,
        p.psum("pZ0", [1, 512], F32) as pZ0,
        p.psum("pZ1", [1, 512], F32) as pZ1,
        p.psum("pB0", [128, 512], F32) as pB0,
        p.psum("pB1", [128, 512], F32) as pB1,
    ):
        pS = [pS0, pS1]
        pC = [pC0, pC1]
        pZ = [pZ0, pZ1]
        pB = [pB0, pB1]
        ps_ae = [None, None]
        pt_vm = [None, None]
        pt_tc = [None, None]
        vc_snap = [None, None]
        vz_snap = [None, None]
        tb_snap = [None, None]
        qi = 0
        for h in range(32):
            hpar = h % 2
            nc.tensor.wait_ge(dkv[hpar].h,
                              dkvb[hpar] + DKV_N * (h // 2 + 1))
            kT_h = kh_sb[:, hpar * TK:(hpar + 1) * TK]
            vbase = hpar * nkt * 128
            for qt in range(nqt):
                par = qi % 2
                KTH = min(nkt, (T + (qt + 1) * 512) // 128)
                qT_t = q_sb[:, h * T + qt * 512:h * T + (qt + 1) * 512]
                pend = [None]

                def flush_av(pend=pend, par=par, KTH=KTH, vbase=vbase):
                    kt_, kpar_, vmv = pend[0]
                    nc.tensor.wait_ge(v_m.h, vmv)
                    if kt_ == 0 and vc_snap[par] is not None:
                        nc.tensor.wait_ge(v_c.h, vc_snap[par])
                    t_c.inc(nc.tensor.matmul(
                        pC[par][:, :],
                        vh_sb[:, vbase + kt_ * 128:vbase + (kt_ + 1) * 128],
                        pT_sb[:, kpar_ * 512:(kpar_ + 1) * 512],
                        start=(kt_ == 0), stop=(kt_ == KTH - 1),
                        skip_group_check=True))
                    pt_tc[kpar_] = t_c.v
                    pend[0] = None

                for kt in range(KTH):
                    kpar = kt % 2
                    pT_t = pT_sb[:, kpar * 512:(kpar + 1) * 512]
                    if ps_ae[kpar] is not None:
                        nc.tensor.wait_ge(a_e.h, ps_ae[kpar])
                    t_s.inc(nc.tensor.matmul(pS[kpar][:, :],
                                             kT_h[:, kt * 128:(kt + 1) * 128],
                                             qT_t, start=True, stop=True,
                                             skip_group_check=True))
                    nc.scalar.wait_ge(t_s.h, t_s.v)
                    if pt_vm[kpar] is not None:
                        nc.scalar.wait_ge(v_m.h, pt_vm[kpar])
                        nc.scalar.wait_ge(t_c.h, pt_tc[kpar])
                    a_e.inc(nc.scalar.activation(pT_t, pS[kpar][:, :], AF.Exp))
                    ps_ae[kpar] = a_e.v
                    nc.vector.wait_ge(a_e.h, a_e.v)
                    mk = m_sb[:, kt * T + qt * 512:kt * T + (qt + 1) * 512]
                    p.g(nc.vector, nc.vector.tensor_tensor(
                        pT_t, pT_t, mk, ALU.mult))
                    za = zacc[:, par * 512:(par + 1) * 512]
                    if kt == 0:
                        v_m.inc(nc.vector.tensor_copy(za, pT_t))
                    else:
                        v_m.inc(nc.vector.tensor_tensor(za, za, pT_t, ALU.add))
                    pt_vm[kpar] = v_m.v
                    if pend[0] is not None:
                        flush_av()
                    pend[0] = (kt, kpar, v_m.v)
                flush_av()
                v_zc.inc(nc.vector.tensor_copy(
                    zbf[:, par * 512:(par + 1) * 512],
                    zacc[:, par * 512:(par + 1) * 512]))
                nc.tensor.wait_ge(v_zc.h, v_zc.v)
                if vz_snap[par] is not None:
                    nc.tensor.wait_ge(v_z.h, vz_snap[par])
                t_zr.inc(nc.tensor.matmul(
                    pZ[par][:, :], onesk[:, :],
                    zbf[:, par * 512:(par + 1) * 512],
                    start=True, stop=True, skip_group_check=True))
                nc.vector.wait_ge(t_zr.h, t_zr.v)
                if tb_snap[par] is not None:
                    nc.vector.wait_ge(t_b.h, tb_snap[par])
                v_z.inc(nc.vector.reciprocal(
                    zrow[:, par * 512:(par + 1) * 512], pZ[par][:, :]))
                vz_snap[par] = v_z.v
                nc.tensor.wait_ge(v_z.h, v_z.v)
                t_b.inc(nc.tensor.matmul(pB[par][:, :], ones1f[:, :],
                                         zrow[:, par * 512:(par + 1) * 512],
                                         start=True, stop=True,
                                         skip_group_check=True))
                tb_snap[par] = t_b.v
                nc.vector.wait_ge(t_b.h, t_b.v)
                p.g(nc.vector, nc.vector.tensor_copy(
                    zbc[:, par * 512:(par + 1) * 512], pB[par][:, :]))
                nc.vector.wait_ge(t_c.h, t_c.v)
                nc.vector.wait_ge(s_c[par].h, s_c[par].v)
                v_c.inc(nc.vector.tensor_tensor(
                    ctx_sb[:, par * 512:(par + 1) * 512], pC[par][:, :],
                    zbc[:, par * 512:(par + 1) * 512], ALU.mult))
                vc_snap[par] = v_c.v
                ins = nc.sync.dma_start(
                    out=fm_chunk(xctx, h, T)[:, qt * 512:(qt + 1) * 512],
                    in_=ctx_sb[:, par * 512:(par + 1) * 512])
                ins._wait_ge(v_c.h, v_c.v)
                s_c[par].inc(ins, 16)
                qi += 1
            if h + 2 < 32:
                dge.wait_ge(t_s.h, t_s.v)
                dge.wait_ge(t_c.h, t_c.v)
                load_head(h + 2, hpar)
        nc.sync.wait_ge(s_c[0].h, s_c[0].v)
        nc.sync.wait_ge(s_c[1].h, s_c[1].v)
    return p.done()


def build_dense_out(T=TPC, p=None):
    p = p or LP("dense_out")
    nc = p.nc
    xfm = p.inp("xfm", (D, T))
    xin = p.inp("xin", (D, T), BF16)
    w = p.inp("w", (NCH, 128, NCH, 128), BF16)
    bias = p.inp("bias", (D, 1))
    xout = p.out("xout", (D, T))

    dma = p.sem("dma")
    dw = [p.sem("dw0"), p.sem("dw1")]
    dwb0 = [dw[0].v, dw[1].v]
    xin_sb = p.sb("xin_sb", [128, NCH * T], BF16)
    w_sb = p.sb("w_sb", [128, 2 * NCH * 128], BF16)
    b_sb = p.sb("b_sb", [128, NCH], F32)

    load_fm(p, nc.sync, xin_sb, xin, NCH, T, dma, splits=4)
    with nc.allow_non_contiguous_dma(reason="4KB bias, one-time"):
        dma.inc(nc.sync.dma_start(
            out=b_sb[:, :].rearrange("p (c o) -> p c o", o=1),
            in_=bias.rearrange("(c p) o -> p c o", p=128)), 16)
    for e in [nc.vector, nc.tensor, nc.gpsimd, nc.sync]:
        e.wait_ge(dma.h, dma.v)

    def load_panel(oc, par):
        dst = w_sb[:, par * NCH * 128:(par + 1) * NCH * 128].rearrange(
            "p (kc m) -> p kc m", m=128)
        dw[par].inc(nc.gpsimd.dma_start(out=dst, in_=w[oc]), 16)

    load_panel(0, 0)
    load_panel(1, 1)
    ct = col_tiles(T)
    t_d = p.sem("t_d")
    with (
        p.psum("pD0", [128, T], F32) as pD0,
        p.psum("pD1", [128, T], F32) as pD1,
    ):
        pD = [pD0, pD1]
        v_res = p.sem("res_vx")
        vrb = v_res.v
        dwb = dwb0

        def delta_cb(oc):
            par = oc % 2
            if oc >= 2:
                nc.tensor.wait_ge(v_res.h, vrb + oc - 1)
            nc.tensor.wait_ge(dw[par].h, dwb[par] + 16 * (oc // 2 + 1))
            ins = None
            for kc in range(NCH):
                wv = w_sb[:, par * NCH * 128 + kc * 128:par * NCH * 128 + (kc + 1) * 128]
                for (t0, t1) in ct:
                    ins = nc.tensor.matmul(pD[par][:, t0:t1], wv,
                                           xin_sb[:, kc * T + t0:kc * T + t1],
                                           start=(kc == 0), stop=(kc == NCH - 1),
                                           skip_group_check=True)
            t_d.inc(ins)
            if oc + 2 < NCH:
                nc.gpsimd.wait_ge(t_d.h, t_d.v)
                load_panel(oc + 2, par)
            return pD[par][:, :], t_d

        d_rp = [p.sem("res_dx0"), p.sem("res_dx1")]
        v_x = v_res
        s_outp = [p.sem("res_st0"), p.sem("res_st1")]
        with p.sbuf("res_xst", [128, 2 * T], F32) as xst:
            for oc in range(NCH):
                par = oc % 2
                if oc >= 2:
                    nc.sync.wait_ge(s_outp[par].h, s_outp[par].v)
                    nc.sync.wait_ge(d_rp[par].h, d_rp[par].v)
                d_rp[par].inc(nc.sync.dma_start(
                    out=xst[:, par * T:(par + 1) * T],
                    in_=fm_chunk(xfm, oc, T)), 16)
                psum_ap, t_sem = delta_cb(oc)
                nc.vector.wait_ge(t_sem.h, t_sem.v)
                nc.vector.wait_ge(d_rp[par].h, d_rp[par].v)
                p.g(nc.vector, nc.vector.tensor_scalar(
                    xst[:, par * T:(par + 1) * T], xst[:, par * T:(par + 1) * T],
                    b_sb[:, oc:oc + 1], None, ALU.add))
                v_x.inc(nc.vector.tensor_tensor(
                    xst[:, par * T:(par + 1) * T], xst[:, par * T:(par + 1) * T],
                    psum_ap, ALU.add))
                ins = nc.sync.dma_start(out=fm_chunk(xout, oc, T),
                                        in_=xst[:, par * T:(par + 1) * T])
                ins._wait_ge(v_x.h, v_x.v)
                s_outp[par].inc(ins, 16)
            nc.sync.wait_ge(s_outp[0].h, s_outp[0].v)
            nc.sync.wait_ge(s_outp[1].h, s_outp[1].v)
    return p.done()


def gqa_inmaps_a(inputs, xs):
    hc = host_const_arrays()
    base = {
        "wq": _bf(blockify(np.asarray(inputs["L1_attn_a_Wq"]))),
        "wkv": _bf(blockify(np.asarray(inputs["L1_attn_a_Wkv"]))),
        "c_onesb": hc["onesb"], "c_ones1f": hc["ones1f"], "c_identb": hc["identb"],
    }
    return [dict(base, xfm=xs[c]) for c in range(NCORES)]


def gqa_inmaps_b(ra, T=TPC):
    hc = host_const_arrays()
    TK = 2 * T
    xk_full = {}
    xv_full = {}
    for b in range(4):
        xk_full[b] = np.concatenate([ra[2 * b]["xk"], ra[2 * b + 1]["xk"]], axis=1)
        xv_full[b] = np.concatenate([ra[2 * b]["xv"], ra[2 * b + 1]["xv"]], axis=0)
    maps = []
    for c in range(NCORES):
        beta, half = c // 2, c % 2
        xkh = np.empty((32, 128, TK), BF)
        xvh = np.empty((32, TK, 128), BF)
        for eta in range(32):
            bq = (eta % 16) // 4
            gq = 2 * beta + (1 if eta >= 16 else 0)
            xkh[eta] = xk_full[bq][gq * 128:(gq + 1) * 128, :]
            xvh[eta] = xv_full[bq][:, gq * 128:(gq + 1) * 128]
        qoff = half * T
        kpos = np.arange(TK)[:, None]
        qpos = qoff + np.arange(T)[None, :]
        vm = (kpos <= qpos).astype(np.float32)
        maps.append({
            "xq": ra[c]["xq"], "xkh": xkh, "xvh": xvh, "vmask": _bf(vm),
            "c_onesbk": hc["onesb"], "c_ones1fk": hc["ones1f"],
        })
    return maps


def gqa_inmaps_c(inputs, xs, rb):
    base = {
        "w": _bf(blockify(np.asarray(inputs["L1_attn_a_Wo"]))),
        "bias": _f32(np.asarray(inputs["L1_attn_a_bo"]).reshape(D, 1)),
    }
    return [dict(base, xfm=xs[c], xin=rb[c]["xctx"]) for c in range(NCORES)]



def build_mamba_a(T=TPC, p=None):
    T2 = T + 4
    p = p or LP("mamba_a")
    nc = p.nc
    xfm = p.inp("xfm", (D, T2))
    wproj = p.inp("wproj", (67, 128, NCH, 128), BF16)
    mupt = p.inp("mupt", (128, 67))
    lnw = p.inp("lnw", (128, NCH))
    wconv = p.inp("wconv", (128, 34 * 4))
    dtb = p.inp("dtb", (128, 1))
    alog = p.inp("alog", (128, 1))
    zs = p.out("zs", (D, T), BF16)
    xbcs = p.out("xbcs", (34 * 128, T), BF16)
    dts = p.out("dts", (64, T), BF16)
    dtas = p.out("dtas", (64, T), BF16)

    cst, dma_c = load_consts(p, [
        ("onesb", (128, 1), BF16), ("ones1f", (1, 128), F32),
    ])
    dmw = p.sem("dmw")
    dmr = p.sem("dmr")
    xn_sb = p.sb("xn_sb", [128, NCH * T2], BF16)
    w_sb = p.sb("w_sb", [128, 2 * NCH * 128], BF16)
    mup_sb = p.sb("mup_sb", [128, 67], F32)
    lnw_sb = p.sb("lnw_sb", [128, NCH], F32)
    wc_sb = p.sb("wc_sb", [128, 34 * 4], F32)
    dtb_sb = p.sb("dtb_sb", [128, 1], F32)
    al_sb = p.sb("al_sb", [128, 4], F32)
    stg = p.sb("stg", [128, 2 * T2], F32)
    stb = p.sb("stb", [128, 2 * T], BF16)
    cacc = p.sb("cacc", [128, 2 * T], F32)

    dmr.inc(nc.sync.dma_start(out=mup_sb[:, :], in_=mupt), 16)
    dmr.inc(nc.sync.dma_start(out=lnw_sb[:, :], in_=lnw), 16)
    dmr.inc(nc.sync.dma_start(out=wc_sb[:, :], in_=wconv), 16)
    dmr.inc(nc.sync.dma_start(out=dtb_sb[:, :], in_=dtb), 16)
    dmr.inc(nc.sync.dma_start(out=al_sb[:, 0:1], in_=alog), 16)
    for e in [nc.vector, nc.scalar, nc.tensor, nc.gpsimd, nc.sync]:
        e.wait_ge(dma_c.h, dma_c.v)
        e.wait_ge(dmr.h, dmr.v)

    a_pre = p.sem("a_pre")
    a_pre.inc(nc.scalar.activation(al_sb[:, 1:2], al_sb[:, 0:1], AF.Exp))
    v_pre = p.sem("v_pre")
    nc.vector.wait_ge(a_pre.h, a_pre.v)
    v_pre.inc(nc.vector.tensor_scalar(al_sb[:, 1:2], al_sb[:, 1:2], -1.0, None,
                                      ALU.mult))

    v_ap = emit_ln(p, xfm, xn_sb, T2, cst, mode="mamba", lnw_col=lnw_sb)
    nc.tensor.wait_ge(v_ap.h, v_ap.v)

    ct = col_tiles(T2)
    dw = [p.sem("dw0"), p.sem("dw1")]
    t_p = p.sem("t_p")
    v_e = p.sem("v_e")
    dwb = [dw[0].v, dw[1].v]
    veb = v_e.v
    v_cv = p.sem("v_cv")
    a_si = p.sem("a_si")
    s_o = [p.sem("s_o0"), p.sem("s_o1")]

    def load_panel(pi, par):
        dst = w_sb[:, par * NCH * 128:(par + 1) * NCH * 128].rearrange(
            "p (kc m) -> p kc m", m=128)
        dw[par].inc(nc.gpsimd.dma_start(out=dst, in_=wproj[pi]), 16)

    load_panel(0, 0)
    load_panel(1, 1)
    with (
        p.psum("pP0", [128, T2], F32) as pP0,
        p.psum("pP1", [128, T2], F32) as pP1,
    ):
        pP = [pP0, pP1]
        for pi in range(67):
            par = pi % 2
            nc.tensor.wait_ge(dw[par].h, dwb[par] + 16 * (pi // 2 + 1))
            if pi >= 2:
                nc.tensor.wait_ge(v_e.h, veb + pi - 1)
            ins = None
            for kc in range(NCH):
                w = w_sb[:, par * NCH * 128 + kc * 128:par * NCH * 128 + (kc + 1) * 128]
                for (t0, t1) in ct:
                    ins = nc.tensor.matmul(pP[par][:, t0:t1], w,
                                           xn_sb[:, kc * T2 + t0:kc * T2 + t1],
                                           start=(kc == 0), stop=(kc == NCH - 1),
                                           skip_group_check=True)
            t_p.inc(ins)
            if pi + 2 < 67:
                nc.gpsimd.wait_ge(t_p.h, t_p.v)
                load_panel(pi + 2, par)
            nc.vector.wait_ge(t_p.h, t_p.v)
            if pi >= 2:
                nc.vector.wait_ge(s_o[par].h, s_o[par].v)
            if pi < 32:
                v_e.inc(nc.vector.tensor_scalar(
                    stb[:, par * T:(par + 1) * T], pP[par][:, 4:T2],
                    mup_sb[:, pi:pi + 1], None, ALU.mult))
                ins = nc.sync.dma_start(out=fm_chunk(zs, pi, T),
                                        in_=stb[:, par * T:(par + 1) * T])
                ins._wait_ge(v_e.h, v_e.v)
                s_o[par].inc(ins, 16)
            elif pi < 66:
                cp = pi - 32
                p.g(nc.vector, nc.vector.tensor_scalar(
                    stg[:, par * T2:(par + 1) * T2], pP[par][:, :],
                    mup_sb[:, pi:pi + 1], None, ALU.mult))
                sg = stg[:, par * T2:(par + 1) * T2]
                ca = cacc[:, par * T:(par + 1) * T]
                p.g(nc.vector, nc.vector.tensor_scalar(
                    ca, sg[:, 1:1 + T], wc_sb[:, 4 * cp:4 * cp + 1], None, ALU.mult))
                for k in range(1, 4):
                    p.g(nc.vector, nc.vector.scalar_tensor_tensor(
                        ca, sg[:, 1 + k:1 + k + T], wc_sb[:, 4 * cp + k:4 * cp + k + 1],
                        ca, ALU.mult, ALU.add))
                v_cv.inc(nc.vector.engine_nop())
                nc.scalar.wait_ge(v_cv.h, v_cv.v)
                a_si.inc(nc.scalar.activation(stb[:, par * T:(par + 1) * T], ca,
                                              AF.Silu))
                ins = nc.sync.dma_start(out=fm_chunk(xbcs, cp, T),
                                        in_=stb[:, par * T:(par + 1) * T])
                ins._wait_ge(a_si.h, a_si.v)
                s_o[par].inc(ins, 16)
                v_e.inc(nc.vector.engine_nop())
            else:
                sg = stg[:, par * T2:par * T2 + T]
                p.g(nc.vector, nc.vector.tensor_scalar(
                    sg, pP[par][:, 4:T2], mup_sb[:, pi:pi + 1], None, ALU.mult))
                p.g(nc.vector, nc.vector.tensor_scalar(
                    sg, sg, dtb_sb[:, 0:1], None, ALU.add))
                ab = cacc[:, 0:T]
                lnb = cacc[:, T:2 * T]
                va = p.sem("v_sp")
                va.inc(nc.vector.engine_nop())
                nc.scalar.wait_ge(va.h, va.v)
                asp = p.sem("a_sp")
                p.g(nc.scalar, nc.scalar.activation(ab, sg, AF.Abs))
                asp.inc(nc.scalar.activation(ab, ab, AF.Exp, scale=-1.0))
                nc.vector.wait_ge(asp.h, asp.v)
                va.inc(nc.vector.tensor_scalar(ab, ab, 1.0, None, ALU.add))
                nc.scalar.wait_ge(va.h, va.v)
                asp.inc(nc.scalar.activation(lnb, ab, AF.Ln))
                nc.vector.wait_ge(asp.h, asp.v)
                p.g(nc.vector, nc.vector.tensor_scalar(sg, sg, 0.0, None, ALU.max))
                p.g(nc.vector, nc.vector.tensor_tensor(sg, sg, lnb, ALU.add))
                nc.vector.wait_ge(s_o[1 - par].h, s_o[1 - par].v)
                nc.vector.wait_ge(v_pre.h, v_pre.v)
                p.g(nc.vector, nc.vector.tensor_copy(stb[:, 0:T], sg))
                v_dt = p.sem("v_dt")
                v_dt.inc(nc.vector.tensor_scalar(stb[:, T:2 * T], sg,
                                                 al_sb[:, 1:2], None, ALU.mult))
                ins = nc.sync.dma_start(out=dts, in_=stb[0:64, 0:T])
                ins._wait_ge(v_dt.h, v_dt.v)
                s_o[par].inc(ins, 16)
                ins = nc.sync.dma_start(out=dtas, in_=stb[0:64, T:2 * T])
                ins._wait_ge(v_dt.h, v_dt.v)
                s_o[par].inc(ins, 16)
                v_e.inc(nc.vector.engine_nop())
        nc.sync.wait_ge(s_o[0].h, s_o[0].v)
        nc.sync.wait_ge(s_o[1].h, s_o[1].v)
    return p.done()


def mamba_inmaps_a(inputs, xs, x_raw, T=TPC):
    hc = host_const_arrays()
    ip = np.asarray(inputs["L0_attn_a_in_proj_w"])
    ippad = np.zeros((4096, 67 * 128), np.float32)
    ippad[:, :8512] = ip
    mup = np.asarray(inputs["L0_attn_a_mup_vector"])
    muppad = np.zeros(67 * 128, np.float32)
    muppad[:8512] = mup
    cw = np.asarray(inputs["L0_attn_a_conv_w"])
    dtb = np.zeros((128, 1), np.float32)
    dtb[:64, 0] = np.asarray(inputs["L0_attn_a_dt_bias"])
    alog = np.zeros((128, 1), np.float32)
    alog[:64, 0] = np.asarray(inputs["L0_attn_a_A_log"])
    base = {
        "wproj": _bf(blockify(ippad)),
        "mupt": _f32(muppad.reshape(67, 128).T),
        "lnw": _f32(np.asarray(inputs["L0_attn_a_ln_w"]).reshape(NCH, 128).T),
        "wconv": _f32(cw.reshape(34, 128, 4).transpose(1, 0, 2).reshape(128, 34 * 4)),
        "dtb": dtb, "alog": alog,
        "c_onesb": hc["onesb"], "c_ones1f": hc["ones1f"],
    }
    maps = []
    for c in range(NCORES):
        b, h = c // 2, c % 2
        lead = np.zeros((4096, 4), np.float32)
        if h == 1:
            lead = _f32(x_raw[b, T - 4:T, :].T)
        xf2 = np.concatenate([lead, xs[c]], axis=1)
        maps.append(dict(base, xfm=_f32(xf2)))
    return maps


def emit_dense_residual(p, xin_sb, w_ap, xfm, xout, T, bias_sb=None, ready_sem=None):
    nc = p.nc
    dw = [p.sem("do_dw0"), p.sem("do_dw1")]
    dwb0 = [dw[0].v, dw[1].v]
    w_sb = p.sb("do_w_sb", [128, 2 * NCH * 128], BF16)
    t_d = p.sem("do_t_d")

    def load_panel(oc, par):
        dst = w_sb[:, par * NCH * 128:(par + 1) * NCH * 128].rearrange(
            "p (kc m) -> p kc m", m=128)
        dw[par].inc(nc.gpsimd.dma_start(out=dst, in_=w_ap[oc]), 16)

    if ready_sem is not None:
        nc.gpsimd.wait_ge(ready_sem.h, ready_sem.v)
    load_panel(0, 0)
    load_panel(1, 1)
    ct = col_tiles(T)
    with (
        p.psum("do_pD0", [128, T], F32) as pD0,
        p.psum("do_pD1", [128, T], F32) as pD1,
    ):
        pD = [pD0, pD1]
        v_x = p.sem("do_res_vx")
        vxb = v_x.v
        dwb = dwb0
        d_rp = [p.sem("do_res_dx0"), p.sem("do_res_dx1")]
        s_outp = [p.sem("do_res_st0"), p.sem("do_res_st1")]
        with p.sbuf("do_res_xst", [128, 2 * T], F32) as xst:
            for oc in range(NCH):
                par = oc % 2
                if oc >= 2:
                    nc.sync.wait_ge(s_outp[par].h, s_outp[par].v)
                    nc.sync.wait_ge(d_rp[par].h, d_rp[par].v)
                d_rp[par].inc(nc.sync.dma_start(
                    out=xst[:, par * T:(par + 1) * T],
                    in_=fm_chunk(xfm, oc, T)), 16)
                if oc >= 2:
                    nc.tensor.wait_ge(v_x.h, vxb + oc - 1)
                nc.tensor.wait_ge(dw[par].h, dwb[par] + 16 * (oc // 2 + 1))
                if ready_sem is not None and oc == 0:
                    nc.tensor.wait_ge(ready_sem.h, ready_sem.v)
                ins = None
                for kc in range(NCH):
                    wv = w_sb[:, par * NCH * 128 + kc * 128:
                              par * NCH * 128 + (kc + 1) * 128]
                    for (t0, t1) in ct:
                        ins = nc.tensor.matmul(pD[par][:, t0:t1], wv,
                                               xin_sb[:, kc * T + t0:kc * T + t1],
                                               start=(kc == 0), stop=(kc == NCH - 1),
                                               skip_group_check=True)
                t_d.inc(ins)
                if oc + 2 < NCH:
                    nc.gpsimd.wait_ge(t_d.h, t_d.v)
                    load_panel(oc + 2, par)
                nc.vector.wait_ge(t_d.h, t_d.v)
                nc.vector.wait_ge(d_rp[par].h, d_rp[par].v)
                if bias_sb is not None:
                    p.g(nc.vector, nc.vector.tensor_scalar(
                        xst[:, par * T:(par + 1) * T], xst[:, par * T:(par + 1) * T],
                        bias_sb[:, oc:oc + 1], None, ALU.add))
                v_x.inc(nc.vector.tensor_tensor(
                    xst[:, par * T:(par + 1) * T], xst[:, par * T:(par + 1) * T],
                    pD[par][:, :], ALU.add))
                ins = nc.sync.dma_start(out=fm_chunk(xout, oc, T),
                                        in_=xst[:, par * T:(par + 1) * T])
                ins._wait_ge(v_x.h, v_x.v)
                s_outp[par].inc(ins, 16)
            nc.sync.wait_ge(s_outp[0].h, s_outp[0].v)
            nc.sync.wait_ge(s_outp[1].h, s_outp[1].v)


def build_mamba_b(T=TPC, p=None):
    p = p or LP("mamba_b")
    nc = p.nc
    NCk = 8
    xbcs = p.inp("xbcs", (34 * 128, T), BF16)
    dts = p.inp("dts", (64, T), BF16)
    dtas = p.inp("dtas", (64, T), BF16)
    drow = p.inp("drow", (1, 64))
    ys = p.out("ys", (T, D), BF16)
    hout = p.out("hout", (128, D))
    elg = p.out("elg", (T, 64))

    cst, dma_c = load_consts(p, [
        ("onesb", (128, 1), BF16), ("ones1f", (1, 128), F32),
        ("identf", (128, 128), F32), ("ltri", (128, 128), BF16),
    ])
    drow_sb = p.sb("drow_sb", [1, 64], F32)
    dma_c.inc(nc.sync.dma_start(out=drow_sb[:, :], in_=drow), 16)

    xs_sb = p.sb("xs_sb", [128, 2 * D], BF16)
    xh_sb = p.sb("xh_sb", [128, D], BF16)
    xd_sb = p.sb("xd_sb", [128, D], BF16)
    xc_sb = p.sb("xc_sb", [128, D], BF16)
    bt_sb = p.sb("bt_sb", [128, 2 * 128], BF16)
    bf_sb = p.sb("bf_sb", [128, 2 * 128], BF16)
    cf_sb = p.sb("cf_sb", [128, 2 * 128], BF16)
    dtt_sb = p.sb("dtt_sb", [128, 2 * 64], BF16)
    dta_sb = p.sb("dta_sb", [128, 2 * 64], BF16)
    L_sb = p.sb("L_sb", [128, 64], F32)
    eL_sb = p.sb("eL_sb", [128, 64], F32)
    eLb_sb = p.sb("eLb_sb", [128, 64], F32)
    w2_sb = p.sb("w2_sb", [128, 64], BF16)
    arg_sb = p.sb("arg_sb", [128, 64], F32)
    rows_sb = p.sb("rows_sb", [1, 256], F32)
    cbm_sb = p.sb("cbm_sb", [128, 128], BF16)
    mc_sb = p.sb("mc_sb", [128, 2 * 128], F32)
    mm_sb = p.sb("mm_sb", [128, 2 * 128], BF16)
    dbc_sb = p.sb("dbc_sb", [128, 64], F32)
    elg_st = p.sb("elg_st", [128, 64], F32)
    h_sb = p.sb("h_sb", [128, D], F32)
    hb_sb = p.sb("hb_sb", [128, D], BF16)
    hs_sb = p.sb("hs_sb", [128, 512], F32)
    y_st = p.sb("y_st", [128, 2 * D], BF16)
    t1_sb = p.sb("t1_sb", [128, 512], F32)

    for e in [nc.vector, nc.scalar, nc.tensor, nc.gpsimd, nc.sync]:
        e.wait_ge(dma_c.h, dma_c.v)

    d_in = [p.sem("d_in0"), p.sem("d_in1")]
    t_L = p.sem("t_L")
    v_L = p.sem("v_L")
    a_L = p.sem("a_L")
    t_bc2 = p.sem("t_bc2")
    v_x2 = p.sem("v_x2")
    t_cb = p.sem("t_cb")
    v_cb = p.sem("v_cb")
    t_lb = p.sem("t_lb")
    v_mc = p.sem("v_mc")
    a_m = p.sem("a_m")
    v_mm = p.sem("v_mm")
    t_y = p.sem("t_y")
    v_y = p.sem("v_y")
    s_yp = [p.sem("s_y0"), p.sem("s_y1")]
    t_u = p.sem("t_u")
    v_h = p.sem("v_h")
    v_cr = p.sem("v_cr")
    a_eg = p.sem("a_eg")
    s_eg = p.sem("s_eg")
    vz = p.sem("vz")

    vz.inc(nc.vector.memset(h_sb[:, :], 0.0))
    vz.inc(nc.vector.memset(hb_sb[:, :], 0.0))
    vz.inc(nc.vector.memset(rows_sb[:, :], 0.0))
    with p.psum("pI", [128, 64], F32) as pI:
        tI = p.sem("tI")
        tI.inc(nc.tensor.matmul(pI[:, :], cst["ones1f"][:, :], drow_sb[:, :],
                                start=True, stop=True))
        nc.vector.wait_ge(tI.h, tI.v)
        vz.inc(nc.vector.tensor_copy(dbc_sb[:, :], pI[:, :]))
    for e in [nc.vector, nc.scalar, nc.tensor, nc.sync]:
        e.wait_ge(vz.h, vz.v)

    dvw = {"d": 64}
    with (
        p.psum("pMISC", [128, 512], F32) as pM,
        p.psum("pCB", [128, 128], F32) as pCB,
        p.psum("pLB0", [128, 128], F32) as pLB0,
        p.psum("pLB1", [128, 128], F32) as pLB1,
        p.psum("pYA0", [128, 512], F32) as pYA0,
        p.psum("pYA1", [128, 512], F32) as pYA1,
        p.psum("pYB0", [128, 512], F32) as pYB0,
        p.psum("pYB1", [128, 512], F32) as pYB1,
    ):
        pLB = [pLB0, pLB1]
        pYA = [pYA0, pYA1]
        pYB = [pYB0, pYB1]
        for sc in range(NCk):
            par = sc % 2
            cols = slice(sc * 128, (sc + 1) * 128)
            if sc >= 2:
                nc.sync.wait_ge(t_u.h, t_u.v)
                for dd in d_in:
                    nc.sync.wait_ge(dd.h, dd.v)
            d_in[par].inc(nc.sync.dma_start(
                out=xs_sb[:, par * D:(par + 1) * D],
                in_=xbcs[0:4096, cols], transpose=True), 16)
            d_in[par].inc(nc.sync.dma_start(
                out=bt_sb[:, par * 128:(par + 1) * 128],
                in_=xbcs[4096:4224, cols], transpose=True), 16)
            d_in[par].inc(nc.sync.dma_start(
                out=dtt_sb[:, par * 64:(par + 1) * 64],
                in_=dts[:, cols], transpose=True), 16)
            d_in[par].inc(nc.sync.dma_start(
                out=dta_sb[:, par * 64:(par + 1) * 64],
                in_=dtas[:, cols], transpose=True), 16)
            d_in[par].inc(nc.sync.dma_start(
                out=bf_sb[:, par * 128:(par + 1) * 128],
                in_=fm_chunk(xbcs, 32, T)[:, cols]), 16)
            d_in[par].inc(nc.sync.dma_start(
                out=cf_sb[:, par * 128:(par + 1) * 128],
                in_=fm_chunk(xbcs, 33, T)[:, cols]), 16)

            nc.tensor.wait_ge(d_in[par].h, d_in[par].v)
            if sc >= 1:
                nc.tensor.wait_ge(a_eg.h, a_eg.v)
            nc.tensor.matmul(pM[:, 0:64], cst["ltri"][:, :],
                             dta_sb[:, par * 64:(par + 1) * 64],
                             start=True, stop=True, skip_group_check=True)
            t_L.inc(nc.tensor.matmul(pM[0:1, 64:128], cst["onesb"][:, :],
                                     dta_sb[:, par * 64:(par + 1) * 64],
                                     start=True, stop=True, skip_group_check=True))
            nc.vector.wait_ge(t_L.h, t_L.v)
            p.g(nc.vector, nc.vector.tensor_copy(L_sb[:, :], pM[:, 0:64]))
            p.g(nc.vector, nc.vector.tensor_copy(rows_sb[:, 64:128], pM[0:1, 64:128]))
            v_L.inc(nc.vector.engine_nop())
            nc.scalar.wait_ge(v_L.h, v_L.v)
            a_L.inc(nc.scalar.activation(eL_sb[:, :], L_sb[:, :], AF.Exp))
            nc.tensor.wait_ge(v_L.h, v_L.v)
            nc.tensor.matmul(pM[:, 128:192], cst["ones1f"][:, :], rows_sb[:, 0:64],
                             start=True, stop=True, skip_group_check=True)
            t_bc2.inc(nc.tensor.matmul(pM[:, 192:256], cst["ones1f"][:, :],
                                       rows_sb[:, 64:128], start=True, stop=True,
                                       skip_group_check=True))
            nc.vector.wait_ge(t_bc2.h, t_bc2.v)
            p.g(nc.vector, nc.vector.tensor_tensor(
                rows_sb[:, 0:64], rows_sb[:, 0:64], rows_sb[:, 64:128], ALU.add))
            if sc >= 1:
                nc.vector.wait_ge(s_eg.h, s_eg.v)
            p.g(nc.vector, nc.vector.tensor_tensor(
                elg_st[:, :], L_sb[:, :], pM[:, 128:192], ALU.add))
            v_cr.inc(nc.vector.tensor_tensor(
                arg_sb[:, :], pM[:, 192:256], L_sb[:, :], ALU.subtract))
            nc.scalar.wait_ge(v_cr.h, v_cr.v)
            a_eg.inc(nc.scalar.activation(elg_st[:, :], elg_st[:, :], AF.Exp))
            nc.scalar.wait_ge(a_eg.h, a_eg.v)
            p.g(nc.scalar, nc.scalar.activation(w2_sb[:, :], arg_sb[:, :], AF.Exp))
            a_eg.inc(nc.scalar.activation(eLb_sb[:, :], pM[:, 192:256], AF.Exp))
            ins = nc.sync.dma_start(out=elg.rearrange("(c p) r -> c p r", p=128)[sc],
                                    in_=elg_st[:, :])
            ins._wait_ge(a_eg.h, a_eg.v - 1)
            s_eg.inc(ins, 16)

            dt3 = dtt_sb[:, par * 64:(par + 1) * 64].rearrange(
                "p (r o) -> p r o", o=1).broadcast_to([128, 64, 64])
            xs3 = xs_sb[:, par * D:(par + 1) * D].rearrange("p (r d) -> p r d", **dvw)
            xh3 = xh_sb[:, :].rearrange("p (r d) -> p r d", **dvw)
            xd3 = xd_sb[:, :].rearrange("p (r d) -> p r d", **dvw)
            xc3 = xc_sb[:, :].rearrange("p (r d) -> p r d", **dvw)
            db3 = dbc_sb[:, :].rearrange("p (r o) -> p r o", o=1).broadcast_to(
                [128, 64, 64])
            p.g(nc.vector, nc.vector.tensor_tensor(xh3, xs3, dt3, ALU.mult))
            p.g(nc.vector, nc.vector.tensor_tensor(xd3, xs3, db3, ALU.mult))
            nc.vector.wait_ge(a_eg.h, a_eg.v)
            w23 = w2_sb[:, :].rearrange("p (r o) -> p r o", o=1).broadcast_to(
                [128, 64, 64])
            v_x2.inc(nc.vector.tensor_tensor(xc3, xh3, w23, ALU.mult))

            if sc >= 1:
                nc.tensor.wait_ge(v_cb.h, v_cb.v)
            t_cb.inc(nc.tensor.matmul(pCB[:, :], bf_sb[:, par * 128:(par + 1) * 128],
                                      cf_sb[:, par * 128:(par + 1) * 128],
                                      start=True, stop=True, skip_group_check=True))
            nc.vector.wait_ge(t_cb.h, t_cb.v)
            v_cb.inc(nc.vector.tensor_tensor(cbm_sb[:, :], pCB[:, :],
                                             cst["ltri"][:, :], ALU.mult))

            for sl in range(8):
                spar = sl % 2
                gsl = sc * 8 + sl
                if sl == 0 and sc >= 2:
                    nc.vector.wait_ge(s_yp[par].h, s_yp[par].v)
                for rr in range(8):
                    r = sl * 8 + rr
                    rpar = r % 2
                    if r >= 2 or sc >= 1:
                        nc.tensor.wait_ge(v_mc.h, v_mc.v - 1)
                    t_lb.inc(nc.tensor.transpose(
                        pLB[rpar][:, :], L_sb[:, r:r + 1].broadcast_to([128, 128]),
                        cst["identf"][:, :]))
                    nc.vector.wait_ge(t_lb.h, t_lb.v)
                    if r >= 2 or sc >= 1:
                        nc.vector.wait_ge(a_m.h, a_m.v - 1)
                    v_mc.inc(nc.vector.tensor_scalar(
                        mc_sb[:, rpar * 128:(rpar + 1) * 128], pLB[rpar][:, :],
                        L_sb[:, r:r + 1], 0.0, ALU.subtract, ALU.min))
                    nc.scalar.wait_ge(v_mc.h, v_mc.v)
                    if r >= 2 or sc >= 1:
                        nc.scalar.wait_ge(v_mm.h, v_mm.v - 1)
                    a_m.inc(nc.scalar.activation(
                        mc_sb[:, rpar * 128:(rpar + 1) * 128],
                        mc_sb[:, rpar * 128:(rpar + 1) * 128], AF.Exp))
                    nc.vector.wait_ge(a_m.h, a_m.v)
                    nc.vector.wait_ge(v_cb.h, v_cb.v)
                    if r >= 2 or sc >= 1:
                        nc.vector.wait_ge(t_y.h, t_y.v - 1)
                    v_mm.inc(nc.vector.tensor_tensor(
                        mm_sb[:, rpar * 128:(rpar + 1) * 128],
                        mc_sb[:, rpar * 128:(rpar + 1) * 128], cbm_sb[:, :],
                        ALU.mult))
                    nc.tensor.wait_ge(v_mm.h, v_mm.v)
                    if rr == 0:
                        nc.tensor.wait_ge(v_x2.h, v_x2.v)
                        if gsl >= 2:
                            nc.tensor.wait_ge(v_y.h, v_y.v - 1)
                            nc.tensor.wait_ge(v_h.h, v_h.v - 1)
                    t_y.inc(nc.tensor.matmul(
                        pYA[spar][:, rr * 64:(rr + 1) * 64],
                        mm_sb[:, rpar * 128:(rpar + 1) * 128],
                        xh_sb[:, r * 64:(r + 1) * 64],
                        start=True, stop=True, skip_group_check=True))
                t_y.inc(nc.tensor.matmul(
                    pYB[spar][:, :], cf_sb[:, par * 128:(par + 1) * 128],
                    hb_sb[:, sl * 512:(sl + 1) * 512],
                    start=True, stop=True, skip_group_check=True))
                nc.vector.wait_ge(t_y.h, t_y.v)
                eview = eL_sb[:, sl * 8:(sl + 1) * 8].rearrange(
                    "p (r o) -> p r o", o=1).broadcast_to([128, 8, 64])
                t13 = t1_sb[:, :].rearrange("p (r d) -> p r d", **dvw)
                pb3 = pYB[spar][:, :].rearrange("p (r d) -> p r d", **dvw)
                p.g(nc.vector, nc.vector.tensor_tensor(t13, pb3, eview, ALU.mult))
                p.g(nc.vector, nc.vector.tensor_tensor(
                    t1_sb[:, :], t1_sb[:, :], xd_sb[:, sl * 512:(sl + 1) * 512],
                    ALU.add))
                v_y.inc(nc.vector.tensor_tensor(
                    y_st[:, par * D + sl * 512:par * D + (sl + 1) * 512],
                    pYA[spar][:, :], t1_sb[:, :], ALU.add))
                nc.tensor.wait_ge(v_y.h, v_y.v)
                t_u.inc(nc.tensor.matmul(
                    pYB[spar][:, :], bt_sb[:, par * 128:(par + 1) * 128],
                    xc_sb[:, sl * 512:(sl + 1) * 512],
                    start=True, stop=True, skip_group_check=True))
                nc.vector.wait_ge(t_u.h, t_u.v)
                ebview = eLb_sb[:, sl * 8:(sl + 1) * 8].rearrange(
                    "p (r o) -> p r o", o=1).broadcast_to([128, 8, 64])
                hs3 = hs_sb[:, :].rearrange("p (r d) -> p r d", **dvw)
                h3 = h_sb[:, sl * 512:(sl + 1) * 512].rearrange(
                    "p (r d) -> p r d", **dvw)
                p.g(nc.vector, nc.vector.tensor_tensor(hs3, h3, ebview, ALU.mult))
                p.g(nc.vector, nc.vector.tensor_tensor(
                    h_sb[:, sl * 512:(sl + 1) * 512], hs_sb[:, :],
                    pYB[spar][:, :], ALU.add))
                v_h.inc(nc.vector.tensor_copy(
                    hb_sb[:, sl * 512:(sl + 1) * 512],
                    h_sb[:, sl * 512:(sl + 1) * 512]))
            ins = nc.sync.dma_start(
                out=ys.rearrange("(c p) d -> c p d", p=128)[sc],
                in_=y_st[:, par * D:(par + 1) * D])
            ins._wait_ge(v_y.h, v_y.v)
            s_yp[par].inc(ins, 16)
        s_h = p.sem("s_h")
        nc.sync.wait_ge(v_h.h, v_h.v)
        s_h.inc(nc.sync.dma_start(out=hout, in_=h_sb[:, :]), 16)
        nc.sync.wait_ge(s_h.h, s_h.v)
        nc.sync.wait_ge(s_yp[0].h, s_yp[0].v)
        nc.sync.wait_ge(s_yp[1].h, s_yp[1].v)
        nc.sync.wait_ge(s_eg.h, s_eg.v)
    return p.done()


def mamba_inmaps_b(inputs, ra, T=TPC):
    hc = host_const_arrays()
    ltri = np.triu(np.ones((128, 128), np.float32))
    drow = np.asarray(inputs["L0_attn_a_D"]).reshape(1, 64)
    base = {
        "c_onesb": hc["onesb"], "c_ones1f": hc["ones1f"],
        "c_identf": hc["identf"], "c_ltri": _bf(ltri),
        "drow": _f32(drow),
    }
    return [dict(base, xbcs=ra[c]["xbcs"], dts=ra[c]["dts"], dtas=ra[c]["dtas"])
            for c in range(NCORES)]


def build_mamba_c(T=TPC, p=None):
    p = p or LP("mamba_c")
    nc = p.nc
    NCk = 8
    xfm = p.inp("xfm", (D, T))
    ys = p.inp("ys", (T, D), BF16)
    elg = p.inp("elg", (T, 64))
    hin = p.inp("hin", (128, D))
    zsp = p.inp("zsp", (D, T), BF16)
    xbcs = p.inp("xbcs", (34 * 128, T), BF16)
    nwrow = p.inp("nwrow", (1, D))
    wout = p.inp("wout", (NCH, 128, NCH, 128), BF16)
    xout = p.out("xout", (D, T))

    cst, dma_c = load_consts(p, [
        ("ones1f", (1, 128), F32), ("identb", (128, 128), BF16),
    ])
    nw_sb = p.sb("nw_sb", [1, D], F32)
    dma_c.inc(nc.sync.dma_start(out=nw_sb[:, :], in_=nwrow), 16)
    hb_sb = p.sb("hb_sb", [128, D], BF16)
    nwb_sb = p.sb("nwb_sb", [128, D], BF16)
    y_sb = p.sb("y_sb", [128, 2 * D], BF16)
    z_sb = p.sb("z_sb", [128, 2 * D], BF16)
    eg_sb = p.sb("eg_sb", [128, 2 * 64], F32)
    ct_sb = p.sb("ct_sb", [128, 2 * 128], BF16)
    ych_sb = p.sb("ych_sb", [128, D], F32)
    yg_sb = p.sb("yg_sb", [128, D], F32)
    sq_sb = p.sb("sq_sb", [128, D], BF16)
    acc_sb = p.sb("acc_sb", [128, 8], F32)
    ygn_sb = p.sb("ygn_sb", [128, D], BF16)
    ygf_sb = p.sb("ygf_sb", [128, NCH * T], BF16)

    d_h = p.sem("d_h")
    d_in = [p.sem("d_in0"), p.sem("d_in1")]
    d_hb = p.sem("d_hb")
    v_hb = p.sem("v_hb")
    t_nw = p.sem("t_nw")
    v_nw = p.sem("v_nw")
    t_cr = p.sem("t_cr")
    v_y1 = p.sem("v_y1")
    a_z = p.sem("a_z")
    v_yg = p.sem("v_yg")
    a_sq = p.sem("a_sq")
    v_rc = p.sem("v_rc")
    a_rs = p.sem("a_rs")
    v_gn = p.sem("v_gn")
    t_tr = p.sem("t_tr")
    v_tr = p.sem("v_tr")

    with p.sbuf("hin_f", [128, D], F32) as hin_f:
        d_hb.inc(nc.sync.dma_start(out=hin_f[:, :], in_=hin), 16)
        for e in [nc.vector, nc.scalar, nc.tensor, nc.gpsimd, nc.sync]:
            e.wait_ge(dma_c.h, dma_c.v)
        nc.vector.wait_ge(d_hb.h, d_hb.v)
        v_hb.inc(nc.vector.tensor_copy(hb_sb[:, :], hin_f[:, :]))

        with p.psum("pNW", [128, 512], F32) as pNW:
            for i in range(8):
                if i >= 1:
                    nc.tensor.wait_ge(v_nw.h, v_nw.v)
                t_nw.inc(nc.tensor.matmul(pNW[:, :], cst["ones1f"][:, :],
                                          nw_sb[:, i * 512:(i + 1) * 512],
                                          start=True, stop=True,
                                          skip_group_check=True))
                nc.vector.wait_ge(t_nw.h, t_nw.v)
                v_nw.inc(nc.vector.tensor_copy(nwb_sb[:, i * 512:(i + 1) * 512],
                                               pNW[:, :]))

    with (
        p.psum("pC0", [128, 512], F32) as pC0,
        p.psum("pC1", [128, 512], F32) as pC1,
        p.psum("pT0", [128, 128], BF16) as pT0,
        p.psum("pT1", [128, 128], BF16) as pT1,
    ):
        pC = [pC0, pC1]
        pT = [pT0, pT1]
        nc.tensor.wait_ge(v_nw.h, v_nw.v)
        for sc in range(NCk):
            par = sc % 2
            cols = slice(sc * 128, (sc + 1) * 128)
            if sc >= 2:
                nc.sync.wait_ge(v_y1.h, v_y1.v)
                for dd in d_in:
                    nc.sync.wait_ge(dd.h, dd.v)
            d_in[par].inc(nc.sync.dma_start(
                out=y_sb[:, par * D:(par + 1) * D],
                in_=ys.rearrange("(c p) d -> c p d", p=128)[sc]), 16)
            d_in[par].inc(nc.sync.dma_start(
                out=z_sb[:, par * D:(par + 1) * D],
                in_=zsp[:, cols], transpose=True), 16)
            d_in[par].inc(nc.sync.dma_start(
                out=eg_sb[:, par * 64:(par + 1) * 64],
                in_=elg.rearrange("(c p) r -> c p r", p=128)[sc]), 16)
            d_in[par].inc(nc.sync.dma_start(
                out=ct_sb[:, par * 128:(par + 1) * 128],
                in_=fm_chunk(xbcs, 33, T)[:, cols]), 16)
            nc.tensor.wait_ge(d_in[par].h, d_in[par].v)
            nc.tensor.wait_ge(v_hb.h, v_hb.v)
            for sl in range(8):
                spar = sl % 2
                gsl = sc * 8 + sl
                if gsl >= 2:
                    nc.tensor.wait_ge(v_y1.h, v_y1.v - 1)
                t_cr.inc(nc.tensor.matmul(
                    pC[spar][:, :], ct_sb[:, par * 128:(par + 1) * 128],
                    hb_sb[:, sl * 512:(sl + 1) * 512],
                    start=True, stop=True, skip_group_check=True))
                nc.vector.wait_ge(t_cr.h, t_cr.v)
                egv = eg_sb[:, par * 64 + sl * 8:par * 64 + (sl + 1) * 8].rearrange(
                    "p (r o) -> p r o", o=1).broadcast_to([128, 8, 64])
                yv = ych_sb[:, sl * 512:(sl + 1) * 512].rearrange(
                    "p (r d) -> p r d", d=64)
                pc3 = pC[spar][:, :].rearrange("p (r d) -> p r d", d=64)
                p.g(nc.vector, nc.vector.tensor_tensor(yv, pc3, egv, ALU.mult))
                v_y1.inc(nc.vector.tensor_tensor(
                    ych_sb[:, sl * 512:(sl + 1) * 512],
                    ych_sb[:, sl * 512:(sl + 1) * 512],
                    y_sb[:, par * D + sl * 512:par * D + (sl + 1) * 512], ALU.add))
            nc.scalar.wait_ge(d_in[par].h, d_in[par].v)
            if sc >= 1:
                nc.scalar.wait_ge(v_yg.h, v_yg.v)
            a_z.inc(nc.scalar.activation(sq_sb[:, :],
                                         z_sb[:, par * D:(par + 1) * D], AF.Silu))
            nc.vector.wait_ge(a_z.h, a_z.v)
            nc.vector.wait_ge(v_y1.h, v_y1.v)
            v_yg.inc(nc.vector.tensor_tensor(yg_sb[:, :], ych_sb[:, :], sq_sb[:, :],
                                             ALU.mult))
            nc.scalar.wait_ge(v_yg.h, v_yg.v)
            a_sq.inc(nc.scalar.activation(sq_sb[:, :], yg_sb[:, :], AF.Square,
                                          accum_out=acc_sb[:, 0:1]))
            nc.vector.wait_ge(a_sq.h, a_sq.v)
            p.g(nc.vector, nc.vector.tensor_scalar(acc_sb[:, 1:2], acc_sb[:, 0:1],
                                                   1.0 / D, 1e-5, ALU.mult, ALU.add))
            v_rc.inc(nc.vector.engine_nop())
            nc.scalar.wait_ge(v_rc.h, v_rc.v)
            p.g(nc.scalar, nc.scalar.activation(acc_sb[:, 2:3], acc_sb[:, 1:2], AF.Ln))
            a_rs.inc(nc.scalar.activation(acc_sb[:, 2:3], acc_sb[:, 2:3], AF.Exp,
                                          scale=-0.5))
            nc.vector.wait_ge(a_rs.h, a_rs.v)
            p.g(nc.vector, nc.vector.tensor_scalar(ygn_sb[:, :], yg_sb[:, :],
                                                   acc_sb[:, 2:3], None, ALU.mult))
            nc.vector.wait_ge(v_nw.h, v_nw.v)
            v_gn.inc(nc.vector.tensor_tensor(ygn_sb[:, :], ygn_sb[:, :],
                                             nwb_sb[:, :], ALU.mult))
            nc.tensor.wait_ge(v_gn.h, v_gn.v)
            for cc in range(NCH):
                tpar = cc % 2
                if sc * NCH + cc >= 2:
                    nc.tensor.wait_ge(v_tr.h, v_tr.v - 1)
                t_tr.inc(nc.tensor.transpose(pT[tpar][:, :],
                                             ygn_sb[:, cc * 128:(cc + 1) * 128],
                                             cst["identb"][:, :]))
                nc.vector.wait_ge(t_tr.h, t_tr.v)
                v_tr.inc(nc.vector.tensor_copy(
                    ygf_sb[:, cc * T + sc * 128:cc * T + (sc + 1) * 128],
                    pT[tpar][:, :]))

    emit_dense_residual(p, ygf_sb, wout, xfm, xout, T, ready_sem=p.sem("v_tr"))
    return p.done()


def mamba_inmaps_c(inputs, xs, ra, rb, T=TPC):
    hc = host_const_arrays()
    base = {
        "nwrow": _f32(np.asarray(inputs["L0_attn_a_norm_w"]).reshape(1, D)),
        "wout": _bf(blockify(np.asarray(inputs["L0_attn_a_out_proj_w"]))),
        "c_ones1f": hc["ones1f"], "c_identb": hc["identb"],
    }
    maps = []
    for c in range(NCORES):
        b, h = c // 2, c % 2
        hin = np.zeros((128, D), np.float32) if h == 0 else _f32(rb[c - 1]["hout"])
        maps.append(dict(base, xfm=xs[c], ys=rb[c]["ys"], elg=rb[c]["elg"],
                         hin=hin, zsp=ra[c]["zs"], xbcs=ra[c]["xbcs"]))
    return maps



def build_gla_a(T=TPC, p=None):
    p = p or LP("gla_a")
    nc = p.nc
    xfm = p.inp("xfm", (D, T))
    wq = p.inp("wq", (NCH, 128, NCH, 128), BF16)
    wk = p.inp("wk", (8, 128, NCH, 128), BF16)
    wv = p.inp("wv", (8, 128, NCH, 128), BF16)
    xq = p.out("xq", (D, T), BF16)
    xk = p.out("xk", (1024, T), BF16)
    xv = p.out("xv", (1024, T), BF16)
    ktp = p._io.get("ktp") if p._io is not None else None
    vtp = p._io.get("vtp") if p._io is not None else None

    cst, dma_c = load_consts(p, [
        ("onesb", (128, 1), BF16), ("ones1f", (1, 128), F32),
    ])
    xn_sb = p.sb("xn_sb", [128, NCH * T], BF16)
    w_sb = p.sb("w_sb", [128, 2 * NCH * 128], BF16)
    stg = p.sb("stg", [128, 2 * T], BF16)
    for e in [nc.vector, nc.scalar, nc.tensor, nc.gpsimd, nc.sync]:
        e.wait_ge(dma_c.h, dma_c.v)
    v_ap = emit_ln(p, xfm, xn_sb, T, cst)
    nc.tensor.wait_ge(v_ap.h, v_ap.v)

    ct = col_tiles(T)
    dw = [p.sem("dw0"), p.sem("dw1")]
    t_p = p.sem("t_p")
    v_e = p.sem("v_e")
    dwb = [dw[0].v, dw[1].v]
    veb = v_e.v
    s_o = [p.sem("s_o0"), p.sem("s_o1")]

    panels = [(wq, i, xq, i) for i in range(NCH)] + \
             [(wk, i, xk, i) for i in range(8)] + \
             [(wv, i, xv, i) for i in range(8)]

    def load_panel(pi, par):
        src, oc, _, _ = panels[pi]
        dst = w_sb[:, par * NCH * 128:(par + 1) * NCH * 128].rearrange(
            "p (kc m) -> p kc m", m=128)
        dw[par].inc(nc.gpsimd.dma_start(out=dst, in_=src[oc]), 16)

    load_panel(0, 0)
    load_panel(1, 1)
    with (
        p.psum("pP0", [128, T], F32) as pP0,
        p.psum("pP1", [128, T], F32) as pP1,
    ):
        pP = [pP0, pP1]
        for pi in range(len(panels)):
            par = pi % 2
            nc.tensor.wait_ge(dw[par].h, dwb[par] + 16 * (pi // 2 + 1))
            if pi >= 2:
                nc.tensor.wait_ge(v_e.h, veb + pi - 1)
            ins = None
            for kc in range(NCH):
                w = w_sb[:, par * NCH * 128 + kc * 128:par * NCH * 128 + (kc + 1) * 128]
                for (t0, t1) in ct:
                    ins = nc.tensor.matmul(pP[par][:, t0:t1], w,
                                           xn_sb[:, kc * T + t0:kc * T + t1],
                                           start=(kc == 0), stop=(kc == NCH - 1),
                                           skip_group_check=True)
            t_p.inc(ins)
            if pi + 2 < len(panels):
                nc.gpsimd.wait_ge(t_p.h, t_p.v)
                load_panel(pi + 2, par)
            nc.vector.wait_ge(t_p.h, t_p.v)
            if pi >= 2:
                nc.vector.wait_ge(s_o[par].h, s_o[par].v)
            v_e.inc(nc.vector.tensor_copy(stg[:, par * T:(par + 1) * T],
                                          pP[par][:, :]))
            _, _, dst, row = panels[pi]
            ins = nc.sync.dma_start(out=fm_chunk(dst, row, T),
                                    in_=stg[:, par * T:(par + 1) * T])
            ins._wait_ge(v_e.h, v_e.v)
            s_o[par].inc(ins, 16)
            if ktp is not None and pi >= 32:
                tp = ktp if pi < 40 else vtp
                ins = nc.sync.dma_start(
                    out=tp.rearrange("(g pp) v -> g pp v", pp=128)[row],
                    in_=stg[:, par * T + T - 64:par * T + T])
                ins._wait_ge(v_e.h, v_e.v)
                s_o[par].inc(ins, 16)
        nc.sync.wait_ge(s_o[0].h, s_o[0].v)
        nc.sync.wait_ge(s_o[1].h, s_o[1].v)
    return p.done()


def build_gla_b(T=TPC, skip=(), p=None):
    p = p or LP("gla_b")
    nc = p.nc
    NCk = T // 128
    GATE = 1.0 / 16.0
    xq = p.inp("xq", (D, T), BF16)
    xk = p.inp("xk", (1024, T), BF16)
    xv = p.inp("xv", (1024, T), BF16)
    ktail = p.inp("ktail", (1024, 128), BF16)
    vtail = p.inp("vtail", (1024, 128), BF16)
    qcos = p.inp("qcos", (T, 128))
    qsin = p.inp("qsin", (T, 128))
    kcos = p.inp("kcos", (T + 64, 128))
    ksin = p.inp("ksin", (T + 64, 128))
    swm = p.inp("swm", (3, 128, 64))
    obase = p.out("obase", (T, D), BF16)
    qgs = p.out("qgs", (D, T), BF16)
    sloc = p.out("sloc", (1024, 128))

    cst, dma_c = load_consts(p, [
        ("onesb", (128, 1), BF16), ("ones1f", (1, 128), F32),
        ("identb", (128, 128), BF16), ("identf", (128, 128), F32),
        ("ltrib", (128, 128), BF16),
    ])
    swm_sb = p.sb("swm_sb", [128, 192], F32)
    dma_c.inc(nc.sync.dma_start(
        out=swm_sb[:, :].rearrange("p (a b) -> p a b", b=64),
        in_=swm.rearrange("a p b -> p a b")), 16)

    q_sb = p.sb("q_sb", [128, 2 * D], BF16)
    k_sb = p.sb("k_sb", [128, 2 * 1024], BF16)
    v_sb = p.sb("v_sb", [128, 2 * 1024], BF16)
    sq_sb = p.sb("sq_sb", [128, D], BF16)
    sk_sb = p.sb("sk_sb", [128, 2 * 1024], BF16)
    cs_sb = p.sb("cs_sb", [128, 4 * 128], F32)
    qs_sb = p.sb("qs_sb", [128, D], F32)
    rp_sb = p.sb("rp_sb", [128, D], F32)
    zs_sb = p.sb("zs_sb", [128, 64], F32)
    ks_sb = p.sb("ks_sb", [128, 1024], F32)
    gs_sb = p.sb("gs_sb", [128, 1024], F32)
    g1_sb = p.sb("g1_sb", [128, 1024], F32)
    gsb_sb = p.sb("gsb_sb", [128, 1024], BF16)
    eG_sb = p.sb("eG_sb", [128, 1024], BF16)
    eGi_sb = p.sb("eGi_sb", [128, 1024], BF16)
    qt_sb = p.sb("qt_sb", [128, D], BF16)
    kt_sb = p.sb("kt_sb", [128, 1024], BF16)
    qtT_sb = p.sb("qtT_sb", [128, D], BF16)
    qgT_sb = p.sb("qgT_sb", [128, D], BF16)
    ktT_sb = p.sb("ktT_sb", [128, 1024], BF16)
    sqT_sb = p.sb("sqT_sb", [128, D], BF16)
    skT_sb = p.sb("skT_sb", [128, 2 * 1024], BF16)
    vaug_sb = p.sb("vaug_sb", [128, 2 * 8 * 136], BF16)
    wv_sb = p.sb("wv_sb", [128, 8 * 136], BF16)
    crow_sb = p.sb("crow_sb", [1, 2 * 1024], F32)
    ec_sb = p.sb("ec_sb", [128, 16], F32)
    ecl_sb = p.sb("ecl_sb", [128, 16], F32)
    S_sb = p.sb("S_sb", [128, 1024], F32)
    Sb_sb = p.sb("Sb_sb", [128, 1024], BF16)
    Ss_sb = p.sb("Ss_sb", [128, 128], F32)
    am_sb = p.sb("am_sb", [128, 2 * 128], BF16)
    pw_sb = p.sb("pw_sb", [128, 2 * 64], BF16)
    psm_sb = p.sb("psm_sb", [128, 2 * 64], F32)
    ost_sb = p.sb("ost_sb", [128, 2 * D], BF16)
    zc_sb = p.sb("zc_sb", [128, 4], F32)

    for e in [nc.vector, nc.scalar, nc.tensor, nc.gpsimd, nc.sync]:
        e.wait_ge(dma_c.h, dma_c.v)

    d_in = [p.sem("d_in0"), p.sem("d_in1")]
    d_cs = p.sem("d_cs")
    d_tl = p.sem("d_tl")
    v_sm = p.sem("v_sm")
    a_sm = p.sem("a_sm")
    v_rp = p.sem("v_rp")
    t_G = p.sem("t_G")
    v_G = p.sem("v_G")
    a_G = p.sem("a_G")
    v_til = p.sem("v_til")
    t_tr = p.sem("t_tr")
    v_tr = p.sem("v_tr")
    t_A = p.sem("t_A")
    v_A = p.sem("v_A")
    t_o = p.sem("t_o")
    v_o = p.sem("v_o")
    s_ob = [p.sem("s_ob0"), p.sem("s_ob1")]
    s_qg = [p.sem("s_qg0"), p.sem("s_qg1")]
    t_u = p.sem("t_u")
    v_S = p.sem("v_S")
    t_sw = p.sem("t_sw")
    v_swm = p.sem("v_swm")
    a_sw = p.sem("a_sw")
    t_pv = p.sem("t_pv")
    v_sw = p.sem("v_sw")
    v_va = p.sem("v_va")
    vz = p.sem("vz")

    def rope(dst3, src3, c1, c2, s1, s2, t3, vsem):
        p.g(nc.vector, nc.vector.tensor_tensor(dst3[:, :, 0:64], src3[:, :, 64:128],
                                               s1, ALU.mult))
        p.g(nc.vector, nc.vector.tensor_scalar(dst3[:, :, 0:64], dst3[:, :, 0:64],
                                               -1.0, None, ALU.mult))
        p.g(nc.vector, nc.vector.tensor_tensor(t3[:, :, 0:64], src3[:, :, 0:64],
                                               c1, ALU.mult))
        p.g(nc.vector, nc.vector.tensor_tensor(dst3[:, :, 0:64], dst3[:, :, 0:64],
                                               t3[:, :, 0:64], ALU.add))
        p.g(nc.vector, nc.vector.tensor_tensor(dst3[:, :, 64:128], src3[:, :, 0:64],
                                               s2, ALU.mult))
        p.g(nc.vector, nc.vector.tensor_tensor(t3[:, :, 64:128], src3[:, :, 64:128],
                                               c2, ALU.mult))
        vsem.inc(nc.vector.tensor_tensor(dst3[:, :, 64:128], dst3[:, :, 64:128],
                                         t3[:, :, 64:128], ALU.add))

    def csview(lo, hi, n, rows=128):
        return cs_sb[0:rows, lo:hi].rearrange("p (h d) -> p h d", h=1).broadcast_to(
            [rows, n, 64])

    vz.inc(nc.vector.memset(S_sb[:, :], 0.0))
    vz.inc(nc.vector.memset(Sb_sb[:, :], 0.0))
    vz.inc(nc.vector.memset(crow_sb[:, :], 0.0))
    vz.inc(nc.vector.memset(vaug_sb[:, :], 1.0))
    d_tl.inc(nc.sync.dma_start(out=k_sb[:, 1024:2048], in_=ktail,
                               transpose=True), 16)
    d_tl.inc(nc.sync.dma_start(out=v_sb[:, 1024:2048], in_=vtail,
                               transpose=True), 16)
    d_tl.inc(nc.sync.dma_start(out=cs_sb[64:128, 256:384], in_=kcos[0:64, :]), 16)
    d_tl.inc(nc.sync.dma_start(out=cs_sb[64:128, 384:512], in_=ksin[0:64, :]), 16)
    nc.vector.wait_ge(d_tl.h, d_tl.v)
    nc.vector.wait_ge(vz.h, vz.v)
    rope(sk_sb[64:128, 1024:2048].rearrange("p (h d) -> p h d", d=128),
         k_sb[64:128, 1024:2048].rearrange("p (h d) -> p h d", d=128),
         cs_sb[64:128, 256:320].rearrange("p (h d) -> p h d", h=1)
         .broadcast_to([64, 8, 64]),
         cs_sb[64:128, 320:384].rearrange("p (h d) -> p h d", h=1)
         .broadcast_to([64, 8, 64]),
         cs_sb[64:128, 384:448].rearrange("p (h d) -> p h d", h=1)
         .broadcast_to([64, 8, 64]),
         cs_sb[64:128, 448:512].rearrange("p (h d) -> p h d", h=1)
         .broadcast_to([64, 8, 64]),
         rp_sb[64:128, 0:1024].rearrange("p (h d) -> p h d", d=128), v_rp)
    va1 = vaug_sb[:, 8 * 136:2 * 8 * 136].rearrange("p (g w) -> p g w", w=136)
    v_rp.inc(nc.vector.tensor_copy(
        va1[64:128, :, 0:128],
        v_sb[64:128, 1024:2048].rearrange("p (g d) -> p g d", d=128)))
    v_rp.inc(nc.vector.memset(sk_sb[0:64, 1024:2048], 0.0))
    with p.psum("pTT0", [128, 128], BF16) as pTT0, \
         p.psum("pTT1", [128, 128], BF16) as pTT1:
        pT2i = [pTT0, pTT1]
        nc.tensor.wait_ge(v_rp.h, v_rp.v)
        for g in range(8):
            tp = g % 2
            if g >= 2:
                nc.tensor.wait_ge(v_tr.h, v_tr.v - 1)
            t_tr.inc(nc.tensor.transpose(pT2i[tp][:, :],
                                         sk_sb[:, 1024 + g * 128:
                                               1024 + (g + 1) * 128],
                                         cst["identb"][:, :]))
            nc.vector.wait_ge(t_tr.h, t_tr.v)
            v_tr.inc(nc.vector.tensor_copy(
                skT_sb[:, 1024 + g * 128:1024 + (g + 1) * 128], pT2i[tp][:, :]))
    nc.tensor.wait_ge(v_tr.h, v_tr.v)

    for sc in range(NCk):
        par = sc % 2
        opp = 1 - par
        cols = slice(sc * 128, (sc + 1) * 128)
        if sc >= 2:
            nc.sync.wait_ge(t_pv.h, t_pv.v)
            nc.sync.wait_ge(t_u.h, t_u.v)
            for dd in d_in:
                nc.sync.wait_ge(dd.h, dd.v)
            nc.sync.wait_ge(d_cs.h, d_cs.v)
        if sc == 0:
            nc.sync.wait_ge(v_rp.h, v_rp.v)
        d_in[par].inc(nc.sync.dma_start(out=q_sb[:, par * D:(par + 1) * D],
                                        in_=xq[:, cols], transpose=True), 16)
        d_in[par].inc(nc.sync.dma_start(out=k_sb[:, par * 1024:(par + 1) * 1024],
                                        in_=xk[:, cols], transpose=True), 16)
        d_in[par].inc(nc.sync.dma_start(out=v_sb[:, par * 1024:(par + 1) * 1024],
                                        in_=xv[:, cols], transpose=True), 16)
        d_cs.inc(nc.sync.dma_start(out=cs_sb[:, 0:128], in_=qcos[cols, :]), 16)
        d_cs.inc(nc.sync.dma_start(out=cs_sb[:, 128:256], in_=qsin[cols, :]), 16)
        d_cs.inc(nc.sync.dma_start(
            out=cs_sb[:, 256:384], in_=kcos[64 + sc * 128:64 + (sc + 1) * 128, :]), 16)
        d_cs.inc(nc.sync.dma_start(
            out=cs_sb[:, 384:512], in_=ksin[64 + sc * 128:64 + (sc + 1) * 128, :]), 16)

        nc.scalar.wait_ge(d_in[par].h, d_in[par].v)
        if sc >= 1:
            nc.scalar.wait_ge(v_til.h, v_til.v)
        a_sm.inc(nc.scalar.activation(qs_sb[:, :], q_sb[:, par * D:(par + 1) * D],
                                      AF.Exp))
        a_sm.inc(nc.scalar.activation(ks_sb[:, :],
                                      k_sb[:, par * 1024:(par + 1) * 1024], AF.Exp))
        p.g(nc.scalar, nc.scalar.activation(
            gs_sb[:, :], k_sb[:, par * 1024:(par + 1) * 1024], AF.Abs))
        a_sm.inc(nc.scalar.activation(gs_sb[:, :], gs_sb[:, :], AF.Exp, scale=-1.0))
        nc.vector.wait_ge(a_sm.h, a_sm.v)
        q3 = qs_sb[:, :].rearrange("p (h d) -> p h d", d=128)
        p.g(nc.vector, nc.vector.reduce_sum(
            zs_sb[:, 0:32].rearrange("p (h o) -> p h o", o=1), q3, axis=AX.X))
        p.g(nc.vector, nc.vector.reciprocal(zs_sb[:, 0:32], zs_sb[:, 0:32]))
        p.g(nc.vector, nc.vector.tensor_tensor(
            q3, q3, zs_sb[:, 0:32].rearrange("p (h o) -> p h o", o=1)
            .broadcast_to([128, 32, 128]), ALU.mult))
        k3 = ks_sb[:, :].rearrange("p (h d) -> p h d", d=128)
        p.g(nc.vector, nc.vector.reduce_sum(
            zs_sb[:, 32:40].rearrange("p (h o) -> p h o", o=1), k3, axis=AX.X))
        p.g(nc.vector, nc.vector.reciprocal(zs_sb[:, 32:40], zs_sb[:, 32:40]))
        p.g(nc.vector, nc.vector.tensor_tensor(
            k3, k3, zs_sb[:, 32:40].rearrange("p (h o) -> p h o", o=1)
            .broadcast_to([128, 8, 128]), ALU.mult))
        v_sm.inc(nc.vector.tensor_scalar(gs_sb[:, :], gs_sb[:, :], 1.0, None,
                                         ALU.add))
        nc.scalar.wait_ge(v_sm.h, v_sm.v)
        a_sm.inc(nc.scalar.activation(g1_sb[:, :], gs_sb[:, :], AF.Ln))
        nc.vector.wait_ge(a_sm.h, a_sm.v)
        p.g(nc.vector, nc.vector.tensor_scalar(
            gs_sb[:, :], k_sb[:, par * 1024:(par + 1) * 1024], 0.0, None, ALU.min))
        p.g(nc.vector, nc.vector.tensor_tensor(gs_sb[:, :], gs_sb[:, :],
                                               g1_sb[:, :], ALU.subtract))
        p.g(nc.vector, nc.vector.tensor_scalar(gs_sb[:, :], gs_sb[:, :], GATE,
                                               None, ALU.mult))
        v_sm.inc(nc.vector.tensor_copy(gsb_sb[:, :], gs_sb[:, :]))
        nc.vector.wait_ge(d_cs.h, d_cs.v)
        rope(sq_sb[:, :].rearrange("p (h d) -> p h d", d=128),
             q_sb[:, par * D:(par + 1) * D].rearrange("p (h d) -> p h d", d=128),
             csview(0, 64, 32), csview(64, 128, 32),
             csview(128, 192, 32), csview(192, 256, 32),
             rp_sb[:, :].rearrange("p (h d) -> p h d", d=128), v_rp)
        rope(sk_sb[:, par * 1024:(par + 1) * 1024].rearrange("p (h d) -> p h d",
                                                             d=128),
             k_sb[:, par * 1024:(par + 1) * 1024].rearrange("p (h d) -> p h d",
                                                            d=128),
             csview(256, 320, 8), csview(320, 384, 8),
             csview(384, 448, 8), csview(448, 512, 8),
             rp_sb[:, 0:1024].rearrange("p (h d) -> p h d", d=128), v_rp)
        with p.psum(f"pG{sc}", [128, 1024], F32) as pG, \
             p.psum(f"pR{sc}", [1, 1024], F32) as pR, \
             p.psum(f"pE{sc}", [128, 16], F32) as pE:
            nc.tensor.wait_ge(v_sm.h, v_sm.v)
            ins = None
            for (t0, t1) in [(0, 512), (512, 1024)]:
                nc.tensor.matmul(pG[:, t0:t1], cst["ltrib"][:, :],
                                 gsb_sb[:, t0:t1], start=True, stop=True,
                                 skip_group_check=True)
                nc.tensor.matmul(pR[:, t0:t1], cst["onesb"][:, :],
                                 gsb_sb[:, t0:t1], start=True, stop=True,
                                 skip_group_check=True)
            for g in range(8):
                ins = nc.tensor.transpose(pE[:, g:g + 1],
                                          crow_sb[:, g * 128:(g + 1) * 128],
                                          cst["identf"][0:1, 0:1])
            t_G.inc(ins)
            nc.vector.wait_ge(t_G.h, t_G.v)
            p.g(nc.vector, nc.vector.tensor_copy(ec_sb[:, 0:8], pE[:, 0:8]))
            p.g(nc.vector, nc.vector.tensor_copy(crow_sb[:, 1024:2048], pR[:, :]))
            p.g(nc.vector, nc.vector.tensor_tensor(
                crow_sb[:, 0:1024], crow_sb[:, 0:1024], crow_sb[:, 1024:2048],
                ALU.add))
            v_G.inc(nc.vector.engine_nop())
            nc.tensor.wait_ge(v_G.h, v_G.v)
            ins = None
            for g in range(8):
                ins = nc.tensor.transpose(pE[:, 8 + g:9 + g],
                                          crow_sb[:, 1024 + g * 128:
                                                  1024 + (g + 1) * 128],
                                          cst["identf"][0:1, 0:1])
            t_G.inc(ins)
            nc.vector.wait_ge(t_G.h, t_G.v)
            v_G.inc(nc.vector.tensor_copy(ecl_sb[:, 0:8], pE[:, 8:16]))
            nc.scalar.wait_ge(v_G.h, v_G.v)
            p.g(nc.scalar, nc.scalar.activation(ec_sb[:, 8:16], ec_sb[:, 0:8],
                                                AF.Exp))
            p.g(nc.scalar, nc.scalar.activation(ecl_sb[:, 8:16], ecl_sb[:, 0:8],
                                                AF.Exp))
            a_G.inc(nc.scalar.activation(eG_sb[:, :], pG[:, :], AF.Exp))
            a_G.inc(nc.scalar.activation(eGi_sb[:, :], pG[:, :], AF.Exp,
                                         scale=-1.0))
        nc.vector.wait_ge(a_G.h, a_G.v)
        e4 = eG_sb[:, :].rearrange("p (g d) -> p g d", d=128).unsqueeze(
            2).broadcast_to([128, 8, 4, 128])
        p.g(nc.vector, nc.vector.tensor_tensor(
            qt_sb[:, :].rearrange("p (g r d) -> p g r d", g=8, r=4),
            qs_sb[:, :].rearrange("p (g r d) -> p g r d", g=8, r=4), e4, ALU.mult))
        v_til.inc(nc.vector.tensor_tensor(
            kt_sb[:, :].rearrange("p (g d) -> p g d", d=128),
            ks_sb[:, :].rearrange("p (g d) -> p g d", d=128),
            eGi_sb[:, :].rearrange("p (g d) -> p g d", d=128), ALU.mult))
        with p.psum(f"pTa{sc}", [128, 128], BF16) as pTa, \
             p.psum(f"pTb{sc}", [128, 128], BF16) as pTb:
            pT2 = [pTa, pTb]
            nc.tensor.wait_ge(v_til.h, v_til.v)
            nc.tensor.wait_ge(v_rp.h, v_rp.v)
            if sc >= 1:
                nc.vector.wait_ge(s_qg[opp].h, s_qg[opp].v)
            plan = ([(kt_sb, hh, ktT_sb, hh, None) for hh in range(8)] +
                    [(qt_sb, hh, qtT_sb, hh, hh // 4) for hh in range(32)] +
                    [(sq_sb, hh, sqT_sb, hh, None) for hh in range(32)] +
                    [(sk_sb, 8 * par + hh, skT_sb, 8 * par + hh, None)
                     for hh in range(8)])
            for i, (src, si, dst, di, gq) in enumerate(plan):
                tp = i % 2
                if i >= 2:
                    nc.tensor.wait_ge(v_tr.h, v_tr.v - 1)
                t_tr.inc(nc.tensor.transpose(pT2[tp][:, :],
                                             src[:, si * 128:(si + 1) * 128],
                                             cst["identb"][:, :]))
                nc.vector.wait_ge(t_tr.h, t_tr.v)
                if gq is not None:
                    p.g(nc.vector, nc.vector.tensor_copy(
                        dst[:, di * 128:(di + 1) * 128], pT2[tp][:, :]))
                    v_tr.inc(nc.vector.tensor_scalar(
                        qgT_sb[:, di * 128:(di + 1) * 128], pT2[tp][:, :],
                        ec_sb[:, 8 + gq:9 + gq], None, ALU.mult))
                else:
                    v_tr.inc(nc.vector.tensor_copy(
                        dst[:, di * 128:(di + 1) * 128], pT2[tp][:, :]))
        if sc >= 2:
            nc.sync.wait_ge(s_qg[par].h, s_qg[par].v)
        ins = nc.sync.dma_start(
            out=qgs.rearrange("(c p) t -> p c t", p=128)[:, :, cols],
            in_=qgT_sb[:, :].rearrange("p (c t) -> p c t", t=128))
        ins._wait_ge(v_tr.h, v_tr.v)
        s_qg[par].inc(ins, 16)
        nc.vector.wait_ge(d_in[par].h, d_in[par].v)
        if sc >= 1:
            nc.vector.wait_ge(t_pv.h, t_pv.v)
        va3 = vaug_sb[:, par * 8 * 136:(par + 1) * 8 * 136].rearrange(
            "p (g w) -> p g w", w=136)
        p.g(nc.vector, nc.vector.tensor_copy(
            va3[:, :, 0:128],
            v_sb[:, par * 1024:(par + 1) * 1024].rearrange("p (g d) -> p g d",
                                                           d=128)))
        v_va.inc(nc.vector.tensor_copy(
            wv_sb[0:64, :], vaug_sb[0:64, par * 8 * 136:(par + 1) * 8 * 136]))
        v_va.inc(nc.vector.tensor_copy(
            wv_sb[64:128, :], vaug_sb[64:128, opp * 8 * 136:(opp + 1) * 8 * 136]))
        with p.psum(f"pA{sc}", [128, 128], F32) as pA, \
             p.psum(f"pO0{sc}", [128, 128], F32) as pO0, \
             p.psum(f"pO1{sc}", [128, 128], F32) as pO1, \
             p.psum(f"pU{sc}", [128, 128], F32) as pU:
            pO = [pO0, pO1]
            nc.tensor.wait_ge(v_tr.h, v_tr.v)
            for h in range(32):
                g = h // 4
                hp = h % 2
                if h >= 1 or sc >= 1:
                    nc.tensor.wait_ge(v_A.h, v_A.v)
                t_A.inc(nc.tensor.matmul(pA[:, :], ktT_sb[:, g * 128:(g + 1) * 128],
                                         qtT_sb[:, h * 128:(h + 1) * 128],
                                         start=True, stop=True,
                                         skip_group_check=True))
                nc.vector.wait_ge(t_A.h, t_A.v)
                if h >= 2 or sc >= 1:
                    nc.vector.wait_ge(t_o.h, t_o.v - 2)
                v_A.inc(nc.vector.tensor_tensor(
                    am_sb[:, hp * 128:(hp + 1) * 128], pA[:, :], cst["ltrib"][:, :],
                    ALU.mult))
                nc.tensor.wait_ge(v_A.h, v_A.v)
                if h >= 2 or sc >= 1:
                    nc.tensor.wait_ge(v_o.h, v_o.v - 1)
                t_o.inc(nc.tensor.matmul(pO[hp][:, :],
                                         am_sb[:, hp * 128:(hp + 1) * 128],
                                         v_sb[:, par * 1024 + g * 128:
                                              par * 1024 + (g + 1) * 128],
                                         start=True, stop=False,
                                         skip_group_check=True))
                t_o.inc(nc.tensor.matmul(pO[hp][:, :],
                                         qtT_sb[:, h * 128:(h + 1) * 128],
                                         Sb_sb[:, g * 128:(g + 1) * 128],
                                         start=False, stop=True,
                                         skip_group_check=True))
                nc.vector.wait_ge(t_o.h, t_o.v)
                if sc >= 2 and h == 0:
                    nc.vector.wait_ge(s_ob[par].h, s_ob[par].v)
                v_o.inc(nc.vector.tensor_scalar(
                    ost_sb[:, par * D + h * 128:par * D + (h + 1) * 128],
                    pO[hp][:, :], 0.5, None, ALU.mult))
            for g in range(8):
                if g >= 1 or sc >= 1:
                    nc.tensor.wait_ge(v_S.h, v_S.v)
                t_u.inc(nc.tensor.matmul(
                    pU[:, :], kt_sb[:, g * 128:(g + 1) * 128],
                    v_sb[:, par * 1024 + g * 128:par * 1024 + (g + 1) * 128],
                    start=True, stop=True, skip_group_check=True))
                nc.vector.wait_ge(t_u.h, t_u.v)
                p.g(nc.vector, nc.vector.tensor_tensor(
                    Ss_sb[:, :], S_sb[:, g * 128:(g + 1) * 128], pU[:, :], ALU.add))
                p.g(nc.vector, nc.vector.tensor_scalar(
                    S_sb[:, g * 128:(g + 1) * 128], Ss_sb[:, :],
                    ecl_sb[:, 8 + g:9 + g], None, ALU.mult))
                v_S.inc(nc.vector.tensor_copy(
                    Sb_sb[:, g * 128:(g + 1) * 128],
                    S_sb[:, g * 128:(g + 1) * 128]))
        with p.psum(f"pS0{sc}", [128, 64], F32) as pS0, \
             p.psum(f"pS1{sc}", [128, 64], F32) as pS1, \
             p.psum(f"pW0{sc}", [128, 136], F32) as pW0, \
             p.psum(f"pW1{sc}", [128, 136], F32) as pW1:
            pS = [pS0, pS1]
            pW = [pW0, pW1]
            qi = 0
            nc.tensor.wait_ge(v_va.h, v_va.v)
            for h in range(0 if 4 in skip else 32):
                g = h // 4
                for blk in range(2):
                    sp = qi % 2
                    first = (sc == 0 and blk == 0)
                    klow = skT_sb[:, par * 1024 + g * 128:par * 1024 + g * 128 + 64]
                    if blk == 0:
                        khigh = skT_sb[:, opp * 1024 + g * 128 + 64:
                                       opp * 1024 + (g + 1) * 128]
                        vwin = wv_sb[:, g * 136:(g + 1) * 136]
                    else:
                        khigh = skT_sb[:, par * 1024 + g * 128 + 64:
                                       par * 1024 + (g + 1) * 128]
                        vwin = vaug_sb[:, par * 8 * 136 + g * 136:
                                       par * 8 * 136 + (g + 1) * 136]
                    qsl = sqT_sb[:, h * 128 + blk * 64:h * 128 + (blk + 1) * 64]
                    if qi >= 2:
                        nc.tensor.wait_ge(v_swm.h, v_swm.v)
                    nc.tensor.matmul(pS[sp][0:64, :], klow, qsl,
                                     start=True, stop=True, skip_group_check=True)
                    t_sw.inc(nc.tensor.matmul(pS[sp][64:128, :], khigh, qsl,
                                              start=True, stop=True,
                                              skip_group_check=True,
                                              tile_position=(0, 64)))
                    if 7 in skip:
                        qi += 1
                        continue
                    nc.vector.wait_ge(t_sw.h, t_sw.v)
                    if qi >= 2:
                        nc.vector.wait_ge(a_sw.h, max(a_sw.v - 1, 0))
                    mvar = 2 if first else blk
                    v_swm.inc(nc.vector.tensor_tensor(
                        psm_sb[:, sp * 64:(sp + 1) * 64], pS[sp][:, :],
                        swm_sb[:, mvar * 64:(mvar + 1) * 64], ALU.add))
                    if 8 in skip:
                        qi += 1
                        continue
                    nc.scalar.wait_ge(v_swm.h, v_swm.v)
                    if qi >= 2:
                        nc.scalar.wait_ge(t_pv.h, max(t_pv.v - 1, 0))
                    a_sw.inc(nc.scalar.activation(pw_sb[:, sp * 64:(sp + 1) * 64],
                                                  psm_sb[:, sp * 64:(sp + 1) * 64],
                                                  AF.Exp))
                    if 5 in skip:
                        qi += 1
                        continue
                    nc.tensor.wait_ge(a_sw.h, a_sw.v)
                    if qi >= 2:
                        nc.tensor.wait_ge(v_sw.h, max(v_sw.v - 1, 0))
                    br = slice(blk * 64, (blk + 1) * 64)
                    tp_kw = {} if blk == 0 else {"tile_position": (0, 64)}
                    t_pv.inc(nc.tensor.matmul(pW[sp][br, :],
                                              pw_sb[:, sp * 64:(sp + 1) * 64],
                                              vwin, start=True, stop=True,
                                              skip_group_check=True, **tp_kw))
                    if 6 in skip:
                        qi += 1
                        continue
                    nc.vector.wait_ge(t_pv.h, t_pv.v)
                    p.g(nc.vector, nc.vector.tensor_scalar(
                        zc_sb[br, sp:sp + 1], pW[sp][br, 128:129], 2.0, None,
                        ALU.mult))
                    p.g(nc.vector, nc.vector.reciprocal(zc_sb[br, sp:sp + 1],
                                                        zc_sb[br, sp:sp + 1]))
                    v_sw.inc(nc.vector.scalar_tensor_tensor(
                        ost_sb[:, par * D + h * 128:par * D + (h + 1) * 128]
                        [br, :],
                        pW[sp][br, 0:128], zc_sb[br, sp:sp + 1],
                        ost_sb[:, par * D + h * 128:par * D + (h + 1) * 128]
                        [br, :], ALU.mult, ALU.add))
                    qi += 1
        ins = nc.sync.dma_start(
            out=obase.rearrange("(c p) d -> c p d", p=128)[sc],
            in_=ost_sb[:, par * D:(par + 1) * D])
        ins._wait_ge(v_sw.h, v_sw.v)
        s_ob[par].inc(ins, 16)

    s_S = p.sem("s_S")
    nc.sync.wait_ge(v_S.h, v_S.v)
    s_S.inc(nc.sync.dma_start(out=sloc.rearrange("(g p) v -> p g v", p=128),
                              in_=S_sb[:, :].rearrange("p (g v) -> p g v", v=128)),
            16)
    nc.sync.wait_ge(s_S.h, s_S.v)
    nc.sync.wait_ge(s_ob[0].h, s_ob[0].v)
    nc.sync.wait_ge(s_ob[1].h, s_ob[1].v)
    nc.sync.wait_ge(s_qg[0].h, s_qg[0].v)
    nc.sync.wait_ge(s_qg[1].h, s_qg[1].v)
    return p.done()


def build_gla_c(T=TPC, p=None):
    p = p or LP("gla_c")
    nc = p.nc
    NCk = T // 128
    xfm = p.inp("xfm", (D, T))
    ob = p.inp("ob", (T, D), BF16)
    qgs = p.inp("qgs", (D, T), BF16)
    sin = p.inp("sin", (1024, 128))
    wo = p.inp("wo", (NCH, 128, NCH, 128), BF16)
    xout = p.out("xout", (D, T))

    cst, dma_c = load_consts(p, [("identb", (128, 128), BF16)])
    sin_sb = p.sb("sin_sb", [128, 8 * 128], BF16)
    dma_s = p.sem("dma_s")
    dma_s.inc(nc.gpsimd.dma_start(
        out=sin_sb[:, :].rearrange("p (g v) -> p g v", v=128),
        in_=sin.rearrange("(g p) v -> p g v", p=128)), 16)
    o_sb = p.sb("o_sb", [128, 2 * D], BF16)
    qg_sb = p.sb("qg_sb", [128, 2 * D], BF16)
    oc_sb = p.sb("oc_sb", [128, D], BF16)
    of_sb = p.sb("of_sb", [128, NCH * T], BF16)

    d_in = [p.sem("d_in0"), p.sem("d_in1")]
    t_c = p.sem("t_c")
    v_c = p.sem("v_c")
    t_tr = p.sem("t_tr")
    v_tr = p.sem("v_tr")
    for e in [nc.vector, nc.scalar, nc.tensor, nc.gpsimd, nc.sync]:
        e.wait_ge(dma_c.h, dma_c.v)
        e.wait_ge(dma_s.h, dma_s.v)

    with (
        p.psum("pC0", [128, 128], F32) as pC0,
        p.psum("pC1", [128, 128], F32) as pC1,
        p.psum("pT0", [128, 128], BF16) as pT0,
        p.psum("pT1", [128, 128], BF16) as pT1,
    ):
        pC = [pC0, pC1]
        pT = [pT0, pT1]
        for sc in range(NCk):
            par = sc % 2
            cols = slice(sc * 128, (sc + 1) * 128)
            if sc >= 2:
                nc.sync.wait_ge(v_c.h, v_c.v)
                for dd in d_in:
                    nc.sync.wait_ge(dd.h, dd.v)
            d_in[par].inc(nc.sync.dma_start(
                out=o_sb[:, par * D:(par + 1) * D],
                in_=ob.rearrange("(c p) d -> c p d", p=128)[sc]), 16)
            d_in[par].inc(nc.sync.dma_start(
                out=qg_sb[:, par * D:(par + 1) * D].rearrange(
                    "p (c t) -> p c t", t=128),
                in_=qgs.rearrange("(c p) t -> p c t", p=128)[:, :, cols]), 16)
            nc.tensor.wait_ge(d_in[par].h, d_in[par].v)
            for h in range(32):
                g = h // 4
                hp = h % 2
                if h >= 2 or sc >= 1:
                    nc.tensor.wait_ge(v_c.h, v_c.v - 1)
                t_c.inc(nc.tensor.matmul(
                    pC[hp][:, :], qg_sb[:, par * D + h * 128:par * D + (h + 1) * 128],
                    sin_sb[:, g * 128:(g + 1) * 128],
                    start=True, stop=True, skip_group_check=True))
                nc.vector.wait_ge(t_c.h, t_c.v)
                v_c.inc(nc.vector.scalar_tensor_tensor(
                    oc_sb[:, h * 128:(h + 1) * 128], pC[hp][:, :], 0.5,
                    o_sb[:, par * D + h * 128:par * D + (h + 1) * 128],
                    ALU.mult, ALU.add))
            nc.tensor.wait_ge(v_c.h, v_c.v)
            for cc in range(NCH):
                tp = cc % 2
                if sc * NCH + cc >= 2:
                    nc.tensor.wait_ge(v_tr.h, v_tr.v - 1)
                t_tr.inc(nc.tensor.transpose(pT[tp][:, :],
                                             oc_sb[:, cc * 128:(cc + 1) * 128],
                                             cst["identb"][:, :]))
                nc.vector.wait_ge(t_tr.h, t_tr.v)
                v_tr.inc(nc.vector.tensor_copy(
                    of_sb[:, cc * T + sc * 128:cc * T + (sc + 1) * 128],
                    pT[tp][:, :]))

    emit_dense_residual(p, of_sb, wo, xfm, xout, T, ready_sem=p.sem("v_tr"))
    return p.done()


def gla_inmaps_a(inputs, xs):
    hc = host_const_arrays()
    base = {
        "wq": _bf(blockify(np.asarray(inputs["L1_attn_b_Wq"]))),
        "wk": _bf(blockify(np.asarray(inputs["L1_attn_b_Wk"]))),
        "wv": _bf(blockify(np.asarray(inputs["L1_attn_b_Wv"]))),
        "c_onesb": hc["onesb"], "c_ones1f": hc["ones1f"],
    }
    return [dict(base, xfm=xs[c]) for c in range(NCORES)]


def _rope_tables(T=TPC):
    half = np.arange(64, dtype=np.float32)
    inv = 1.0 / (10000.0 ** (2.0 * half / 128.0))
    out = {}
    for h in range(2):
        qpos = np.arange(h * T, (h + 1) * T, dtype=np.float32)
        kpos = np.arange(h * T - 64, (h + 1) * T, dtype=np.float32)
        fq = qpos[:, None] * inv[None, :]
        fk = kpos[:, None] * inv[None, :]
        out[h] = (
            _f32(np.concatenate([np.cos(fq), np.cos(fq)], 1) / np.sqrt(128.0)),
            _f32(np.concatenate([np.sin(fq), np.sin(fq)], 1) / np.sqrt(128.0)),
            _f32(np.concatenate([np.cos(fk), np.cos(fk)], 1)),
            _f32(np.concatenate([np.sin(fk), np.sin(fk)], 1)),
        )
    return out


def _swa_masks(is_odd):
    i = np.arange(64)[:, None]
    j = np.arange(64)[None, :]
    neg = np.float32(-1e30)
    m0 = np.zeros((128, 64), np.float32)
    m0[0:64] = np.where(i <= j, 0.0, neg)
    m0[64:128] = np.where(i >= j, 0.0, neg)
    m1 = np.zeros((128, 64), np.float32)
    m1[0:64] = np.where(i >= j, 0.0, neg)
    m1[64:128] = np.where(i <= j, 0.0, neg)
    m2 = m0.copy()
    if not is_odd:
        m2[64:128] = neg
    return np.stack([m0, m1, m2])


def gla_inmaps_b(ra, T=TPC):
    hc = host_const_arrays()
    rt = _rope_tables(T)
    maps = []
    for c in range(NCORES):
        b, h = c // 2, c % 2
        ktail = np.zeros((1024, 128), BF)
        vtail = np.zeros((1024, 128), BF)
        if h == 1:
            ktail[:, 64:128] = ra[c - 1]["xk"][:, T - 64:T]
            vtail[:, 64:128] = ra[c - 1]["xv"][:, T - 64:T]
        qc, qs_, kc, ks_ = rt[h]
        maps.append({
            "xq": ra[c]["xq"], "xk": ra[c]["xk"], "xv": ra[c]["xv"],
            "ktail": ktail, "vtail": vtail,
            "qcos": qc, "qsin": qs_, "kcos": kc, "ksin": ks_,
            "swm": _swa_masks(h == 1),
            "c_onesb": hc["onesb"], "c_ones1f": hc["ones1f"],
            "c_identb": hc["identb"], "c_identf": hc["identf"],
            "c_ltrib": _bf(np.triu(np.ones((128, 128), np.float32))),
        })
    return maps


def gla_inmaps_c(inputs, xs, rb):
    hc = host_const_arrays()
    base = {
        "wo": _bf(blockify(np.asarray(inputs["L1_attn_b_Wo"]))),
        "c_identb": hc["identb"],
    }
    maps = []
    for c in range(NCORES):
        b, h = c // 2, c % 2
        sin = np.zeros((1024, 128), np.float32) if h == 0 else _f32(rb[c - 1]["sloc"])
        maps.append(dict(base, xfm=xs[c], ob=rb[c]["obase"], qgs=rb[c]["qgs"],
                         sin=sin))
    return maps
